# version 08
# baseline (speedup 1.0000x reference)
import numpy as np
import ml_dtypes
import concourse.bass as bass
import concourse.mybir as mybir
from concourse.bass_utils import run_bass_kernel_spmd

F32 = mybir.dt.float32
BF16 = mybir.dt.bfloat16
AF = mybir.ActivationFunctionType
ALU = mybir.AluOpType
AX = mybir.AxisListType.X

D = 2048
DC = 16
HE = 1408
HIE = 11
HS = 2816
HIS = 22
NE = 8
GAMMA = 0.96875
EPS = 1e-5
G = 512
TOK = 1024
NGRP = TOK // G
S = 4096
B = 2
ENGS = ("sync", "gpsimd", "tensor", "vector", "scalar")
ROUNDS = [(0, 1), (2, 3), (4, 5), (6, 7), ("sh",)]


def host_xpos_tables(rows, downscale):
    rows = np.asarray(rows, dtype=np.float64)
    half = D // 2
    j = np.arange(half, dtype=np.float64)
    scale_vec = (2.0 * j + 0.4 * D) / (1.4 * D)
    scale = scale_vec[None, :] ** (rows / 512.0)[:, None]
    if downscale:
        scale = 1.0 / scale
    inv_freq = 1.0 / 10000.0 ** (j / half)
    sinusoid = rows[:, None] * inv_freq[None, :]
    sin = np.sin(sinusoid) * scale
    cos = np.cos(sinusoid) * scale
    C = np.empty((D, len(rows)), dtype=np.float64)
    Sg = np.empty((D, len(rows)), dtype=np.float64)
    C[0::2, :] = cos.T
    C[1::2, :] = cos.T
    Sg[0::2, :] = -sin.T
    Sg[1::2, :] = sin.T
    return C.astype(ml_dtypes.bfloat16), Sg.astype(ml_dtypes.bfloat16)


def host_decay(row_ids, m_ext):
    n = np.asarray(row_ids, dtype=np.float64)[None, :]
    m = np.arange(m_ext, dtype=np.float64)[:, None]
    diff = n - m
    with np.errstate(over="ignore", under="ignore"):
        Dm = np.where(diff >= 0, np.exp(np.log(GAMMA) * np.maximum(diff, 0.0)), 0.0)
    return Dm.astype(np.float32)


def fm(a):
    return np.ascontiguousarray(np.asarray(a).T)



class Ctr:
    __slots__ = ("h", "v")

    def __init__(self, handle):
        self.h = handle
        self.v = 0


class P:
    def __init__(self, nc, active, sems):
        self.nc = nc
        self.active = active
        self.sems = sems
        self.pend = {e: [] for e in ENGS}

    def defer_inc(self, eng, sem, n=1):
        self.pend[eng].append((sem, n))

    def flush_pend(self, eng, ins, limit=1):
        k = 0
        while self.pend[eng] and k < limit:
            sem, n = self.pend[eng].pop(0)
            sem.v += n
            if self.active == eng and ins is not None:
                ins.then_inc(sem.h, n)
            k += 1

    def do(self, eng, fn, *args, **kw):
        if self.active == eng:
            return getattr(getattr(self.nc, eng), fn)(*args, **kw)
        return None

    def inc(self, eng, ins, sem, n=1):
        sem.v += n
        if self.active == eng:
            assert ins is not None, f"inc on missing ins ({eng})"
            ins.then_inc(sem.h, n)
        return sem.v

    def wait(self, eng, sem, val=None):
        if val is None:
            val = sem.v
        assert val <= sem.v, f"wait {val} > emitted {sem.v}"
        if self.active == eng and val > 0:
            getattr(self.nc, eng).wait_ge(sem.h, val)

    def dma(self, eng, out, in_, sem, **kw):
        ins = self.do(eng, "dma_start", out=out, in_=in_, **kw)
        return self.inc(eng, ins, sem, 16)


class DBuf:

    def __init__(self, p, bufs, loaded, prog, feng="gpsimd"):
        self.p = p
        self.bufs = bufs
        self.n = len(bufs)
        self.loaded = loaded
        self.prog = prog
        self.feng = feng
        self.fills = 0
        self.loaded_at = {}
        self.done_at = []

    def fill(self, srcs):
        if self.fills >= self.n:
            assert len(self.done_at) > self.fills - self.n, "DBuf.done() missing"
            self.p.wait(self.feng, self.prog, self.done_at[self.fills - self.n])
        slot = self.bufs[self.fills % self.n]
        for dst_fn, src in srcs:
            self.p.dma(self.feng, dst_fn(slot), src, self.loaded)
        self.loaded_at[self.fills] = self.loaded.v
        self.fills += 1
        return self.fills - 1

    def use(self, i, eng="tensor"):
        self.p.wait(eng, self.loaded, self.loaded_at[i])
        return self.bufs[i % self.n]

    def done(self):
        self.done_at.append(self.prog.v)


class Slot:

    def __init__(self, nslots):
        self.readers = [None] * nslots

    def producer_wait(self, p, eng, sem, slot):
        r = self.readers[slot]
        if r is not None:
            p.wait(eng, r[0], r[1])

    def consumed(self, sem, slot):
        self.readers[slot] = (sem, sem.v)



def emit_ln(p, bf, sm, grp, pre_t=None, dst=None):
    sv, st, sa = sm["v"], sm["t"], sm["a"]
    g0 = grp * G
    x_ln = bf["x_ln"] if dst is None else dst
    rows = bf["rows"]
    mu, ex2, var, lnv, rstd = (rows[:, i * G:(i + 1) * G] for i in range(5))
    ins = None
    for dc in range(DC):
        ins = p.do("vector", "tensor_copy", x_ln[:, dc, :], bf["x_res"][:, dc, g0:g0 + G])
    cast_done = p.inc("vector", ins, sv)
    ins = None
    bf["xsq_guard"].producer_wait(p, "scalar", st, 0)
    for dc in range(DC):
        ins = p.do("scalar", "activation", bf["xsq"][:, dc, :], bf["x_res"][:, dc, g0:g0 + G], AF.Square)
    sq_done = p.inc("scalar", ins, sa)

    p.wait("tensor", sv, cast_done)
    p.wait("tensor", sa, sq_done)
    if pre_t is not None:
        p.wait("tensor", pre_t[0], pre_t[1])
    for dc in range(DC):
        ins = p.do("tensor", "matmul", bf["ps_b"][0:1, :], bf["ones_bf"][:, :], x_ln[:, dc, :], start=(dc == 0), stop=(dc == DC - 1))
        if dc == 0:
            p.flush_pend("tensor", ins, limit=1)
    ins = None
    for dc in range(DC):
        ins = p.do("tensor", "matmul", bf["ps_d"][0:1, :], bf["ones_bf"][:, :], bf["xsq"][:, dc, :], start=(dc == 0), stop=(dc == DC - 1))
    stats_done = p.inc("tensor", ins, st)
    bf["xsq_guard"].consumed(st, 0)

    p.wait("vector", st, stats_done)
    p.do("vector", "tensor_scalar", mu, bf["ps_b"][0:1, :], 1.0 / D, None, ALU.mult)
    p.do("vector", "tensor_scalar", ex2, bf["ps_d"][0:1, :], 1.0 / D, None, ALU.mult)
    p.do("vector", "drain")
    p.do("vector", "tensor_tensor", var, mu, mu, ALU.mult)
    p.do("vector", "drain")
    ins = p.do("vector", "tensor_tensor", var, ex2, var, ALU.subtract)
    var_done = p.inc("vector", ins, sv)
    p.wait("scalar", sv, var_done)
    p.do("scalar", "activation", lnv, var, AF.Ln, bias=bf["rows"][0:1, 5 * G:5 * G + 1])
    p.do("scalar", "drain")
    ins = p.do("scalar", "activation", rstd, lnv, AF.Exp, scale=-0.5)
    rstd_done = p.inc("scalar", ins, sa)

    p.wait("tensor", sa, rstd_done)
    p.do("tensor", "matmul", bf["ps_a"][:, :], bf["ones_f32"][:, :], mu, start=True, stop=True)
    ins = p.do("tensor", "matmul", bf["ps_c"][:, :], bf["ones_f32"][:, :], rstd, start=True, stop=True)
    bcast_done = p.inc("tensor", ins, st)

    p.wait("vector", st, bcast_done)
    ins = None
    for dc in range(DC):
        p.do("vector", "tensor_tensor", bf["ln_tmp"][:, dc % 2, :], bf["x_res"][:, dc, g0:g0 + G], bf["ps_a"][:, :], ALU.subtract)
        if dc >= 1:
            ins = p.do("vector", "tensor_tensor", x_ln[:, dc - 1, :], bf["ln_tmp"][:, (dc - 1) % 2, :], bf["ps_c"][:, :], ALU.mult)
    p.do("vector", "drain")
    ins = p.do("vector", "tensor_tensor", x_ln[:, DC - 1, :], bf["ln_tmp"][:, (DC - 1) % 2, :], bf["ps_c"][:, :], ALU.mult)
    return p.inc("vector", ins, sv)


def emit_gating(p, bf, sm, xln_done):
    sv, st, sa = sm["v"], sm["t"], sm["a"]
    p.wait("tensor", sv, xln_done)
    ins = None
    for tb in range(4):
        for dc in range(DC):
            ins = p.do("tensor", "matmul", bf["ps_b"][:, 32 + tb * 8:40 + tb * 8],
                       bf["x_ln"][:, dc, tb * 128:(tb + 1) * 128],
                       bf["gw"][:, dc, :], start=(dc == 0), stop=(dc == DC - 1))
    logits_done = p.inc("tensor", ins, st)

    comb_sb = bf["comb_f32"]
    stmp = bf["scores_tmp"]
    srows = bf["srows"]
    mx = lambda tb: srows[:, tb, 0:1]
    smr = lambda tb: srows[:, tb, 1:2]
    rcp = lambda tb: srows[:, tb, 2:3]
    m1 = lambda tb: srows[:, tb, 3:4]
    m2 = lambda tb: srows[:, tb, 4:5]
    lg = lambda tb: bf["ps_b"][:, 32 + tb * 8:40 + tb * 8]

    p.wait("vector", st, logits_done)
    for tb in range(4):
        p.do("vector", "tensor_reduce", mx(tb), lg(tb), AX, ALU.max, negate=True)
    ins = None
    for tb in range(4):
        ins = p.do("vector", "tensor_copy", stmp[:, tb, :], lg(tb))
    pre_exp = p.inc("vector", ins, sv)
    p.wait("scalar", sv, pre_exp)
    ins = None
    for tb in range(4):
        ins = p.do("scalar", "activation", comb_sb[:, tb, :], stmp[:, tb, :], AF.Exp, bias=mx(tb))
    exp_done = p.inc("scalar", ins, sa)
    p.wait("vector", sa, exp_done)

    def step(fn):
        for tb in range(4):
            fn(tb)
        p.do("vector", "drain")

    step(lambda tb: p.do("vector", "tensor_reduce", smr(tb), comb_sb[:, tb, :], AX, ALU.add))
    step(lambda tb: p.do("vector", "reciprocal", rcp(tb), smr(tb)))
    step(lambda tb: p.do("vector", "tensor_scalar", comb_sb[:, tb, :], comb_sb[:, tb, :], rcp(tb), None, ALU.mult))
    step(lambda tb: p.do("vector", "tensor_reduce", m1(tb), comb_sb[:, tb, :], AX, ALU.max))
    step(lambda tb: p.do("vector", "tensor_scalar", stmp[:, tb, :], comb_sb[:, tb, :], m1(tb), None, ALU.is_ge))
    step(lambda tb: p.do("vector", "scalar_tensor_tensor", stmp[:, tb, :], stmp[:, tb, :], -2.0, comb_sb[:, tb, :], ALU.mult, ALU.add))
    step(lambda tb: p.do("vector", "tensor_reduce", m2(tb), stmp[:, tb, :], AX, ALU.max))
    step(lambda tb: p.do("vector", "tensor_scalar", stmp[:, tb, :], comb_sb[:, tb, :], m2(tb), None, ALU.is_ge))
    ins = None
    for tb in range(4):
        ins = p.do("vector", "tensor_tensor", comb_sb[:, tb, :], comb_sb[:, tb, :], stmp[:, tb, :], ALU.mult)
    comb_done = p.inc("vector", ins, sv)

    p.wait("tensor", sv, comb_done)
    ins = None
    for tb in range(4):
        ins = p.do("tensor", "transpose", bf["ps_d"][0:8, tb * 128:(tb + 1) * 128], comb_sb[:, tb, :], bf["ident"][:, :])
    tr_done = p.inc("tensor", ins, st)
    p.wait("vector", st, tr_done)
    ins = p.do("vector", "tensor_copy", bf["comb8"][:, :], bf["ps_d"][0:8, 0:G])
    c8_done = p.inc("vector", ins, sv)

    banks = [bf["ps_a"], bf["ps_b"], bf["ps_c"], bf["ps_d"]] + bf["ps_dn0"]
    p.wait("tensor", sv, c8_done)
    ins = None
    for e in range(NE):
        ins = p.do("tensor", "matmul", banks[e][:, :], bf["onehot"][:, e * 128:(e + 1) * 128], bf["comb8"][:, :], start=True, stop=True)
    be = p.inc("tensor", ins, st)
    p.wait("vector", st, be)
    ins = None
    for e in range(NE):
        ins = p.do("vector", "tensor_copy", bf["comb"][:, e, :], banks[e][:, :])
    p.inc("vector", ins, sv)
    return sv.v


def emit_ffn_group(p, bf, sm, w, grp, pre_t=None):
    sv, st, sa, sg = sm["v"], sm["t"], sm["a"], sm["g"]
    g0 = grp * G
    xln_done = emit_ln(p, bf, sm, grp, pre_t=pre_t)
    comb_done = emit_gating(p, bf, sm, xln_done)

    wgu, wda = bf["wgu_db"], bf["wda_db"]
    pp = bf["pp_slot"]
    qq = bf["qq_slot"]
    silu_guard = bf["silu_guard"]
    gtmp_guard = bf["gtmp_guard"]

    for ri, rnd in enumerate(ROUNDS):
        shared = rnd == ("sh",)
        n_hi = HIS if shared else HIE
        gdst = bf["gsh"] if shared else bf["grnd"]
        hb_sizes = []
        left = n_hi
        while left > 0:
            hb_sizes.append(min(2, left))
            left -= 2
        elist = [None] if shared else list(rnd)
        for ei, e in enumerate(elist):
            for hbi, hbsz in enumerate(hb_sizes):
                h0 = hbi * 2 * 128
                if shared:
                    gsrc = w["sw_gate"][:, h0:h0 + hbsz * 128]
                    usrc = w["sw_up"][:, h0:h0 + hbsz * 128]
                else:
                    gsrc = w["w_gate"][e, :, h0:h0 + hbsz * 128]
                    usrc = w["w_up"][e, :, h0:h0 + hbsz * 128]
                fgu = wgu.fill([
                    (lambda sl, n=hbsz: sl[0][:, :, 0:n * 128], gsrc.rearrange("(dc p) h -> p dc h", p=128)),
                    (lambda sl, n=hbsz: sl[1][:, :, 0:n * 128], usrc.rearrange("(dc p) h -> p dc h", p=128)),
                ])
                for hj in range(hbsz):
                    hi = hbi * 2 + hj
                    par = hi % 4
                    pg, pu = bf["gu_pairs"][par]
                    if ri == 0 and ei == 0 and hi <= 3:
                        p.wait("tensor", sv, comb_done)
                    else:
                        pp.producer_wait(p, "tensor", sv, par)
                        qq.producer_wait(p, "tensor", sv, 0)
                        qq.producer_wait(p, "tensor", sv, 1)
                    gsl, usl = wgu.use(fgu)
                    ins = None
                    for dc in range(DC):
                        ins = p.do("tensor", "matmul", pg[:, :], gsl[:, dc, hj * 128:hj * 128 + 128], bf["x_ln"][:, dc, :], start=(dc == 0), stop=(dc == DC - 1))
                    for dc in range(DC):
                        ins = p.do("tensor", "matmul", pu[:, :], usl[:, dc, hj * 128:hj * 128 + 128], bf["x_ln"][:, dc, :], start=(dc == 0), stop=(dc == DC - 1))
                    gu_done = p.inc("tensor", ins, st)
                    if hj == hbsz - 1:
                        wgu.done()

                    p.wait("scalar", st, gu_done)
                    silu_guard.producer_wait(p, "scalar", sv, par)
                    ins = p.do("scalar", "activation", bf["silu_sb"][:, par, :], pg[:, :], AF.Silu)
                    silu_done = p.inc("scalar", ins, sa)
                    p.wait("vector", sa, silu_done)
                    if shared:
                        ins = p.do("vector", "tensor_tensor", gdst[:, hi, :], pu[:, :], bf["silu_sb"][:, par, :], ALU.mult)
                        mul_done = p.inc("vector", ins, sv)
                        silu_guard.consumed(sv, par)
                        pp.consumed(sv, par)
                    else:
                        gtmp_guard.producer_wait(p, "vector", sg, par)
                        ins = p.do("vector", "tensor_tensor", bf["gtmp"][:, par, :], pu[:, :], bf["silu_sb"][:, par, :], ALU.mult)
                        mul_done = p.inc("vector", ins, sv)
                        silu_guard.consumed(sv, par)
                        pp.consumed(sv, par)
                        p.wait("gpsimd", sv, mul_done)
                        ins = p.do("gpsimd", "tensor_tensor", gdst[:, ei, hi, :], bf["gtmp"][:, par, :], bf["comb"][:, e, :], ALU.mult)
                        p.inc("gpsimd", ins, sg)
                        gtmp_guard.consumed(sg, par)
        g_ready_sem = sv if shared else sg
        g_ready = g_ready_sem.v

        for dq in range(4):
            d0 = dq * 512
            qset = 0 if dq % 2 == 0 else 1
            pq = bf["ps_dn0"] if qset == 0 else bf["ps_dn1"]
            subs = [0, 1] if shared else list(rnd)
            ins = None
            first_mm = True
            for si, sub in enumerate(subs):
                if shared:
                    srcw = w["sw_down"][sub * HE:(sub + 1) * HE, d0:d0 + 512]
                else:
                    srcw = w["w_down"][sub, :, d0:d0 + 512]
                fd = wda.fill([(lambda sl: sl[:, :, :], srcw.rearrange("(hc p) d -> p hc d", p=128))])
                if si == 0:
                    p.wait("tensor", g_ready_sem, g_ready)
                    qq.producer_wait(p, "tensor", sv, qset)
                dsl = wda.use(fd)
                last_si = si == len(subs) - 1
                for hi in range(HIE):
                    gsrc_t = bf["gsh"][:, (sub * HIE + hi) if shared else 0, :] if shared else bf["grnd"][:, si, hi, :]
                    for dt in range(4):
                        ins = p.do("tensor", "matmul", pq[dt][:, :], dsl[:, hi, dt * 128:(dt + 1) * 128], gsrc_t,
                                   start=first_mm, stop=(last_si and hi == HIE - 1))
                    first_mm = False
                p.inc("tensor", ins, st)
                wda.done()
            dq_done = st.v

            p.wait("vector", st, dq_done)
            ins = None
            for dt in range(4):
                dcg = dq * 4 + dt
                ins = p.do("vector", "scalar_tensor_tensor", bf["x_res"][:, dcg, g0:g0 + G],
                           pq[dt][:, :], 0.5, bf["x_res"][:, dcg, g0:g0 + G], ALU.mult, ALU.add)
            ev = p.inc("vector", ins, sv)
            qq.consumed(sv, qset)
            if qset == 0:
                pp.consumed(sv, 2)
                pp.consumed(sv, 3)
            else:
                pp.consumed(sv, 0)
                pp.consumed(sv, 1)
    return sv.v



def emit_qkv(p, bf, sm, w, outs):
    sv, st, sa, sg, so = sm["v"], sm["t"], sm["a"], sm["g"], sm["o"]
    wload, tl = sm["wload"], sm["tabload"]

    bf["xsq_guard"] = Slot(1)
    for grp in range(NGRP):
        emit_ln(p, bf, sm, grp, dst=bf["x_ln2"][:, grp])

    qk_g = Slot(2)
    ob_g = Slot(1)
    qpp = Slot(2)
    vps = Slot(1)
    shuffle_mask = [i ^ 1 for i in range(32)]

    for wname in ("W_Q", "W_K", "W_V"):
        for half in range(2):
            p.wait("gpsimd", st)
            p.dma("gpsimd", bf["wfull"][:, :, :],
                  w[wname][:, half * 1024:(half + 1) * 1024].rearrange("(dc p) o -> p dc o", p=128), wload)
            for grp in range(NGRP):
                if wname in ("W_Q", "W_K"):
                    tabs = bf["cs_q"] if wname == "W_Q" else bf["cs_k"]
                    p.wait("sync", sg)
                    p.wait("sync", sv)
                    p.dma("sync", bf["ctab"][:, :, :], tabs[0].rearrange("(dc p) t -> p dc t", p=128)[:, :, grp * G:grp * G + G], tl)
                    p.dma("sync", bf["stab"][:, :, :], tabs[1].rearrange("(dc p) t -> p dc t", p=128)[:, :, grp * G:grp * G + G], tl)
                    p.wait("tensor", wload)
                    for dto in range(8):
                        dt = half * 8 + dto
                        par = dt % 2
                        pq = bf["ps_a"] if par == 0 else bf["ps_c"]
                        if dto <= 1:
                            p.wait("tensor", sv)
                        qpp.producer_wait(p, "tensor", sv, par)
                        ins = None
                        for dc in range(DC):
                            ins = p.do("tensor", "matmul", pq[:, :], bf["wfull"][:, dc, dto * 128:(dto + 1) * 128],
                                       bf["x_ln2"][:, grp, dc, :], start=(dc == 0), stop=(dc == DC - 1))
                            if dc == 0:
                                p.flush_pend("tensor", ins, limit=1)
                        mm_done = p.inc("tensor", ins, st)
                        p.wait("vector", st, mm_done)
                        if dto == 0:
                            p.wait("vector", tl)
                            ob_g.producer_wait(p, "vector", so, 0)
                        qk_g.producer_wait(p, "vector", sv, par)
                        p.do("vector", "tensor_copy", bf["qk_t"][:, par, :], pq[:, :])
                        p.do("vector", "drain")
                        ins = p.do("vector", "stream_shuffle", bf["qk_r"][:, par, :], bf["qk_t"][:, par, :], shuffle_mask)
                        sh_done = p.inc("vector", ins, sv)
                        qpp.consumed(sv, par)
                        p.wait("gpsimd", sv, sh_done)
                        ins = p.do("gpsimd", "tensor_tensor", bf["qk_a"][:, par, :], bf["qk_t"][:, par, :], bf["ctab"][:, dt, :], ALU.mult)
                        a_done = p.inc("gpsimd", ins, sg)
                        p.wait("vector", sg, a_done)
                        p.do("vector", "tensor_tensor", bf["qk_r"][:, par, :], bf["qk_r"][:, par, :], bf["stab"][:, dt, :], ALU.mult)
                        p.do("vector", "drain")
                        ins = p.do("vector", "tensor_tensor", bf["qk_out"][:, dto, :], bf["qk_r"][:, par, :], bf["qk_a"][:, par, :], ALU.add)
                        p.inc("vector", ins, sv)
                        qk_g.consumed(sv, par)
                    p.wait("sync", sv)
                    dst = outs["qt"] if wname == "W_Q" else outs["kt"]
                    p.dma("sync", dst.rearrange("(dc p) t -> p dc t", p=128)[:, half * 8:(half + 1) * 8, grp * G:grp * G + G],
                          bf["qk_out"][:, 0:8, :], so)
                    ob_g.consumed(so, 0)
                else:
                    p.wait("tensor", wload)
                    for tb in range(4):
                        if tb == 0:
                            p.wait("tensor", sv)
                        vps.producer_wait(p, "tensor", sv, 0)
                        ins = None
                        for dc in range(DC):
                            for ds in range(2):
                                ins = p.do("tensor", "matmul", bf["ps_dn0"][ds][:, :],
                                           bf["x_ln2"][:, grp, dc, tb * 128:(tb + 1) * 128],
                                           bf["wfull"][:, dc, ds * 512:(ds + 1) * 512],
                                           start=(dc == 0), stop=(dc == DC - 1))
                        v_done = p.inc("tensor", ins, st)
                        p.wait("vector", st, v_done)
                        ob_g.producer_wait(p, "vector", so, 0)
                        ins = None
                        for ds in range(2):
                            ins = p.do("vector", "tensor_copy", bf["v_out"][:, ds * 512:(ds + 1) * 512], bf["ps_dn0"][ds][:, :])
                        ve = p.inc("vector", ins, sv)
                        vps.consumed(sv, 0)
                        p.wait("sync", sv, ve)
                        p.dma("sync", outs["v"][grp * G + tb * 128:grp * G + (tb + 1) * 128, half * 1024:(half + 1) * 1024],
                              bf["v_out"][:, 0:1024], so)
                        ob_g.consumed(so, 0)
    p.wait("sync", so)



def emit_attention(p, bf, sm, ind, m_ext=None):
    sv, st = sm["v"], sm["t"]
    kload, vload, dload = sm["kload"], sm["vload"], sm["dload"]
    MT = (m_ext if m_ext is not None else S) // 128
    KB = 4
    pband = bf["pband"]
    kdb = DBuf(p, bf["kbufs"], kload, sm["t"], feng="gpsimd")
    ddb = DBuf(p, bf["dbufs"], dload, sm["v"], feng="sync")
    vdb = DBuf(p, bf["vbufs"], vload, sm["t"], feng="gpsimd")
    app = Slot(2)
    aq = Slot(1)

    for grp in range(NGRP):
        n0 = grp * G
        for mtb in range(MT // KB):
            fk = kdb.fill([(lambda sl: sl[:, :, :],
                            ind["kt_full"].rearrange("(dc p) m -> p dc m", p=128)[:, :, mtb * KB * 128:(mtb + 1) * KB * 128])])
            fdt = ddb.fill([(lambda sl: sl[:, :, :],
                             ind["dtab"][mtb * KB * 128:(mtb + 1) * KB * 128, n0:n0 + G].rearrange("(mc p) n -> p mc n", p=128))])
            for mi in range(KB):
                mt = mtb * KB + mi
                par = mt % 2
                pq = pband[par]
                app.producer_wait(p, "tensor", sv, par)
                if grp == 0 and mt <= 1:
                    p.wait("tensor", sv)
                if mt <= 1:
                    aq.producer_wait(p, "tensor", sv, 0)
                ksl = kdb.use(fk)
                ins = None
                for dc in range(DC):
                    ins = p.do("tensor", "matmul", pq[:, :], ksl[:, dc, mi * 128:(mi + 1) * 128],
                               bf["qt_sb"][:, dc, n0:n0 + G], start=(dc == 0), stop=(dc == DC - 1))
                qk_done = p.inc("tensor", ins, st)
                if mi == KB - 1:
                    kdb.done()
                p.wait("vector", st, qk_done)
                if mi == 0:
                    p.wait("vector", dload, ddb.loaded_at[fdt])
                dsl = bf["dbufs"][fdt % 2]
                ins = p.do("vector", "tensor_tensor", bf["att_sb"][:, mt, :], pq[:, :], dsl[:, mi, :], ALU.mult)
                p.inc("vector", ins, sv)
                app.consumed(sv, par)
                if mi == KB - 1:
                    ddb.done()
        att_all = sv.v
        for dh in range(2):
            ins = None
            for mtb in range(MT // KB):
                fv = vdb.fill([(lambda sl: sl[:, :, :],
                                ind["v_full"][mtb * KB * 128:(mtb + 1) * KB * 128, dh * 1024:(dh + 1) * 1024].rearrange("(mc p) d -> p mc d", p=128))])
                if mtb == 0:
                    p.wait("tensor", sv, att_all)
                    aq.producer_wait(p, "tensor", sv, 0)
                    app.producer_wait(p, "tensor", sv, 0)
                    app.producer_wait(p, "tensor", sv, 1)
                vsl = vdb.use(fv)
                for mi in range(KB):
                    mt = mtb * KB + mi
                    for dt in range(8):
                        ins = p.do("tensor", "matmul", pband[dt][:, :], vsl[:, mi, dt * 128:(dt + 1) * 128],
                                   bf["att_sb"][:, mt, :], start=(mt == 0), stop=(mt == MT - 1))
                p.inc("tensor", ins, st)
                vdb.done()
            oh_done = st.v
            p.wait("vector", st, oh_done)
            ins = None
            for dt in range(8):
                dcg = dh * 8 + dt
                ins = p.do("vector", "tensor_tensor", bf["x_res"][:, dcg, n0:n0 + G], pband[dt][:, :],
                           bf["x_res"][:, dcg, n0:n0 + G], ALU.add)
            ev = p.inc("vector", ins, sv)
            aq.consumed(sv, 0)
            app.consumed(sv, 0)
            app.consumed(sv, 1)



import contextlib


def run_phases(nc, sems, phases):
    items = [(k, c) for k, c in sems.items() if isinstance(c, Ctr)]
    for emitfn in phases:
        snap = {k: c.v for k, c in items}
        final = None
        with nc.Block() as blk:
            for eng in ENGS:
                for k, c in items:
                    c.v = snap[k]
                def body(e, eng=eng, fn=emitfn):
                    fn(P(nc, eng, sems))
                getattr(blk, eng)(body)
                f = {k: c.v for k, c in items}
                if final is None:
                    final = f
                else:
                    assert f == final, f"lockstep divergence in pass {eng}: " + str(
                        {k: (final[k], f[k]) for k in f if f[k] != final[k]})


def alloc_sems(nc, stack):
    names = ["v", "t", "a", "g", "o", "wload", "tabload", "kload", "vload", "dload",
             "kfree", "vfree", "dfree", "xload", "gload", "qtload", "att_q", "vq_free",
             "wga_l", "wga_f", "wua_l", "wua_f", "wda_l", "wda_f"]
    sems = {n: Ctr(stack.enter_context(nc.semaphore(n))) for n in names}
    for pair in ["pp_free", "qq_free", "att_pp"]:
        for i in range(2):
            sems[f"{pair}{i}"] = Ctr(stack.enter_context(nc.semaphore(f"{pair}{i}")))
    sems["pp_free"] = [sems["pp_free0"], sems["pp_free1"]]
    sems["qq_free"] = [sems["qq_free0"], sems["qq_free1"]]
    sems["att_pp"] = [sems["att_pp0"], sems["att_pp1"]]
    return sems


def alloc_core_bufs(nc, stack):
    bf = {}
    def S_(name, shape, dt):
        return stack.enter_context(nc.sbuf_tensor(name, shape, dt))
    def PS(name, shape):
        return stack.enter_context(nc.psum_tensor(name, shape, F32))
    bf["x_res"] = S_("x_res", [128, DC, TOK], F32)
    bf["ln_tmp"] = S_("ln_tmp", [128, 2, G], F32)
    bf["rows"] = S_("rows", [1, 5 * G + 8], F32)
    bf["srows"] = S_("srows", [128, 4, 8], F32)
    bf["ones_bf"] = S_("ones_bf", [128, 1], BF16)
    bf["ones_f32"] = S_("ones_f32", [1, 128], F32)
    bf["ps_a"] = PS("ps_a", [128, G])
    bf["ps_b"] = PS("ps_b", [128, G])
    bf["ps_c"] = PS("ps_c", [128, G])
    bf["ps_d"] = PS("ps_d", [128, G])
    bf["ps_dn0"] = [PS(f"ps_dn0_{i}", [128, G]) for i in range(4)]
    bf["ps_dn1"] = [bf["ps_a"], bf["ps_b"], bf["ps_c"], bf["ps_d"]]
    bf["pband"] = [bf["ps_a"], bf["ps_b"], bf["ps_c"], bf["ps_d"]] + bf["ps_dn0"]
    return bf


def alloc_ffn_bufs(nc, stack):
    bf = {}
    def S_(name, shape, dt):
        return stack.enter_context(nc.sbuf_tensor(name, shape, dt))
    bf["x_ln"] = S_("x_ln", [128, DC, G], BF16)
    bf["xsq"] = S_("xsq", [128, DC, G], BF16)
    bf["grnd"] = S_("grnd", [128, 2, HIE, G], BF16)
    bf["gsh"] = bf["grnd"].rearrange("p a b t -> p (a b) t")
    bf["comb"] = S_("comb", [128, NE, G], BF16)
    bf["comb_f32"] = S_("comb_f32", [128, 4, 8], F32)
    bf["scores_tmp"] = S_("scores_tmp", [128, 4, 8], F32)
    bf["comb8"] = S_("comb8", [8, G], BF16)
    bf["silu_sb"] = S_("silu_sb", [128, 4, G], BF16)
    bf["gtmp"] = S_("gtmp", [128, 4, G], BF16)
    bf["wga"] = [S_(f"wga{i}", [128, DC, 256], BF16) for i in range(2)]
    bf["wua"] = [S_(f"wua{i}", [128, DC, 256], BF16) for i in range(2)]
    bf["wda"] = [S_(f"wda{i}", [128, HIE, G], BF16) for i in range(2)]
    return bf


def alloc_qkv_bufs(nc, stack):
    bf = {}
    def S_(name, shape, dt):
        return stack.enter_context(nc.sbuf_tensor(name, shape, dt))
    bf["x_ln2"] = S_("x_ln2", [128, NGRP, DC, G], BF16)
    bf["xsq"] = S_("xsq_q", [128, DC, G], BF16)
    bf["wfull"] = S_("wfull", [128, DC, D // 2], BF16)
    bf["ctab"] = S_("ctab", [128, DC, G], BF16)
    bf["stab"] = S_("stab", [128, DC, G], BF16)
    bf["qk_t"] = S_("qk_t", [128, 2, G], BF16)
    bf["qk_r"] = S_("qk_r", [128, 2, G], BF16)
    bf["qk_a"] = S_("qk_a", [128, 2, G], BF16)
    bf["qk_out"] = S_("qk_out", [128, 8, G], BF16)
    bf["v_out"] = bf["qk_out"].rearrange("p a t -> p (a t)")[:, 0:D]
    return bf


def alloc_attn_bufs(nc, stack):
    bf = {}
    def S_(name, shape, dt):
        return stack.enter_context(nc.sbuf_tensor(name, shape, dt))
    bf["qt_sb"] = S_("qt_sb", [128, DC, TOK], BF16)
    bf["att_sb"] = S_("att_sb", [128, S // 128, G], BF16)
    bf["kbufs"] = [S_(f"kbuf{i}", [128, DC, 512], BF16) for i in range(2)]
    bf["vbufs"] = [S_(f"vbuf{i}", [128, 4, 1024], BF16) for i in range(2)]
    bf["dbufs"] = [S_(f"dbuf{i}", [128, 4, G], BF16) for i in range(2)]
    return bf


def emit_const_init(p, bf, sm):
    p.do("vector", "memset", bf["ones_bf"][:, :], 1.0)
    p.do("vector", "memset", bf["rows"][0:1, 5 * G:5 * G + 8], EPS)
    ins = p.do("vector", "memset", bf["ones_f32"][:, :], 1.0)
    return p.inc("vector", ins, sm["v"])


def ffn_weight_aps(nc, kind):
    w = {}
    w["w_gate"] = nc.dram_tensor("w_gate", [NE, D, HE], F32, kind=kind).ap()
    w["w_up"] = nc.dram_tensor("w_up", [NE, D, HE], F32, kind=kind).ap()
    w["w_down"] = nc.dram_tensor("w_down", [NE, HE, D], F32, kind=kind).ap()
    w["sw_gate"] = nc.dram_tensor("sw_gate", [D, HS], F32, kind=kind).ap()
    w["sw_up"] = nc.dram_tensor("sw_up", [D, HS], F32, kind=kind).ap()
    w["sw_down"] = nc.dram_tensor("sw_down", [HS, D], F32, kind=kind).ap()
    w["gate_w"] = nc.dram_tensor("gate_w", [D, NE], F32, kind=kind).ap()
    return w


def emit_barrier(p, sems):
    for k, c in sems.items():
        if isinstance(c, Ctr):
            p.wait(p.active, c)


def emit_ffn_preamble(p, bf, sm, w):
    p.dma("sync", bf["ident"][:, :], bf["ident_dram"][:, :], sm["xload"])
    p.dma("sync", bf["onehot"][:, :], bf["onehot_dram"][:, :], sm["xload"])
    p.dma("gpsimd", bf["gw"][:, :, :], w["gate_w"].rearrange("(dc p) e -> p dc e", p=128), sm["gload"])
    for e in ENGS:
        p.wait(e, sm["xload"])
        p.wait(e, sm["gload"])


def alloc_gating_consts(nc, stack, bf):
    bf["ident"] = stack.enter_context(nc.sbuf_tensor("ident_sb", [128, 128], F32))
    bf["onehot"] = stack.enter_context(nc.sbuf_tensor("onehot_sb", [8, NE * 128], BF16))
    bf["gw"] = stack.enter_context(nc.sbuf_tensor("gw_sb", [128, DC, NE], BF16))


def _mk_nc():
    return bass.Bass("TRN2", target_bir_lowering=False, detect_race_conditions=False)


def build_l1(ffn_only=False):
    nc = _mk_nc()
    stack = contextlib.ExitStack()
    x0 = nc.dram_tensor("x0", [D, TOK], F32, kind="ExternalInput").ap()
    w = ffn_weight_aps(nc, "ExternalInput")
    w["W_Q"] = nc.dram_tensor("W_Q", [D, D], F32, kind="ExternalInput").ap()
    w["W_K"] = nc.dram_tensor("W_K", [D, D], F32, kind="ExternalInput").ap()
    w["W_V"] = nc.dram_tensor("W_V", [D, D], F32, kind="ExternalInput").ap()
    ident = nc.dram_tensor("ident", [128, 128], F32, kind="ExternalInput").ap()
    onehot = nc.dram_tensor("onehot", [8, NE * 128], BF16, kind="ExternalInput").ap()
    cs_q0 = nc.dram_tensor("cq", [2, D, TOK], BF16, kind="ExternalInput").ap()
    cs_k0 = nc.dram_tensor("ck", [2, D, TOK], BF16, kind="ExternalInput").ap()
    x1_out = nc.dram_tensor("x1", [D, TOK], F32, kind="ExternalOutput").ap()
    qt_out = nc.dram_tensor("qt", [D, TOK], BF16, kind="ExternalOutput").ap()
    kt_out = nc.dram_tensor("kt", [D, TOK], BF16, kind="ExternalOutput").ap()
    v_out = nc.dram_tensor("v", [TOK, D], BF16, kind="ExternalOutput").ap()

    bf = alloc_core_bufs(nc, stack)
    bf["ident_dram"] = ident
    bf["onehot_dram"] = onehot
    bf["cs_q"] = (cs_q0[0], cs_q0[1])
    bf["cs_k"] = (cs_k0[0], cs_k0[1])
    sems = alloc_sems(nc, stack)

    def phase_load(p):
        emit_const_init(p, bf, sems)
        p.dma("sync", bf["x_res"][:, :, :], x0.rearrange("(dc p) t -> p dc t", p=128), sems["xload"])
        emit_ffn_preamble(p, bf, sems, w)
        p.wait("tensor", sems["v"])

    def phase_ffn(p):
        bf["silu_guard"] = Slot(4)
        bf["gtmp_guard"] = Slot(4)
        bf["xsq_guard"] = Slot(1)
        bf["wgu_db"] = DBuf(p, list(zip(bf["wga"], bf["wua"])), sems["wga_l"], sems["t"])
        bf["wda_db"] = DBuf(p, bf["wda"], sems["wda_l"], sems["t"])
        bf["pp_slot"] = Slot(4)
        bf["qq_slot"] = Slot(2)
        bf["gu_pairs"] = [(bf["ps_a"], bf["ps_b"]), (bf["ps_c"], bf["ps_d"]),
                          (bf["ps_dn0"][0], bf["ps_dn0"][1]), (bf["ps_dn0"][2], bf["ps_dn0"][3])]
        for grp in range(NGRP):
            emit_ffn_group(p, bf, sems, w, grp)

    with contextlib.ExitStack() as fstack:
        alloc_gating_consts(nc, fstack, bf)
        for k, v_ in alloc_ffn_bufs(nc, fstack).items():
            bf[k] = v_
        run_phases(nc, sems, [phase_load, phase_ffn])

    if not ffn_only:
        def phase_qkv(p):
            emit_barrier(p, sems)
            emit_qkv(p, bf, sems, w, {"qt": qt_out, "kt": kt_out, "v": v_out})
        with contextlib.ExitStack() as qstack:
            for k, v_ in alloc_qkv_bufs(nc, qstack).items():
                bf[k] = v_
            run_phases(nc, sems, [phase_qkv])

    def phase_store(p):
        emit_barrier(p, sems)
        p.dma("sync", x1_out.rearrange("(dc p) t -> p dc t", p=128), bf["x_res"][:, :, :], sems["o"])
        p.wait("sync", sems["o"])

    run_phases(nc, sems, [phase_store])
    stack.close()
    return nc


def build_l2(attn_only=False):
    nc = _mk_nc()
    stack = contextlib.ExitStack()
    x1 = nc.dram_tensor("x1", [D, TOK], F32, kind="ExternalInput").ap()
    qt_in = nc.dram_tensor("qt", [D, TOK], BF16, kind="ExternalInput").ap()
    kt_full = nc.dram_tensor("kt_full", [D, S], BF16, kind="ExternalInput").ap()
    v_full = nc.dram_tensor("v_full", [S, D], BF16, kind="ExternalInput").ap()
    dtab = nc.dram_tensor("dtab", [S, TOK], BF16, kind="ExternalInput").ap()
    w = ffn_weight_aps(nc, "ExternalInput")
    ident = nc.dram_tensor("ident", [128, 128], F32, kind="ExternalInput").ap()
    onehot = nc.dram_tensor("onehot", [8, NE * 128], BF16, kind="ExternalInput").ap()
    xout = nc.dram_tensor("xout", [D, TOK], F32, kind="ExternalOutput").ap()

    bf = alloc_core_bufs(nc, stack)
    bf["ident_dram"] = ident
    bf["onehot_dram"] = onehot
    sems = alloc_sems(nc, stack)

    def phase_load(p):
        emit_const_init(p, bf, sems)
        p.dma("sync", bf["x_res"][:, :, :], x1.rearrange("(dc p) t -> p dc t", p=128), sems["xload"])
        p.dma("sync", bf["qt_sb"][:, :, :], qt_in.rearrange("(dc p) t -> p dc t", p=128), sems["xload"])
        for e in ENGS:
            p.wait(e, sems["xload"])
        p.wait("tensor", sems["v"])

    def phase_attn(p):
        emit_attention(p, bf, sems, {"kt_full": kt_full, "v_full": v_full, "dtab": dtab})

    with contextlib.ExitStack() as astack:
        for k, v_ in alloc_attn_bufs(nc, astack).items():
            bf[k] = v_
        run_phases(nc, sems, [phase_load, phase_attn])

    if not attn_only:
        def phase_ffn(p):
            emit_barrier(p, sems)
            emit_ffn_preamble(p, bf, sems, w)
            bf["silu_guard"] = Slot(4)
            bf["gtmp_guard"] = Slot(4)
            bf["xsq_guard"] = Slot(1)
            bf["wgu_db"] = DBuf(p, list(zip(bf["wga"], bf["wua"])), sems["wga_l"], sems["t"])
            bf["wda_db"] = DBuf(p, bf["wda"], sems["wda_l"], sems["t"])
            bf["pp_slot"] = Slot(4)
            bf["qq_slot"] = Slot(2)
            bf["gu_pairs"] = [(bf["ps_a"], bf["ps_b"]), (bf["ps_c"], bf["ps_d"]),
                              (bf["ps_dn0"][0], bf["ps_dn0"][1]), (bf["ps_dn0"][2], bf["ps_dn0"][3])]
            for grp in range(NGRP):
                pre = (sems["v"], sems["v"].v) if grp == 0 else None
                emit_ffn_group(p, bf, sems, w, grp, pre_t=pre)
        with contextlib.ExitStack() as fstack:
            alloc_gating_consts(nc, fstack, bf)
            for k, v_ in alloc_ffn_bufs(nc, fstack).items():
                bf[k] = v_
            run_phases(nc, sems, [phase_ffn])

    def phase_store(p):
        emit_barrier(p, sems)
        p.dma("sync", xout.rearrange("(dc p) t -> p dc t", p=128), bf["x_res"][:, :, :], sems["o"])
        p.wait("sync", sems["o"])

    run_phases(nc, sems, [phase_store])
    stack.close()
    return nc



def _consts():
    ident = np.eye(128, dtype=np.float32)
    onehot = np.zeros((8, NE * 128), dtype=ml_dtypes.bfloat16)
    for e in range(NE):
        onehot[e, e * 128:(e + 1) * 128] = 1.0
    return ident, onehot


def kernel_unfused(L0_attn_X, L0_attn_W_Q, L0_attn_W_K, L0_attn_W_V, L0_ffn_gate_w,
           L0_ffn_w_gate, L0_ffn_w_up, L0_ffn_w_down, L0_ffn_sw_gate,
           L0_ffn_sw_up, L0_ffn_sw_down):
    X = np.asarray(L0_attn_X)
    ident, onehot = _consts()
    wmap = {
        "w_gate": np.ascontiguousarray(L0_ffn_w_gate),
        "w_up": np.ascontiguousarray(L0_ffn_w_up),
        "w_down": np.ascontiguousarray(L0_ffn_w_down),
        "sw_gate": np.ascontiguousarray(L0_ffn_sw_gate),
        "sw_up": np.ascontiguousarray(L0_ffn_sw_up),
        "sw_down": np.ascontiguousarray(L0_ffn_sw_down),
        "gate_w": np.ascontiguousarray(L0_ffn_gate_w),
    }

    nc1 = build_l1()
    in1 = []
    for c in range(8):
        b, r0 = c // 4, (c % 4) * TOK
        rows = np.arange(r0, r0 + TOK)
        cq = np.stack(host_xpos_tables(rows, downscale=False))
        ck = np.stack(host_xpos_tables(rows, downscale=True))
        m = dict(wmap)
        m.update({
            "x0": fm(X[b, r0:r0 + TOK, :]),
            "W_Q": np.ascontiguousarray(L0_attn_W_Q),
            "W_K": np.ascontiguousarray(L0_attn_W_K),
            "W_V": np.ascontiguousarray(L0_attn_W_V),
            "ident": ident, "onehot": onehot, "cq": cq, "ck": ck,
        })
        in1.append(m)
    r1 = run_bass_kernel_spmd(nc1, in1, core_ids=list(range(8)))

    ktb = [np.concatenate([r1.results[b * 4 + i]["kt"] for i in range(4)], axis=1) for b in range(B)]
    vtb = [np.concatenate([r1.results[b * 4 + i]["v"] for i in range(4)], axis=0) for b in range(B)]

    nc2 = build_l2()
    in2 = []
    for c in range(8):
        b, r0 = c // 4, (c % 4) * TOK
        rows = np.arange(r0, r0 + TOK)
        m = dict(wmap)
        m.update({
            "x1": r1.results[c]["x1"],
            "qt": r1.results[c]["qt"],
            "kt_full": ktb[b],
            "v_full": vtb[b],
            "dtab": host_decay(rows, S).astype(ml_dtypes.bfloat16),
            "ident": ident, "onehot": onehot,
        })
        in2.append(m)
    r2 = run_bass_kernel_spmd(nc2, in2, core_ids=list(range(8)))

    out = np.empty((B, S, D), dtype=np.float32)
    for c in range(8):
        b, r0 = c // 4, (c % 4) * TOK
        out[b, r0:r0 + TOK, :] = r2.results[c]["xout"].T
    return out



RA = 2048


def emit_qkv2(p, bf, sm, w, tabs_c, tabs_k, n_groups, do_q, qt_dst, kv_store, tok_base):
    sv, st, sa, sg, so = sm["v"], sm["t"], sm["a"], sm["g"], sm["o"]
    wload, tl = sm["wload"], sm["tabload"]

    bf["xsq_guard"] = Slot(1)
    for grp in range(n_groups):
        emit_ln(p, bf, sm, grp, dst=bf["x_ln2"][:, grp])

    qk_g = Slot(4)
    ob_g = Slot(1)
    qpp = Slot(4)
    vps = Slot(2)
    tab_g = Slot(1)
    wdb = DBuf(p, bf["wq4"], wload, sm["t"], feng="gpsimd")
    shuffle_mask = [i ^ 1 for i in range(32)]
    wnames = (("W_Q", "W_K", "W_V") if do_q else ("W_K", "W_V"))
    banks4 = [bf["ps_a"], bf["ps_b"], bf["ps_c"], bf["ps_d"]]

    fills = {}
    order = [(wn, q) for wn in wnames for q in range(4)]
    fills[order[0]] = wdb.fill([(lambda sl: sl[:, :, :],
                                 w[order[0][0]][:, 0:512].rearrange("(dc p) o -> p dc o", p=128))])
    for oi, (wname, q) in enumerate(order):
        if True:
            fw = fills[(wname, q)]
            if oi + 1 < len(order):
                wn2, q2 = order[oi + 1]
                fills[(wn2, q2)] = wdb.fill([(lambda sl, wn2=wn2, q2=q2: sl[:, :, :],
                                              w[wn2][:, q2 * 512:(q2 + 1) * 512].rearrange("(dc p) o -> p dc o", p=128))])
            for grp in range(n_groups):
                t0 = tok_base + grp * G
                if wname in ("W_Q", "W_K"):
                    tabs = (tabs_c if wname == "W_Q" else tabs_k)
                    tab_g.producer_wait(p, "sync", sv, 0)
                    p.wait("sync", sv)
                    p.dma("sync", bf["ctab"][:, :, :], tabs[0].rearrange("(dc p) t -> p dc t", p=128)[:, q * 4:(q + 1) * 4, t0:t0 + G], tl)
                    p.dma("sync", bf["stab"][:, :, :], tabs[1].rearrange("(dc p) t -> p dc t", p=128)[:, q * 4:(q + 1) * 4, t0:t0 + G], tl)
                    wsl = wdb.use(fw)
                    ins = None
                    for dto in range(4):
                        par = dto
                        pq = banks4[par]
                        if q == 0 and grp == 0 and dto <= 3:
                            p.wait("tensor", sv)
                        qpp.producer_wait(p, "tensor", sv, par)
                        for dc in range(DC):
                            ins = p.do("tensor", "matmul", pq[:, :], wsl[:, dc, dto * 128:(dto + 1) * 128],
                                       bf["x_ln2"][:, grp, dc, :], start=(dc == 0), stop=(dc == DC - 1))
                        mm_done = p.inc("tensor", ins, st)
                        p.wait("vector", st, mm_done)
                        if dto == 0:
                            p.wait("vector", tl)
                            ob_g.producer_wait(p, "vector", so, 0)
                        qk_g.producer_wait(p, "vector", sv, par)
                        p.do("vector", "tensor_copy", bf["qk_t"][:, par, :], pq[:, :])
                        p.do("vector", "drain")
                        ins = p.do("vector", "stream_shuffle", bf["qk_r"][:, par, :], bf["qk_t"][:, par, :], shuffle_mask)
                        p.inc("vector", ins, sv)
                        qpp.consumed(sv, par)
                        ins = p.do("vector", "tensor_tensor", bf["qk_a"][:, par, :], bf["qk_t"][:, par, :], bf["ctab"][:, dto, :], ALU.mult)
                        p.do("vector", "drain")
                        p.do("vector", "tensor_tensor", bf["qk_r"][:, par, :], bf["qk_r"][:, par, :], bf["stab"][:, dto, :], ALU.mult)
                        p.do("vector", "drain")
                        ins = p.do("vector", "tensor_tensor", bf["qk_out"][:, dto, :], bf["qk_r"][:, par, :], bf["qk_a"][:, par, :], ALU.add)
                        a_done = p.inc("vector", ins, sv)
                        if dto == 3:
                            tab_g.consumed(sv, 0)
                        qk_g.consumed(sv, par)
                    p.wait("sync", sv)
                    if wname == "W_Q":
                        p.dma("sync", qt_dst.rearrange("(dc p) t -> p dc t", p=128)[:, q * 4:(q + 1) * 4, grp * G:grp * G + G],
                              bf["qk_out"][:, 0:4, :], so)
                        ob_g.consumed(so, 0)
                    else:
                        ksem = kv_store("k", grp, q, bf["qk_out"][:, 0:4, :], None)
                        ob_g.consumed(ksem if ksem is not None else so, 0)
                else:
                    wsl = wdb.use(fw)
                    for tb in range(4):
                        par = tb % 2
                        if q == 0 and grp == 0 and tb <= 1:
                            p.wait("tensor", sv)
                        vps.producer_wait(p, "tensor", sv, par)
                        ins = None
                        for dc in range(DC):
                            ins = p.do("tensor", "matmul", bf["ps_dn0"][par][:, :],
                                       bf["x_ln2"][:, grp, dc, tb * 128:(tb + 1) * 128],
                                       wsl[:, dc, :],
                                       start=(dc == 0), stop=(dc == DC - 1))
                        v_done = p.inc("tensor", ins, st)
                        p.wait("vector", st, v_done)
                        ob_g.producer_wait(p, "vector", so, 0)
                        ins = p.do("vector", "tensor_copy", bf["v_out"][:, 0:512], bf["ps_dn0"][par][:, :])
                        ve = p.inc("vector", ins, sv)
                        vps.consumed(sv, par)
                        p.wait("sync", sv, ve)
                        vsem = kv_store("v", grp, q, bf["v_out"][:, 0:512], tb)
                        ob_g.consumed(vsem if vsem is not None else so, 0)
            wdb.done()
    p.wait("sync", so)


def run_phases_roled(nc, sems, regs, phases):
    items = [(k, c) for k, c in sems.items() if isinstance(c, Ctr)]
    state = {k: c.v for k, c in items}
    rstate = None
    for kind, fn in phases:
        if kind == "common":
            snap = dict(state)
            final = None
            with nc.Block() as blk:
                for eng in ENGS:
                    for k, c in items:
                        c.v = snap[k]
                    def body(e, eng=eng, fn=fn):
                        fn(P(nc, eng, sems))
                    getattr(blk, eng)(body)
                    f = {k: c.v for k, c in items}
                    assert final is None or f == final, f"lockstep div {eng}"
                    final = f
            state = final
            assert rstate is None, "common phase after fork unsupported"
        else:
            if rstate is None:
                rstate = {0: dict(state), 1: dict(state)}
            finals = {0: None, 1: None}
            with nc.Block() as blk:
                for eng in ENGS:
                    def body(e, eng=eng, fn=fn):
                        for r in (0, 1):
                            for k, c in items:
                                c.v = rstate[r][k]
                            geng = getattr(nc, eng)
                            with geng.If_eq(regs[eng]["role"], r):
                                fn(P(nc, eng, sems), r)
                            geng.end_ifs()
                            f = {k: c.v for k, c in items}
                            assert finals[r] is None or f == finals[r], f"lockstep div {eng} role {r}: " + str(
                                {k: (finals[r][k], f[k]) for k in f if f[k] != finals[r][k]})
                            finals[r] = f
                    getattr(blk, eng)(body)
            rstate = finals


def build_fused():
    nc = _mk_nc()
    stack = contextlib.ExitStack()
    x0 = nc.dram_tensor("x0", [D, TOK], F32, kind="ExternalInput").ap()
    x0f = nc.dram_tensor("x0f", [D, TOK], F32, kind="ExternalInput").ap()
    w = ffn_weight_aps(nc, "ExternalInput")
    w["W_Q"] = nc.dram_tensor("W_Q", [D, D], F32, kind="ExternalInput").ap()
    w["W_K"] = nc.dram_tensor("W_K", [D, D], F32, kind="ExternalInput").ap()
    w["W_V"] = nc.dram_tensor("W_V", [D, D], F32, kind="ExternalInput").ap()
    ident = nc.dram_tensor("ident", [128, 128], F32, kind="ExternalInput").ap()
    onehot = nc.dram_tensor("onehot", [8, NE * 128], BF16, kind="ExternalInput").ap()
    cq = nc.dram_tensor("cq", [2, D, TOK], BF16, kind="ExternalInput").ap()
    ckv = nc.dram_tensor("ckv", [2, D, 2 * TOK], BF16, kind="ExternalInput").ap()
    dtab = nc.dram_tensor("dtab", [S, TOK], BF16, kind="ExternalInput").ap()
    meta = nc.dram_tensor("meta", [1, 4], mybir.dt.int32, kind="ExternalInput").ap()
    xout = nc.dram_tensor("xout", [D, TOK], F32, kind="ExternalOutput").ap()
    qt_d = nc.dram_tensor("qt_d", [D, TOK], BF16, kind="Internal").ap()
    kt_sh = nc.dram_tensor("kt_sh", [D, S], BF16, kind="Internal", addr_space="Shared").ap()
    v_sh = nc.dram_tensor("v_sh", [S, D], BF16, kind="Internal", addr_space="Shared").ap()
    flag_sh = nc.dram_tensor("flag_sh", [2, 1, 4], mybir.dt.int32, kind="Internal", addr_space="Shared").ap()

    bf = alloc_core_bufs(nc, stack)
    bf["ident_dram"] = ident
    bf["onehot_dram"] = onehot
    bf["meta_sb"] = stack.enter_context(nc.sbuf_tensor("meta_sb", [1, 4], mybir.dt.int32))
    sems = alloc_sems(nc, stack)
    for n in ("kvw", "flg", "pol"):
        sems[n] = Ctr(stack.enter_context(nc.semaphore(n)))
    regs = {}
    for eng in ENGS:
        regs[eng] = {"role": getattr(nc, eng).alloc_register(name=f"role_{eng}")}
    regs["sync"]["half"] = nc.sync.alloc_register(name="half_sync")
    regs["sync"]["nonce"] = nc.sync.alloc_register(name="nonce_sync")
    regs["sync"]["pollcond"] = nc.sync.alloc_register(name="pollcond")
    regs["sync"]["pollfv"] = nc.sync.alloc_register(name="pollfv")
    bf["regs"] = regs
    ind = {"x0": x0, "x0f": x0f, "cq": cq, "ckv": ckv, "dtab": dtab, "qt_d": qt_d,
           "kt_sh": kt_sh, "v_sh": v_sh, "flag_sh": flag_sh, "xout": xout}

    def kv_store_factory(p, role, wv_half_holder):
        half_reg = regs["sync"]["half"]

        def kv_store(kind, grp, q, buf, tb):
            for h in (0, 1):
                gbase = (h * TOK) if (role == 0 or bf["cur_pass"] == "foreign") else (RA + h * TOK)
                if p.active == "sync":
                    e = nc.sync
                    with e.If_eq(half_reg, h):
                        if kind == "k":
                            dst = kt_sh.rearrange("(dc p) m -> p dc m", p=128)[:, q * 4:(q + 1) * 4,
                                                                              gbase + grp * G:gbase + grp * G + G]
                        else:
                            r0 = gbase + grp * G + tb * 128
                            dst = v_sh[r0:r0 + 128, q * 512:(q + 1) * 512]
                        e.dma_start(out=dst, in_=buf).then_inc(sems["kvw"].h, 16)
                    e.end_ifs()
            sems["kvw"].v += 16
            return sems["kvw"]
        return kv_store

    def p0(p):
        emit_const_init(p, bf, sems)
        p.dma("sync", bf["meta_sb"][:, :], meta[:, :], sems["xload"])
        for eng in ENGS:
            p.wait(eng, sems["xload"])
            if p.active == eng:
                ge = getattr(nc, eng)
                ge.reg_load(regs[eng]["role"], bf["meta_sb"][0:1, 0:1])
                if eng == "sync":
                    ge.reg_load(regs[eng]["half"], bf["meta_sb"][0:1, 1:2])
                    ge.reg_load(regs[eng]["nonce"], bf["meta_sb"][0:1, 2:3])

    def mk_ffn_phase(src_ap, tag):
        def f(p, role):
            if src_ap is not None:
                if role == 0 and tag == "foreign":
                    return
                ap = x0f if tag == "foreign" else x0
                p.wait("sync", sems["v"])
                p.wait("sync", sems["t"])
                p.dma("sync", bf["x_res"][:, :, :], ap.rearrange("(dc p) t -> p dc t", p=128), sems["xload"])
                for eng in ("vector", "tensor", "scalar", "gpsimd"):
                    p.wait(eng, sems["xload"])
            emit_ffn_preamble(p, bf, sems, w)
            bf["wgu_db"] = DBuf(p, list(zip(bf["wga"], bf["wua"])), sems["wga_l"], sems["t"])
            bf["wda_db"] = DBuf(p, bf["wda"], sems["wda_l"], sems["t"])
            bf["pp_slot"] = Slot(4)
            bf["qq_slot"] = Slot(2)
            bf["gu_pairs"] = [(bf["ps_a"], bf["ps_b"]), (bf["ps_c"], bf["ps_d"]),
                              (bf["ps_dn0"][0], bf["ps_dn0"][1]), (bf["ps_dn0"][2], bf["ps_dn0"][3])]
            bf["silu_guard"] = Slot(4)
            bf["gtmp_guard"] = Slot(4)
            bf["xsq_guard"] = Slot(1)
            for grp in range(NGRP):
                pre = (sems["v"], sems["v"].v) if grp == 0 else None
                emit_ffn_group(p, bf, sems, w, grp, pre_t=pre)
        return f

    def mk_qkv_phase(tag):
        def f(p, role):
            if role == 0 and tag == "foreign":
                return
            bf["cur_pass"] = tag
            kv_store = kv_store_factory(p, role, None)
            if tag == "foreign":
                emit_qkv2(p, bf, sems, w, None, (ckv[0][:, 0:TOK], ckv[1][:, 0:TOK]), NGRP,
                          do_q=False, qt_dst=None, kv_store=kv_store, tok_base=0)
            else:
                emit_qkv2(p, bf, sems, w, (cq[0], cq[1]), (ckv[0][:, TOK:2 * TOK], ckv[1][:, TOK:2 * TOK]),
                          NGRP, do_q=True, qt_dst=qt_d, kv_store=kv_store, tok_base=0)
        return f

    def p_flags(p, role):
        half_reg = regs["sync"]["half"]
        nonce_reg = regs["sync"]["nonce"]
        p.wait("sync", sems["kvw"])
        for h in (0, 1):
            if p.active == "sync":
                with nc.sync.If_eq(half_reg, h):
                    nc.sync.dma_start(out=flag_sh[h, :, 0:1], in_=bf["meta_sb"][0:1, 2:3]).then_inc(sems["flg"].h, 16)
                nc.sync.end_ifs()
        sems["flg"].v += 16
        p.wait("sync", sems["flg"])
        if p.active == "sync":
            e = nc.sync
            cond = regs["sync"]["pollcond"]
            fv = regs["sync"]["pollfv"]
            for ph in (0, 1):
                e.reg_mov(cond, 1)
                with e.While(cond):
                    e.sem_clear(sems["pol"].h)
                    e.dma_start(out=bf["meta_sb"][0:1, 3:4], in_=flag_sh[ph, :, 0:1]).then_inc(sems["pol"].h, 16)
                    e.wait_ge(sems["pol"].h, 16)
                    e.reg_load(fv, bf["meta_sb"][0:1, 3:4])
                    e.reg_alu(cond, fv, nonce_reg, ALU.is_equal)
                    e.reg_alu(cond, cond, 1, ALU.bitwise_xor)
                e.sem_clear(sems["pol"].h)
        if p.active == "sync":
            nc.sync.sem_inc(sems["g"].h, 1)
        sems["g"].v += 1
        for eng in ("tensor", "gpsimd", "vector"):
            p.wait(eng, sems["g"])

    def p_attn(p, role):
        p.dma("sync", bf["qt_sb"][:, :, :], qt_d.rearrange("(dc p) t -> p dc t", p=128), sems["qtload"])
        for e2 in ("tensor", "vector", "gpsimd"):
            p.wait(e2, sems["qtload"])
        mx = RA if role == 0 else 4096
        emit_attention(p, bf, sems, {"kt_full": kt_sh[:, 0:mx], "v_full": v_sh[0:mx, :], "dtab": dtab[0:mx, :]}, m_ext=mx)

    def p_store(p, role):
        p.wait("sync", sems["v"])
        p.dma("sync", xout.rearrange("(dc p) t -> p dc t", p=128), bf["x_res"][:, :, :], sems["o"])
        p.wait("sync", sems["o"])

    run_phases_roled(nc, sems, regs, [("common", p0)])

    def roled_group(alloc_fns, fns):
        st2 = contextlib.ExitStack()
        for af in alloc_fns:
            for k2, v2 in af(nc, st2).items():
                bf[k2] = v2
        run2 = [("roled", fn) for fn in fns]
        _run_roled_cont(nc, sems, regs, run2)
        st2.close()

    global _ROLE_STATE
    _ROLE_STATE = None

    def _run_roled_cont(nc2, sems2, regs2, phs):
        global _ROLE_STATE
        items = [(k, c) for k, c in sems2.items() if isinstance(c, Ctr)]
        if _ROLE_STATE is None:
            base = {k: c.v for k, c in items}
            _ROLE_STATE = {0: dict(base), 1: dict(base)}
        for kind, fn in phs:
            finals = {0: None, 1: None}
            with nc2.Block() as blk:
                for eng in ENGS:
                    def body(e, eng=eng, fn=fn):
                        for r in (0, 1):
                            for k, c in items:
                                c.v = _ROLE_STATE[r][k]
                            geng = getattr(nc2, eng)
                            with geng.If_eq(regs2[eng]["role"], r):
                                fn(P(nc2, eng, sems2), r)
                            geng.end_ifs()
                            f = {k: c.v for k, c in items}
                            assert finals[r] is None or f == finals[r], f"lockstep div {eng} role {r}: " + str(
                                {k: (finals[r][k], f[k]) for k in (finals[r] or f) if f.get(k) != (finals[r] or {}).get(k)})
                            finals[r] = f
                    getattr(blk, eng)(body)
            _ROLE_STATE = finals

    roled_group([_mk_ffn_alloc("_1")], [mk_ffn_phase(x0f, "foreign")])
    roled_group([_mk_qkv_alloc("_1")], [mk_qkv_phase("foreign")])
    roled_group([_mk_ffn_alloc("_2")], [mk_ffn_phase(x0, "own")])
    roled_group([_mk_qkv_alloc("_2")], [mk_qkv_phase("own")])
    roled_group([], [p_flags])
    roled_group([alloc_attn_bufs], [p_attn])
    roled_group([_mk_ffn_alloc("_3")], [mk_ffn_phase(None, "ffn2")])
    roled_group([], [p_store])

    stack.close()
    return nc


def _mk_ffn_alloc(tagname):
    def f(nc, stack):
        bf = {}
        def S_(name, shape, dt):
            return stack.enter_context(nc.sbuf_tensor(name + tagname, shape, dt))
        bf["x_ln"] = S_("x_ln", [128, DC, G], BF16)
        bf["xsq"] = S_("xsq", [128, DC, G], BF16)
        bf["grnd"] = S_("grnd", [128, 2, HIE, G], BF16)
        bf["gsh"] = bf["grnd"].rearrange("p a b t -> p (a b) t")
        bf["comb"] = S_("comb", [128, NE, G], BF16)
        bf["comb_f32"] = S_("comb_f32", [128, 4, 8], F32)
        bf["scores_tmp"] = S_("scores_tmp", [128, 4, 8], F32)
        bf["comb8"] = S_("comb8", [8, G], BF16)
        bf["silu_sb"] = S_("silu_sb", [128, 4, G], BF16)
        bf["gtmp"] = S_("gtmp", [128, 4, G], BF16)
        bf["wga"] = [S_(f"wga{i}", [128, DC, 256], BF16) for i in range(2)]
        bf["wua"] = [S_(f"wua{i}", [128, DC, 256], BF16) for i in range(2)]
        bf["wda"] = [S_(f"wda{i}", [128, HIE, G], BF16) for i in range(2)]
        bf["ident"] = S_("ident_sb", [128, 128], F32)
        bf["onehot"] = S_("onehot_sb", [8, NE * 128], BF16)
        bf["gw"] = S_("gw_sb", [128, DC, NE], BF16)
        return bf
    return f


def _mk_qkv_alloc(tagname):
    def f(nc, stack):
        bf = {}
        def S_(name, shape, dt):
            return stack.enter_context(nc.sbuf_tensor(name + tagname, shape, dt))
        bf["x_ln2"] = S_("x_ln2", [128, NGRP, DC, G], BF16)
        bf["xsq"] = S_("xsq_q", [128, DC, G], BF16)
        bf["wq4"] = [S_(f"wq4_{i}", [128, DC, 512], BF16) for i in range(2)]
        bf["ctab"] = S_("ctab", [128, 4, G], BF16)
        bf["stab"] = S_("stab", [128, 4, G], BF16)
        bf["qk_t"] = S_("qk_t", [128, 4, G], BF16)
        bf["qk_r"] = S_("qk_r", [128, 4, G], BF16)
        bf["qk_a"] = S_("qk_a", [128, 4, G], BF16)
        bf["qk_out"] = S_("qk_out", [128, 4, G], BF16)
        bf["v_out"] = bf["qk_out"].rearrange("p a t -> p (a t)")[:, 0:512]
        return bf
    return f


alloc_ffn_bufs2 = _mk_ffn_alloc("_2")
alloc_ffn_bufs3 = _mk_ffn_alloc("_3")
alloc_qkv_bufs2 = _mk_qkv_alloc("_2")


def fused_core_inputs(c, X, wmap, WQ, WK, WV, nonce):
    b, role, half = c // 4, (c % 4) // 2, c % 2
    own0 = (RA if role else 0) + half * TOK
    ident, onehot = _consts()
    own_rows = np.arange(own0, own0 + TOK)
    for_rows = np.arange(half * TOK, half * TOK + TOK)
    cq = np.stack(host_xpos_tables(own_rows, False))
    ck_own = host_xpos_tables(own_rows, True)
    if role:
        ck_for = host_xpos_tables(for_rows, True)
    else:
        z = np.zeros((D, TOK), dtype=ml_dtypes.bfloat16)
        ck_for = (z, z)
    ckv = np.stack([np.concatenate([ck_for[0], ck_own[0]], axis=1),
                    np.concatenate([ck_for[1], ck_own[1]], axis=1)])
    m = dict(wmap)
    m.update({
        "x0": fm(X[b, own0:own0 + TOK, :]),
        "x0f": fm(X[b, half * TOK:half * TOK + TOK, :]) if role else np.zeros((D, TOK), np.float32),
        "W_Q": WQ, "W_K": WK, "W_V": WV,
        "ident": ident, "onehot": onehot, "cq": cq, "ckv": ckv,
        "dtab": host_decay(own_rows, S).astype(ml_dtypes.bfloat16),
        "meta": np.array([[role, half, nonce, 0]], dtype=np.int32),
    })
    return m


def kernel(L0_attn_X, L0_attn_W_Q, L0_attn_W_K, L0_attn_W_V, L0_ffn_gate_w,
           L0_ffn_w_gate, L0_ffn_w_up, L0_ffn_w_down, L0_ffn_sw_gate,
           L0_ffn_sw_up, L0_ffn_sw_down):
    X = np.asarray(L0_attn_X)
    wmap = {
        "w_gate": np.ascontiguousarray(L0_ffn_w_gate),
        "w_up": np.ascontiguousarray(L0_ffn_w_up),
        "w_down": np.ascontiguousarray(L0_ffn_w_down),
        "sw_gate": np.ascontiguousarray(L0_ffn_sw_gate),
        "sw_up": np.ascontiguousarray(L0_ffn_sw_up),
        "sw_down": np.ascontiguousarray(L0_ffn_sw_down),
        "gate_w": np.ascontiguousarray(L0_ffn_gate_w),
    }
    WQ = np.ascontiguousarray(L0_attn_W_Q)
    WK = np.ascontiguousarray(L0_attn_W_K)
    WV = np.ascontiguousarray(L0_attn_W_V)
    nonce = int(np.random.randint(1, 2**31 - 1))
    nc = build_fused()
    ins = [fused_core_inputs(c, X, wmap, WQ, WK, WV, nonce) for c in range(8)]
    r = run_bass_kernel_spmd(nc, ins, core_ids=list(range(8)))
    out = np.empty((B, S, D), dtype=np.float32)
    for c in range(8):
        b, role, half = c // 4, (c % 4) // 2, c % 2
        own0 = (RA if role else 0) + half * TOK
        out[b, own0:own0 + TOK, :] = r.results[c]["xout"].T
    return out


# version 09
# speedup vs baseline: 1.0039x; 1.0039x over previous
import numpy as np
import ml_dtypes
import concourse.bass as bass
import concourse.mybir as mybir
from concourse.bass_utils import run_bass_kernel_spmd

F32 = mybir.dt.float32
BF16 = mybir.dt.bfloat16
AF = mybir.ActivationFunctionType
ALU = mybir.AluOpType
AX = mybir.AxisListType.X

D = 2048
DC = 16
HE = 1408
HIE = 11
HS = 2816
HIS = 22
NE = 8
GAMMA = 0.96875
EPS = 1e-5
G = 512
TOK = 1024
NGRP = TOK // G
S = 4096
B = 2
ENGS = ("sync", "gpsimd", "tensor", "vector", "scalar")
ROUNDS = [(0, 1), (2, 3), (4, 5), (6, 7), ("sh",)]


def host_xpos_tables(rows, downscale):
    rows = np.asarray(rows, dtype=np.float64)
    half = D // 2
    j = np.arange(half, dtype=np.float64)
    scale_vec = (2.0 * j + 0.4 * D) / (1.4 * D)
    scale = scale_vec[None, :] ** (rows / 512.0)[:, None]
    if downscale:
        scale = 1.0 / scale
    inv_freq = 1.0 / 10000.0 ** (j / half)
    sinusoid = rows[:, None] * inv_freq[None, :]
    sin = np.sin(sinusoid) * scale
    cos = np.cos(sinusoid) * scale
    C = np.empty((D, len(rows)), dtype=np.float64)
    Sg = np.empty((D, len(rows)), dtype=np.float64)
    C[0::2, :] = cos.T
    C[1::2, :] = cos.T
    Sg[0::2, :] = -sin.T
    Sg[1::2, :] = sin.T
    return C.astype(ml_dtypes.bfloat16), Sg.astype(ml_dtypes.bfloat16)


def host_decay(row_ids, m_ext):
    n = np.asarray(row_ids, dtype=np.float64)[None, :]
    m = np.arange(m_ext, dtype=np.float64)[:, None]
    diff = n - m
    with np.errstate(over="ignore", under="ignore"):
        Dm = np.where(diff >= 0, np.exp(np.log(GAMMA) * np.maximum(diff, 0.0)), 0.0)
    return Dm.astype(np.float32)


def fm(a):
    return np.ascontiguousarray(np.asarray(a).T)



class Ctr:
    __slots__ = ("h", "v")

    def __init__(self, handle):
        self.h = handle
        self.v = 0


class P:
    def __init__(self, nc, active, sems):
        self.nc = nc
        self.active = active
        self.sems = sems
        self.pend = {e: [] for e in ENGS}

    def defer_inc(self, eng, sem, n=1):
        self.pend[eng].append((sem, n))

    def flush_pend(self, eng, ins, limit=1):
        k = 0
        while self.pend[eng] and k < limit:
            sem, n = self.pend[eng].pop(0)
            sem.v += n
            if self.active == eng and ins is not None:
                ins.then_inc(sem.h, n)
            k += 1

    def do(self, eng, fn, *args, **kw):
        if self.active == eng:
            return getattr(getattr(self.nc, eng), fn)(*args, **kw)
        return None

    def inc(self, eng, ins, sem, n=1):
        sem.v += n
        if self.active == eng:
            assert ins is not None, f"inc on missing ins ({eng})"
            ins.then_inc(sem.h, n)
        return sem.v

    def wait(self, eng, sem, val=None):
        if val is None:
            val = sem.v
        assert val <= sem.v, f"wait {val} > emitted {sem.v}"
        if self.active == eng and val > 0:
            getattr(self.nc, eng).wait_ge(sem.h, val)

    def dma(self, eng, out, in_, sem, **kw):
        ins = self.do(eng, "dma_start", out=out, in_=in_, **kw)
        return self.inc(eng, ins, sem, 16)


class DBuf:

    def __init__(self, p, bufs, loaded, prog, feng="gpsimd"):
        self.p = p
        self.bufs = bufs
        self.n = len(bufs)
        self.loaded = loaded
        self.prog = prog
        self.feng = feng
        self.fills = 0
        self.loaded_at = {}
        self.done_at = []

    def fill(self, srcs):
        if self.fills >= self.n:
            assert len(self.done_at) > self.fills - self.n, "DBuf.done() missing"
            self.p.wait(self.feng, self.prog, self.done_at[self.fills - self.n])
        slot = self.bufs[self.fills % self.n]
        for dst_fn, src in srcs:
            self.p.dma(self.feng, dst_fn(slot), src, self.loaded)
        self.loaded_at[self.fills] = self.loaded.v
        self.fills += 1
        return self.fills - 1

    def use(self, i, eng="tensor"):
        self.p.wait(eng, self.loaded, self.loaded_at[i])
        return self.bufs[i % self.n]

    def done(self):
        self.done_at.append(self.prog.v)


class Slot:

    def __init__(self, nslots):
        self.readers = [None] * nslots

    def producer_wait(self, p, eng, sem, slot):
        r = self.readers[slot]
        if r is not None:
            p.wait(eng, r[0], r[1])

    def consumed(self, sem, slot):
        self.readers[slot] = (sem, sem.v)



def emit_ln(p, bf, sm, grp, pre_t=None, dst=None):
    sv, st, sa = sm["v"], sm["t"], sm["a"]
    g0 = grp * G
    x_ln = bf["x_ln"] if dst is None else dst
    rows = bf["rows"]
    mu, ex2, var, lnv, rstd = (rows[:, i * G:(i + 1) * G] for i in range(5))
    ins = None
    for dc in range(DC):
        ins = p.do("vector", "tensor_copy", x_ln[:, dc, :], bf["x_res"][:, dc, g0:g0 + G])
    cast_done = p.inc("vector", ins, sv)
    ins = None
    bf["xsq_guard"].producer_wait(p, "scalar", st, 0)
    for dc in range(DC):
        ins = p.do("scalar", "activation", bf["xsq"][:, dc, :], bf["x_res"][:, dc, g0:g0 + G], AF.Square)
    sq_done = p.inc("scalar", ins, sa)

    p.wait("tensor", sv, cast_done)
    p.wait("tensor", sa, sq_done)
    if pre_t is not None:
        p.wait("tensor", pre_t[0], pre_t[1])
    for dc in range(DC):
        ins = p.do("tensor", "matmul", bf["ps_b"][0:1, :], bf["ones_bf"][:, :], x_ln[:, dc, :], start=(dc == 0), stop=(dc == DC - 1))
        if dc == 0:
            p.flush_pend("tensor", ins, limit=1)
    ins = None
    for dc in range(DC):
        ins = p.do("tensor", "matmul", bf["ps_d"][0:1, :], bf["ones_bf"][:, :], bf["xsq"][:, dc, :], start=(dc == 0), stop=(dc == DC - 1))
    stats_done = p.inc("tensor", ins, st)
    bf["xsq_guard"].consumed(st, 0)

    p.wait("vector", st, stats_done)
    p.do("vector", "tensor_scalar", mu, bf["ps_b"][0:1, :], 1.0 / D, None, ALU.mult)
    p.do("vector", "tensor_scalar", ex2, bf["ps_d"][0:1, :], 1.0 / D, None, ALU.mult)
    p.do("vector", "drain")
    p.do("vector", "tensor_tensor", var, mu, mu, ALU.mult)
    p.do("vector", "drain")
    ins = p.do("vector", "tensor_tensor", var, ex2, var, ALU.subtract)
    var_done = p.inc("vector", ins, sv)
    p.wait("scalar", sv, var_done)
    p.do("scalar", "activation", lnv, var, AF.Ln, bias=bf["rows"][0:1, 5 * G:5 * G + 1])
    p.do("scalar", "drain")
    ins = p.do("scalar", "activation", rstd, lnv, AF.Exp, scale=-0.5)
    rstd_done = p.inc("scalar", ins, sa)

    p.wait("tensor", sa, rstd_done)
    p.do("tensor", "matmul", bf["ps_a"][:, :], bf["ones_f32"][:, :], mu, start=True, stop=True)
    ins = p.do("tensor", "matmul", bf["ps_c"][:, :], bf["ones_f32"][:, :], rstd, start=True, stop=True)
    bcast_done = p.inc("tensor", ins, st)

    p.wait("vector", st, bcast_done)
    ins = None
    for dc in range(DC):
        p.do("vector", "tensor_tensor", bf["ln_tmp"][:, dc % 2, :], bf["x_res"][:, dc, g0:g0 + G], bf["ps_a"][:, :], ALU.subtract)
        if dc >= 1:
            ins = p.do("vector", "tensor_tensor", x_ln[:, dc - 1, :], bf["ln_tmp"][:, (dc - 1) % 2, :], bf["ps_c"][:, :], ALU.mult)
    p.do("vector", "drain")
    ins = p.do("vector", "tensor_tensor", x_ln[:, DC - 1, :], bf["ln_tmp"][:, (DC - 1) % 2, :], bf["ps_c"][:, :], ALU.mult)
    return p.inc("vector", ins, sv)


def emit_gating(p, bf, sm, xln_done):
    sv, st, sa = sm["v"], sm["t"], sm["a"]
    p.wait("tensor", sv, xln_done)
    ins = None
    for tb in range(4):
        for dc in range(DC):
            ins = p.do("tensor", "matmul", bf["ps_b"][:, 32 + tb * 8:40 + tb * 8],
                       bf["x_ln"][:, dc, tb * 128:(tb + 1) * 128],
                       bf["gw"][:, dc, :], start=(dc == 0), stop=(dc == DC - 1))
    logits_done = p.inc("tensor", ins, st)

    comb_sb = bf["comb_f32"]
    stmp = bf["scores_tmp"]
    srows = bf["srows"]
    mx = lambda tb: srows[:, tb, 0:1]
    smr = lambda tb: srows[:, tb, 1:2]
    rcp = lambda tb: srows[:, tb, 2:3]
    m1 = lambda tb: srows[:, tb, 3:4]
    m2 = lambda tb: srows[:, tb, 4:5]
    lg = lambda tb: bf["ps_b"][:, 32 + tb * 8:40 + tb * 8]

    p.wait("vector", st, logits_done)
    for tb in range(4):
        p.do("vector", "tensor_reduce", mx(tb), lg(tb), AX, ALU.max, negate=True)
    ins = None
    for tb in range(4):
        ins = p.do("vector", "tensor_copy", stmp[:, tb, :], lg(tb))
    pre_exp = p.inc("vector", ins, sv)
    p.wait("scalar", sv, pre_exp)
    ins = None
    for tb in range(4):
        ins = p.do("scalar", "activation", comb_sb[:, tb, :], stmp[:, tb, :], AF.Exp, bias=mx(tb))
    exp_done = p.inc("scalar", ins, sa)
    p.wait("vector", sa, exp_done)

    def step(fn):
        for tb in range(4):
            fn(tb)
        p.do("vector", "drain")

    step(lambda tb: p.do("vector", "tensor_reduce", smr(tb), comb_sb[:, tb, :], AX, ALU.add))
    step(lambda tb: p.do("vector", "reciprocal", rcp(tb), smr(tb)))
    step(lambda tb: p.do("vector", "tensor_scalar", comb_sb[:, tb, :], comb_sb[:, tb, :], rcp(tb), None, ALU.mult))
    step(lambda tb: p.do("vector", "tensor_reduce", m1(tb), comb_sb[:, tb, :], AX, ALU.max))
    step(lambda tb: p.do("vector", "tensor_scalar", stmp[:, tb, :], comb_sb[:, tb, :], m1(tb), None, ALU.is_ge))
    step(lambda tb: p.do("vector", "scalar_tensor_tensor", stmp[:, tb, :], stmp[:, tb, :], -2.0, comb_sb[:, tb, :], ALU.mult, ALU.add))
    step(lambda tb: p.do("vector", "tensor_reduce", m2(tb), stmp[:, tb, :], AX, ALU.max))
    step(lambda tb: p.do("vector", "tensor_scalar", stmp[:, tb, :], comb_sb[:, tb, :], m2(tb), None, ALU.is_ge))
    ins = None
    for tb in range(4):
        ins = p.do("vector", "tensor_tensor", comb_sb[:, tb, :], comb_sb[:, tb, :], stmp[:, tb, :], ALU.mult)
    comb_done = p.inc("vector", ins, sv)

    p.wait("tensor", sv, comb_done)
    ins = None
    for tb in range(4):
        ins = p.do("tensor", "transpose", bf["ps_d"][0:8, tb * 128:(tb + 1) * 128], comb_sb[:, tb, :], bf["ident"][:, :])
    tr_done = p.inc("tensor", ins, st)
    p.wait("vector", st, tr_done)
    ins = p.do("vector", "tensor_copy", bf["comb8"][:, :], bf["ps_d"][0:8, 0:G])
    c8_done = p.inc("vector", ins, sv)

    banks = [bf["ps_a"], bf["ps_b"], bf["ps_c"], bf["ps_d"]] + bf["ps_dn0"]
    p.wait("tensor", sv, c8_done)
    ins = None
    for e in range(NE):
        ins = p.do("tensor", "matmul", banks[e][:, :], bf["onehot"][:, e * 128:(e + 1) * 128], bf["comb8"][:, :], start=True, stop=True)
    be = p.inc("tensor", ins, st)
    p.wait("vector", st, be)
    ins = None
    for e in range(NE):
        ins = p.do("vector", "tensor_copy", bf["comb"][:, e, :], banks[e][:, :])
    p.inc("vector", ins, sv)
    return sv.v


def emit_ffn_group(p, bf, sm, w, grp, pre_t=None):
    sv, st, sa, sg = sm["v"], sm["t"], sm["a"], sm["g"]
    g0 = grp * G
    xln_done = emit_ln(p, bf, sm, grp, pre_t=pre_t)
    comb_done = emit_gating(p, bf, sm, xln_done)

    wgu, wda = bf["wgu_db"], bf["wda_db"]
    pp = bf["pp_slot"]
    qq = bf["qq_slot"]
    silu_guard = bf["silu_guard"]
    gtmp_guard = bf["gtmp_guard"]

    for ri, rnd in enumerate(ROUNDS):
        shared = rnd == ("sh",)
        n_hi = HIS if shared else HIE
        gdst = bf["gsh"] if shared else bf["grnd"]
        hb_sizes = []
        left = n_hi
        while left > 0:
            hb_sizes.append(min(2, left))
            left -= 2
        elist = [None] if shared else list(rnd)
        for ei, e in enumerate(elist):
            for hbi, hbsz in enumerate(hb_sizes):
                h0 = hbi * 2 * 128
                if shared:
                    gsrc = w["sw_gate"][hbi * 2:hbi * 2 + hbsz]
                    usrc = w["sw_up"][hbi * 2:hbi * 2 + hbsz]
                else:
                    gsrc = w["w_gate"][e, hbi * 2:hbi * 2 + hbsz]
                    usrc = w["w_up"][e, hbi * 2:hbi * 2 + hbsz]
                fgu = wgu.fill([
                    (lambda sl, n=hbsz: sl[0][:, 0:n, :, :], gsrc.rearrange("hi p dc h -> p hi dc h")),
                    (lambda sl, n=hbsz: sl[1][:, 0:n, :, :], usrc.rearrange("hi p dc h -> p hi dc h")),
                ])
                for hj in range(hbsz):
                    hi = hbi * 2 + hj
                    par = hi % 4
                    pg, pu = bf["gu_pairs"][par]
                    if ri == 0 and ei == 0 and hi <= 3:
                        p.wait("tensor", sv, comb_done)
                    else:
                        pp.producer_wait(p, "tensor", sv, par)
                        qq.producer_wait(p, "tensor", sv, 0)
                        qq.producer_wait(p, "tensor", sv, 1)
                    gsl, usl = wgu.use(fgu)
                    ins = None
                    for dc in range(DC):
                        ins = p.do("tensor", "matmul", pg[:, :], gsl[:, hj, dc, :], bf["x_ln"][:, dc, :], start=(dc == 0), stop=(dc == DC - 1))
                    for dc in range(DC):
                        ins = p.do("tensor", "matmul", pu[:, :], usl[:, hj, dc, :], bf["x_ln"][:, dc, :], start=(dc == 0), stop=(dc == DC - 1))
                    gu_done = p.inc("tensor", ins, st)
                    if hj == hbsz - 1:
                        wgu.done()

                    p.wait("scalar", st, gu_done)
                    silu_guard.producer_wait(p, "scalar", sv, par)
                    ins = p.do("scalar", "activation", bf["silu_sb"][:, par, :], pg[:, :], AF.Silu)
                    silu_done = p.inc("scalar", ins, sa)
                    p.wait("vector", sa, silu_done)
                    if shared:
                        ins = p.do("vector", "tensor_tensor", gdst[:, hi, :], pu[:, :], bf["silu_sb"][:, par, :], ALU.mult)
                        mul_done = p.inc("vector", ins, sv)
                        silu_guard.consumed(sv, par)
                        pp.consumed(sv, par)
                    else:
                        gtmp_guard.producer_wait(p, "vector", sg, par)
                        ins = p.do("vector", "tensor_tensor", bf["gtmp"][:, par, :], pu[:, :], bf["silu_sb"][:, par, :], ALU.mult)
                        mul_done = p.inc("vector", ins, sv)
                        silu_guard.consumed(sv, par)
                        pp.consumed(sv, par)
                        p.wait("gpsimd", sv, mul_done)
                        ins = p.do("gpsimd", "tensor_tensor", gdst[:, ei, hi, :], bf["gtmp"][:, par, :], bf["comb"][:, e, :], ALU.mult)
                        p.inc("gpsimd", ins, sg)
                        gtmp_guard.consumed(sg, par)
        g_ready_sem = sv if shared else sg
        g_ready = g_ready_sem.v

        for dq in range(4):
            d0 = dq * 512
            qset = 0 if dq % 2 == 0 else 1
            pq = bf["ps_dn0"] if qset == 0 else bf["ps_dn1"]
            subs = [0, 1] if shared else list(rnd)
            ins = None
            first_mm = True
            for si, sub in enumerate(subs):
                if shared:
                    srcw = w["sw_down"][sub, dq]
                else:
                    srcw = w["w_down"][sub, dq]
                fd = wda.fill([(lambda sl: sl[:, :, :], srcw)])
                if si == 0:
                    p.wait("tensor", g_ready_sem, g_ready)
                    qq.producer_wait(p, "tensor", sv, qset)
                dsl = wda.use(fd)
                last_si = si == len(subs) - 1
                for hi in range(HIE):
                    gsrc_t = bf["gsh"][:, (sub * HIE + hi) if shared else 0, :] if shared else bf["grnd"][:, si, hi, :]
                    for dt in range(4):
                        ins = p.do("tensor", "matmul", pq[dt][:, :], dsl[:, hi, dt * 128:(dt + 1) * 128], gsrc_t,
                                   start=first_mm, stop=(last_si and hi == HIE - 1))
                    first_mm = False
                p.inc("tensor", ins, st)
                wda.done()
            dq_done = st.v

            p.wait("vector", st, dq_done)
            ins = None
            for dt in range(4):
                dcg = dq * 4 + dt
                ins = p.do("vector", "scalar_tensor_tensor", bf["x_res"][:, dcg, g0:g0 + G],
                           pq[dt][:, :], 0.5, bf["x_res"][:, dcg, g0:g0 + G], ALU.mult, ALU.add)
            ev = p.inc("vector", ins, sv)
            qq.consumed(sv, qset)
            if qset == 0:
                pp.consumed(sv, 2)
                pp.consumed(sv, 3)
            else:
                pp.consumed(sv, 0)
                pp.consumed(sv, 1)
    return sv.v



def emit_qkv(p, bf, sm, w, outs):
    sv, st, sa, sg, so = sm["v"], sm["t"], sm["a"], sm["g"], sm["o"]
    wload, tl = sm["wload"], sm["tabload"]

    bf["xsq_guard"] = Slot(1)
    for grp in range(NGRP):
        emit_ln(p, bf, sm, grp, dst=bf["x_ln2"][:, grp])

    qk_g = Slot(2)
    ob_g = Slot(1)
    qpp = Slot(2)
    vps = Slot(1)
    shuffle_mask = [i ^ 1 for i in range(32)]

    for wname in ("W_Q", "W_K", "W_V"):
        for half in range(2):
            p.wait("gpsimd", st)
            p.dma("gpsimd", bf["wfull"][:, :, :],
                  w[wname][:, half * 1024:(half + 1) * 1024].rearrange("(dc p) o -> p dc o", p=128), wload)
            for grp in range(NGRP):
                if wname in ("W_Q", "W_K"):
                    tabs = bf["cs_q"] if wname == "W_Q" else bf["cs_k"]
                    p.wait("sync", sg)
                    p.wait("sync", sv)
                    p.dma("sync", bf["ctab"][:, :, :], tabs[0].rearrange("(dc p) t -> p dc t", p=128)[:, :, grp * G:grp * G + G], tl)
                    p.dma("sync", bf["stab"][:, :, :], tabs[1].rearrange("(dc p) t -> p dc t", p=128)[:, :, grp * G:grp * G + G], tl)
                    p.wait("tensor", wload)
                    for dto in range(8):
                        dt = half * 8 + dto
                        par = dt % 2
                        pq = bf["ps_a"] if par == 0 else bf["ps_c"]
                        if dto <= 1:
                            p.wait("tensor", sv)
                        qpp.producer_wait(p, "tensor", sv, par)
                        ins = None
                        for dc in range(DC):
                            ins = p.do("tensor", "matmul", pq[:, :], bf["wfull"][:, dc, dto * 128:(dto + 1) * 128],
                                       bf["x_ln2"][:, grp, dc, :], start=(dc == 0), stop=(dc == DC - 1))
                            if dc == 0:
                                p.flush_pend("tensor", ins, limit=1)
                        mm_done = p.inc("tensor", ins, st)
                        p.wait("vector", st, mm_done)
                        if dto == 0:
                            p.wait("vector", tl)
                            ob_g.producer_wait(p, "vector", so, 0)
                        qk_g.producer_wait(p, "vector", sv, par)
                        p.do("vector", "tensor_copy", bf["qk_t"][:, par, :], pq[:, :])
                        p.do("vector", "drain")
                        ins = p.do("vector", "stream_shuffle", bf["qk_r"][:, par, :], bf["qk_t"][:, par, :], shuffle_mask)
                        sh_done = p.inc("vector", ins, sv)
                        qpp.consumed(sv, par)
                        p.wait("gpsimd", sv, sh_done)
                        ins = p.do("gpsimd", "tensor_tensor", bf["qk_a"][:, par, :], bf["qk_t"][:, par, :], bf["ctab"][:, dt, :], ALU.mult)
                        a_done = p.inc("gpsimd", ins, sg)
                        p.wait("vector", sg, a_done)
                        p.do("vector", "tensor_tensor", bf["qk_r"][:, par, :], bf["qk_r"][:, par, :], bf["stab"][:, dt, :], ALU.mult)
                        p.do("vector", "drain")
                        ins = p.do("vector", "tensor_tensor", bf["qk_out"][:, dto, :], bf["qk_r"][:, par, :], bf["qk_a"][:, par, :], ALU.add)
                        p.inc("vector", ins, sv)
                        qk_g.consumed(sv, par)
                    p.wait("sync", sv)
                    dst = outs["qt"] if wname == "W_Q" else outs["kt"]
                    p.dma("sync", dst.rearrange("(dc p) t -> p dc t", p=128)[:, half * 8:(half + 1) * 8, grp * G:grp * G + G],
                          bf["qk_out"][:, 0:8, :], so)
                    ob_g.consumed(so, 0)
                else:
                    p.wait("tensor", wload)
                    for tb in range(4):
                        if tb == 0:
                            p.wait("tensor", sv)
                        vps.producer_wait(p, "tensor", sv, 0)
                        ins = None
                        for dc in range(DC):
                            for ds in range(2):
                                ins = p.do("tensor", "matmul", bf["ps_dn0"][ds][:, :],
                                           bf["x_ln2"][:, grp, dc, tb * 128:(tb + 1) * 128],
                                           bf["wfull"][:, dc, ds * 512:(ds + 1) * 512],
                                           start=(dc == 0), stop=(dc == DC - 1))
                        v_done = p.inc("tensor", ins, st)
                        p.wait("vector", st, v_done)
                        ob_g.producer_wait(p, "vector", so, 0)
                        ins = None
                        for ds in range(2):
                            ins = p.do("vector", "tensor_copy", bf["v_out"][:, ds * 512:(ds + 1) * 512], bf["ps_dn0"][ds][:, :])
                        ve = p.inc("vector", ins, sv)
                        vps.consumed(sv, 0)
                        p.wait("sync", sv, ve)
                        p.dma("sync", outs["v"][grp * G + tb * 128:grp * G + (tb + 1) * 128, half * 1024:(half + 1) * 1024],
                              bf["v_out"][:, 0:1024], so)
                        ob_g.consumed(so, 0)
    p.wait("sync", so)



def emit_attention(p, bf, sm, ind, m_ext=None):
    sv, st = sm["v"], sm["t"]
    kload, vload, dload = sm["kload"], sm["vload"], sm["dload"]
    MT = (m_ext if m_ext is not None else S) // 128
    KB = 4
    pband = bf["pband"]
    kdb = DBuf(p, bf["kbufs"], kload, sm["t"], feng="gpsimd")
    ddb = DBuf(p, bf["dbufs"], dload, sm["v"], feng="sync")
    vdb = DBuf(p, bf["vbufs"], vload, sm["t"], feng="gpsimd")
    app = Slot(2)
    aq = Slot(1)

    for grp in range(NGRP):
        n0 = grp * G
        for mtb in range(MT // KB):
            fk = kdb.fill([(lambda sl: sl[:, :, :],
                            ind["kt_full"].rearrange("(dc p) m -> p dc m", p=128)[:, :, mtb * KB * 128:(mtb + 1) * KB * 128])])
            fdt = ddb.fill([(lambda sl: sl[:, :, :],
                             ind["dtab"][mtb * KB * 128:(mtb + 1) * KB * 128, n0:n0 + G].rearrange("(mc p) n -> p mc n", p=128))])
            for mi in range(KB):
                mt = mtb * KB + mi
                par = mt % 2
                pq = pband[par]
                app.producer_wait(p, "tensor", sv, par)
                if grp == 0 and mt <= 1:
                    p.wait("tensor", sv)
                if mt <= 1:
                    aq.producer_wait(p, "tensor", sv, 0)
                ksl = kdb.use(fk)
                ins = None
                for dc in range(DC):
                    ins = p.do("tensor", "matmul", pq[:, :], ksl[:, dc, mi * 128:(mi + 1) * 128],
                               bf["qt_sb"][:, dc, n0:n0 + G], start=(dc == 0), stop=(dc == DC - 1))
                qk_done = p.inc("tensor", ins, st)
                if mi == KB - 1:
                    kdb.done()
                p.wait("vector", st, qk_done)
                if mi == 0:
                    p.wait("vector", dload, ddb.loaded_at[fdt])
                dsl = bf["dbufs"][fdt % 2]
                ins = p.do("vector", "tensor_tensor", bf["att_sb"][:, mt, :], pq[:, :], dsl[:, mi, :], ALU.mult)
                p.inc("vector", ins, sv)
                app.consumed(sv, par)
                if mi == KB - 1:
                    ddb.done()
        att_all = sv.v
        for dh in range(2):
            ins = None
            for mtb in range(MT // KB):
                fv = vdb.fill([(lambda sl: sl[:, :, :],
                                ind["v_full"][mtb * KB * 128:(mtb + 1) * KB * 128, dh * 1024:(dh + 1) * 1024].rearrange("(mc p) d -> p mc d", p=128))])
                if mtb == 0:
                    p.wait("tensor", sv, att_all)
                    aq.producer_wait(p, "tensor", sv, 0)
                    app.producer_wait(p, "tensor", sv, 0)
                    app.producer_wait(p, "tensor", sv, 1)
                vsl = vdb.use(fv)
                for mi in range(KB):
                    mt = mtb * KB + mi
                    for dt in range(8):
                        ins = p.do("tensor", "matmul", pband[dt][:, :], vsl[:, mi, dt * 128:(dt + 1) * 128],
                                   bf["att_sb"][:, mt, :], start=(mt == 0), stop=(mt == MT - 1))
                p.inc("tensor", ins, st)
                vdb.done()
            oh_done = st.v
            p.wait("vector", st, oh_done)
            ins = None
            for dt in range(8):
                dcg = dh * 8 + dt
                ins = p.do("vector", "tensor_tensor", bf["x_res"][:, dcg, n0:n0 + G], pband[dt][:, :],
                           bf["x_res"][:, dcg, n0:n0 + G], ALU.add)
            ev = p.inc("vector", ins, sv)
            aq.consumed(sv, 0)
            app.consumed(sv, 0)
            app.consumed(sv, 1)



import contextlib


def run_phases(nc, sems, phases):
    items = [(k, c) for k, c in sems.items() if isinstance(c, Ctr)]
    for emitfn in phases:
        snap = {k: c.v for k, c in items}
        final = None
        with nc.Block() as blk:
            for eng in ENGS:
                for k, c in items:
                    c.v = snap[k]
                def body(e, eng=eng, fn=emitfn):
                    fn(P(nc, eng, sems))
                getattr(blk, eng)(body)
                f = {k: c.v for k, c in items}
                if final is None:
                    final = f
                else:
                    assert f == final, f"lockstep divergence in pass {eng}: " + str(
                        {k: (final[k], f[k]) for k in f if f[k] != final[k]})


def alloc_sems(nc, stack):
    names = ["v", "t", "a", "g", "o", "wload", "tabload", "kload", "vload", "dload",
             "kfree", "vfree", "dfree", "xload", "gload", "qtload", "att_q", "vq_free",
             "wga_l", "wga_f", "wua_l", "wua_f", "wda_l", "wda_f"]
    sems = {n: Ctr(stack.enter_context(nc.semaphore(n))) for n in names}
    for pair in ["pp_free", "qq_free", "att_pp"]:
        for i in range(2):
            sems[f"{pair}{i}"] = Ctr(stack.enter_context(nc.semaphore(f"{pair}{i}")))
    sems["pp_free"] = [sems["pp_free0"], sems["pp_free1"]]
    sems["qq_free"] = [sems["qq_free0"], sems["qq_free1"]]
    sems["att_pp"] = [sems["att_pp0"], sems["att_pp1"]]
    return sems


def alloc_core_bufs(nc, stack):
    bf = {}
    def S_(name, shape, dt):
        return stack.enter_context(nc.sbuf_tensor(name, shape, dt))
    def PS(name, shape):
        return stack.enter_context(nc.psum_tensor(name, shape, F32))
    bf["x_res"] = S_("x_res", [128, DC, TOK], F32)
    bf["ln_tmp"] = S_("ln_tmp", [128, 2, G], F32)
    bf["rows"] = S_("rows", [1, 5 * G + 8], F32)
    bf["srows"] = S_("srows", [128, 4, 8], F32)
    bf["ones_bf"] = S_("ones_bf", [128, 1], BF16)
    bf["ones_f32"] = S_("ones_f32", [1, 128], F32)
    bf["ps_a"] = PS("ps_a", [128, G])
    bf["ps_b"] = PS("ps_b", [128, G])
    bf["ps_c"] = PS("ps_c", [128, G])
    bf["ps_d"] = PS("ps_d", [128, G])
    bf["ps_dn0"] = [PS(f"ps_dn0_{i}", [128, G]) for i in range(4)]
    bf["ps_dn1"] = [bf["ps_a"], bf["ps_b"], bf["ps_c"], bf["ps_d"]]
    bf["pband"] = [bf["ps_a"], bf["ps_b"], bf["ps_c"], bf["ps_d"]] + bf["ps_dn0"]
    return bf


def alloc_ffn_bufs(nc, stack):
    bf = {}
    def S_(name, shape, dt):
        return stack.enter_context(nc.sbuf_tensor(name, shape, dt))
    bf["x_ln"] = S_("x_ln", [128, DC, G], BF16)
    bf["xsq"] = S_("xsq", [128, DC, G], BF16)
    bf["grnd"] = S_("grnd", [128, 2, HIE, G], BF16)
    bf["gsh"] = bf["grnd"].rearrange("p a b t -> p (a b) t")
    bf["comb"] = S_("comb", [128, NE, G], BF16)
    bf["comb_f32"] = S_("comb_f32", [128, 4, 8], F32)
    bf["scores_tmp"] = S_("scores_tmp", [128, 4, 8], F32)
    bf["comb8"] = S_("comb8", [8, G], BF16)
    bf["silu_sb"] = S_("silu_sb", [128, 4, G], BF16)
    bf["gtmp"] = S_("gtmp", [128, 4, G], BF16)
    bf["wga"] = [S_(f"wga{i}", [128, 2, DC, 128], BF16) for i in range(2)]
    bf["wua"] = [S_(f"wua{i}", [128, 2, DC, 128], BF16) for i in range(2)]
    bf["wda"] = [S_(f"wda{i}", [128, HIE, G], BF16) for i in range(2)]
    return bf


def alloc_qkv_bufs(nc, stack):
    bf = {}
    def S_(name, shape, dt):
        return stack.enter_context(nc.sbuf_tensor(name, shape, dt))
    bf["x_ln2"] = S_("x_ln2", [128, NGRP, DC, G], BF16)
    bf["xsq"] = S_("xsq_q", [128, DC, G], BF16)
    bf["wfull"] = S_("wfull", [128, DC, D // 2], BF16)
    bf["ctab"] = S_("ctab", [128, DC, G], BF16)
    bf["stab"] = S_("stab", [128, DC, G], BF16)
    bf["qk_t"] = S_("qk_t", [128, 2, G], BF16)
    bf["qk_r"] = S_("qk_r", [128, 2, G], BF16)
    bf["qk_a"] = S_("qk_a", [128, 2, G], BF16)
    bf["qk_out"] = S_("qk_out", [128, 8, G], BF16)
    bf["v_out"] = bf["qk_out"].rearrange("p a t -> p (a t)")[:, 0:D]
    return bf


def alloc_attn_bufs(nc, stack):
    bf = {}
    def S_(name, shape, dt):
        return stack.enter_context(nc.sbuf_tensor(name, shape, dt))
    bf["qt_sb"] = S_("qt_sb", [128, DC, TOK], BF16)
    bf["att_sb"] = S_("att_sb", [128, S // 128, G], BF16)
    bf["kbufs"] = [S_(f"kbuf{i}", [128, DC, 512], BF16) for i in range(2)]
    bf["vbufs"] = [S_(f"vbuf{i}", [128, 4, 1024], BF16) for i in range(2)]
    bf["dbufs"] = [S_(f"dbuf{i}", [128, 4, G], BF16) for i in range(2)]
    return bf


def emit_const_init(p, bf, sm):
    p.do("vector", "memset", bf["ones_bf"][:, :], 1.0)
    p.do("vector", "memset", bf["rows"][0:1, 5 * G:5 * G + 8], EPS)
    ins = p.do("vector", "memset", bf["ones_f32"][:, :], 1.0)
    return p.inc("vector", ins, sm["v"])


def ffn_weight_aps(nc, kind):
    w = {}
    w["w_gate"] = nc.dram_tensor("w_gate", [NE, HIE, 128, DC, 128], F32, kind=kind).ap()
    w["w_up"] = nc.dram_tensor("w_up", [NE, HIE, 128, DC, 128], F32, kind=kind).ap()
    w["w_down"] = nc.dram_tensor("w_down", [NE, 4, 128, HIE, 512], F32, kind=kind).ap()
    w["sw_gate"] = nc.dram_tensor("sw_gate", [HIS, 128, DC, 128], F32, kind=kind).ap()
    w["sw_up"] = nc.dram_tensor("sw_up", [HIS, 128, DC, 128], F32, kind=kind).ap()
    w["sw_down"] = nc.dram_tensor("sw_down", [2, 4, 128, HIE, 512], F32, kind=kind).ap()
    w["gate_w"] = nc.dram_tensor("gate_w", [D, NE], F32, kind=kind).ap()
    return w


def stage_weights(wmap):
    o = {}
    wg = np.ascontiguousarray(wmap["w_gate"]).reshape(NE, DC, 128, HIE, 128)
    o["w_gate"] = np.ascontiguousarray(wg.transpose(0, 3, 2, 1, 4))
    wu = np.ascontiguousarray(wmap["w_up"]).reshape(NE, DC, 128, HIE, 128)
    o["w_up"] = np.ascontiguousarray(wu.transpose(0, 3, 2, 1, 4))
    wd = np.ascontiguousarray(wmap["w_down"]).reshape(NE, HIE, 128, 4, 512)
    o["w_down"] = np.ascontiguousarray(wd.transpose(0, 3, 2, 1, 4))
    sg_ = np.ascontiguousarray(wmap["sw_gate"]).reshape(DC, 128, HIS, 128)
    o["sw_gate"] = np.ascontiguousarray(sg_.transpose(2, 1, 0, 3))
    su = np.ascontiguousarray(wmap["sw_up"]).reshape(DC, 128, HIS, 128)
    o["sw_up"] = np.ascontiguousarray(su.transpose(2, 1, 0, 3))
    sd = np.ascontiguousarray(wmap["sw_down"]).reshape(2, HIE, 128, 4, 512)
    o["sw_down"] = np.ascontiguousarray(sd.transpose(0, 3, 2, 1, 4))
    o["gate_w"] = wmap["gate_w"]
    return o


def stage_wqkv(W):
    wq = np.ascontiguousarray(W).reshape(DC, 128, 4, 512)
    return np.ascontiguousarray(wq.transpose(2, 1, 0, 3))


def emit_barrier(p, sems):
    for k, c in sems.items():
        if isinstance(c, Ctr):
            p.wait(p.active, c)


def emit_ffn_preamble(p, bf, sm, w):
    p.dma("sync", bf["ident"][:, :], bf["ident_dram"][:, :], sm["xload"])
    p.dma("sync", bf["onehot"][:, :], bf["onehot_dram"][:, :], sm["xload"])
    p.dma("gpsimd", bf["gw"][:, :, :], w["gate_w"].rearrange("(dc p) e -> p dc e", p=128), sm["gload"])
    for e in ENGS:
        p.wait(e, sm["xload"])
        p.wait(e, sm["gload"])


def alloc_gating_consts(nc, stack, bf):
    bf["ident"] = stack.enter_context(nc.sbuf_tensor("ident_sb", [128, 128], F32))
    bf["onehot"] = stack.enter_context(nc.sbuf_tensor("onehot_sb", [8, NE * 128], BF16))
    bf["gw"] = stack.enter_context(nc.sbuf_tensor("gw_sb", [128, DC, NE], BF16))


def _mk_nc():
    return bass.Bass("TRN2", target_bir_lowering=False, detect_race_conditions=False)


def build_l1(ffn_only=False):
    nc = _mk_nc()
    stack = contextlib.ExitStack()
    x0 = nc.dram_tensor("x0", [D, TOK], F32, kind="ExternalInput").ap()
    w = ffn_weight_aps(nc, "ExternalInput")
    w["W_Q"] = nc.dram_tensor("W_Q", [D, D], F32, kind="ExternalInput").ap()
    w["W_K"] = nc.dram_tensor("W_K", [D, D], F32, kind="ExternalInput").ap()
    w["W_V"] = nc.dram_tensor("W_V", [D, D], F32, kind="ExternalInput").ap()
    ident = nc.dram_tensor("ident", [128, 128], F32, kind="ExternalInput").ap()
    onehot = nc.dram_tensor("onehot", [8, NE * 128], BF16, kind="ExternalInput").ap()
    cs_q0 = nc.dram_tensor("cq", [2, D, TOK], BF16, kind="ExternalInput").ap()
    cs_k0 = nc.dram_tensor("ck", [2, D, TOK], BF16, kind="ExternalInput").ap()
    x1_out = nc.dram_tensor("x1", [D, TOK], F32, kind="ExternalOutput").ap()
    qt_out = nc.dram_tensor("qt", [D, TOK], BF16, kind="ExternalOutput").ap()
    kt_out = nc.dram_tensor("kt", [D, TOK], BF16, kind="ExternalOutput").ap()
    v_out = nc.dram_tensor("v", [TOK, D], BF16, kind="ExternalOutput").ap()

    bf = alloc_core_bufs(nc, stack)
    bf["ident_dram"] = ident
    bf["onehot_dram"] = onehot
    bf["cs_q"] = (cs_q0[0], cs_q0[1])
    bf["cs_k"] = (cs_k0[0], cs_k0[1])
    sems = alloc_sems(nc, stack)

    def phase_load(p):
        emit_const_init(p, bf, sems)
        p.dma("sync", bf["x_res"][:, :, :], x0.rearrange("(dc p) t -> p dc t", p=128), sems["xload"])
        emit_ffn_preamble(p, bf, sems, w)
        p.wait("tensor", sems["v"])

    def phase_ffn(p):
        bf["silu_guard"] = Slot(4)
        bf["gtmp_guard"] = Slot(4)
        bf["xsq_guard"] = Slot(1)
        bf["wgu_db"] = DBuf(p, list(zip(bf["wga"], bf["wua"])), sems["wga_l"], sems["t"])
        bf["wda_db"] = DBuf(p, bf["wda"], sems["wda_l"], sems["t"])
        bf["pp_slot"] = Slot(4)
        bf["qq_slot"] = Slot(2)
        bf["gu_pairs"] = [(bf["ps_a"], bf["ps_b"]), (bf["ps_c"], bf["ps_d"]),
                          (bf["ps_dn0"][0], bf["ps_dn0"][1]), (bf["ps_dn0"][2], bf["ps_dn0"][3])]
        for grp in range(NGRP):
            emit_ffn_group(p, bf, sems, w, grp)

    with contextlib.ExitStack() as fstack:
        alloc_gating_consts(nc, fstack, bf)
        for k, v_ in alloc_ffn_bufs(nc, fstack).items():
            bf[k] = v_
        run_phases(nc, sems, [phase_load, phase_ffn])

    if not ffn_only:
        def phase_qkv(p):
            emit_barrier(p, sems)
            emit_qkv(p, bf, sems, w, {"qt": qt_out, "kt": kt_out, "v": v_out})
        with contextlib.ExitStack() as qstack:
            for k, v_ in alloc_qkv_bufs(nc, qstack).items():
                bf[k] = v_
            run_phases(nc, sems, [phase_qkv])

    def phase_store(p):
        emit_barrier(p, sems)
        p.dma("sync", x1_out.rearrange("(dc p) t -> p dc t", p=128), bf["x_res"][:, :, :], sems["o"])
        p.wait("sync", sems["o"])

    run_phases(nc, sems, [phase_store])
    stack.close()
    return nc


def build_l2(attn_only=False):
    nc = _mk_nc()
    stack = contextlib.ExitStack()
    x1 = nc.dram_tensor("x1", [D, TOK], F32, kind="ExternalInput").ap()
    qt_in = nc.dram_tensor("qt", [D, TOK], BF16, kind="ExternalInput").ap()
    kt_full = nc.dram_tensor("kt_full", [D, S], BF16, kind="ExternalInput").ap()
    v_full = nc.dram_tensor("v_full", [S, D], BF16, kind="ExternalInput").ap()
    dtab = nc.dram_tensor("dtab", [S, TOK], BF16, kind="ExternalInput").ap()
    w = ffn_weight_aps(nc, "ExternalInput")
    ident = nc.dram_tensor("ident", [128, 128], F32, kind="ExternalInput").ap()
    onehot = nc.dram_tensor("onehot", [8, NE * 128], BF16, kind="ExternalInput").ap()
    xout = nc.dram_tensor("xout", [D, TOK], F32, kind="ExternalOutput").ap()

    bf = alloc_core_bufs(nc, stack)
    bf["ident_dram"] = ident
    bf["onehot_dram"] = onehot
    sems = alloc_sems(nc, stack)

    def phase_load(p):
        emit_const_init(p, bf, sems)
        p.dma("sync", bf["x_res"][:, :, :], x1.rearrange("(dc p) t -> p dc t", p=128), sems["xload"])
        p.dma("sync", bf["qt_sb"][:, :, :], qt_in.rearrange("(dc p) t -> p dc t", p=128), sems["xload"])
        for e in ENGS:
            p.wait(e, sems["xload"])
        p.wait("tensor", sems["v"])

    def phase_attn(p):
        emit_attention(p, bf, sems, {"kt_full": kt_full, "v_full": v_full, "dtab": dtab})

    with contextlib.ExitStack() as astack:
        for k, v_ in alloc_attn_bufs(nc, astack).items():
            bf[k] = v_
        run_phases(nc, sems, [phase_load, phase_attn])

    if not attn_only:
        def phase_ffn(p):
            emit_barrier(p, sems)
            emit_ffn_preamble(p, bf, sems, w)
            bf["silu_guard"] = Slot(4)
            bf["gtmp_guard"] = Slot(4)
            bf["xsq_guard"] = Slot(1)
            bf["wgu_db"] = DBuf(p, list(zip(bf["wga"], bf["wua"])), sems["wga_l"], sems["t"])
            bf["wda_db"] = DBuf(p, bf["wda"], sems["wda_l"], sems["t"])
            bf["pp_slot"] = Slot(4)
            bf["qq_slot"] = Slot(2)
            bf["gu_pairs"] = [(bf["ps_a"], bf["ps_b"]), (bf["ps_c"], bf["ps_d"]),
                              (bf["ps_dn0"][0], bf["ps_dn0"][1]), (bf["ps_dn0"][2], bf["ps_dn0"][3])]
            for grp in range(NGRP):
                pre = (sems["v"], sems["v"].v) if grp == 0 else None
                emit_ffn_group(p, bf, sems, w, grp, pre_t=pre)
        with contextlib.ExitStack() as fstack:
            alloc_gating_consts(nc, fstack, bf)
            for k, v_ in alloc_ffn_bufs(nc, fstack).items():
                bf[k] = v_
            run_phases(nc, sems, [phase_ffn])

    def phase_store(p):
        emit_barrier(p, sems)
        p.dma("sync", xout.rearrange("(dc p) t -> p dc t", p=128), bf["x_res"][:, :, :], sems["o"])
        p.wait("sync", sems["o"])

    run_phases(nc, sems, [phase_store])
    stack.close()
    return nc



def _consts():
    ident = np.eye(128, dtype=np.float32)
    onehot = np.zeros((8, NE * 128), dtype=ml_dtypes.bfloat16)
    for e in range(NE):
        onehot[e, e * 128:(e + 1) * 128] = 1.0
    return ident, onehot


def kernel_unfused(L0_attn_X, L0_attn_W_Q, L0_attn_W_K, L0_attn_W_V, L0_ffn_gate_w,
           L0_ffn_w_gate, L0_ffn_w_up, L0_ffn_w_down, L0_ffn_sw_gate,
           L0_ffn_sw_up, L0_ffn_sw_down):
    X = np.asarray(L0_attn_X)
    ident, onehot = _consts()
    wmap = {
        "w_gate": np.ascontiguousarray(L0_ffn_w_gate),
        "w_up": np.ascontiguousarray(L0_ffn_w_up),
        "w_down": np.ascontiguousarray(L0_ffn_w_down),
        "sw_gate": np.ascontiguousarray(L0_ffn_sw_gate),
        "sw_up": np.ascontiguousarray(L0_ffn_sw_up),
        "sw_down": np.ascontiguousarray(L0_ffn_sw_down),
        "gate_w": np.ascontiguousarray(L0_ffn_gate_w),
    }

    nc1 = build_l1()
    in1 = []
    for c in range(8):
        b, r0 = c // 4, (c % 4) * TOK
        rows = np.arange(r0, r0 + TOK)
        cq = np.stack(host_xpos_tables(rows, downscale=False))
        ck = np.stack(host_xpos_tables(rows, downscale=True))
        m = dict(wmap)
        m.update({
            "x0": fm(X[b, r0:r0 + TOK, :]),
            "W_Q": np.ascontiguousarray(L0_attn_W_Q),
            "W_K": np.ascontiguousarray(L0_attn_W_K),
            "W_V": np.ascontiguousarray(L0_attn_W_V),
            "ident": ident, "onehot": onehot, "cq": cq, "ck": ck,
        })
        in1.append(m)
    r1 = run_bass_kernel_spmd(nc1, in1, core_ids=list(range(8)))

    ktb = [np.concatenate([r1.results[b * 4 + i]["kt"] for i in range(4)], axis=1) for b in range(B)]
    vtb = [np.concatenate([r1.results[b * 4 + i]["v"] for i in range(4)], axis=0) for b in range(B)]

    nc2 = build_l2()
    in2 = []
    for c in range(8):
        b, r0 = c // 4, (c % 4) * TOK
        rows = np.arange(r0, r0 + TOK)
        m = dict(wmap)
        m.update({
            "x1": r1.results[c]["x1"],
            "qt": r1.results[c]["qt"],
            "kt_full": ktb[b],
            "v_full": vtb[b],
            "dtab": host_decay(rows, S).astype(ml_dtypes.bfloat16),
            "ident": ident, "onehot": onehot,
        })
        in2.append(m)
    r2 = run_bass_kernel_spmd(nc2, in2, core_ids=list(range(8)))

    out = np.empty((B, S, D), dtype=np.float32)
    for c in range(8):
        b, r0 = c // 4, (c % 4) * TOK
        out[b, r0:r0 + TOK, :] = r2.results[c]["xout"].T
    return out



RA = 2048


def emit_qkv2(p, bf, sm, w, tabs_c, tabs_k, n_groups, do_q, qt_dst, kv_store, tok_base):
    sv, st, sa, sg, so = sm["v"], sm["t"], sm["a"], sm["g"], sm["o"]
    wload, tl = sm["wload"], sm["tabload"]

    bf["xsq_guard"] = Slot(1)
    for grp in range(n_groups):
        emit_ln(p, bf, sm, grp, dst=bf["x_ln2"][:, grp])

    qk_g = Slot(4)
    ob_g = Slot(1)
    qpp = Slot(4)
    vps = Slot(2)
    tab_g = Slot(1)
    wdb = DBuf(p, bf["wq4"], wload, sm["t"], feng="gpsimd")
    shuffle_mask = [i ^ 1 for i in range(32)]
    wnames = (("W_Q", "W_K", "W_V") if do_q else ("W_K", "W_V"))
    banks4 = [bf["ps_a"], bf["ps_b"], bf["ps_c"], bf["ps_d"]]

    fills = {}
    order = [(wn, q) for wn in wnames for q in range(4)]
    fills[order[0]] = wdb.fill([(lambda sl: sl[:, :, :], w[order[0][0]][0])])
    for oi, (wname, q) in enumerate(order):
        if True:
            fw = fills[(wname, q)]
            if oi + 1 < len(order):
                wn2, q2 = order[oi + 1]
                fills[(wn2, q2)] = wdb.fill([(lambda sl, wn2=wn2, q2=q2: sl[:, :, :], w[wn2][q2])])
            for grp in range(n_groups):
                t0 = tok_base + grp * G
                if wname in ("W_Q", "W_K"):
                    tabs = (tabs_c if wname == "W_Q" else tabs_k)
                    tab_g.producer_wait(p, "sync", sv, 0)
                    p.wait("sync", sv)
                    p.dma("sync", bf["ctab"][:, :, :], tabs[0].rearrange("(dc p) t -> p dc t", p=128)[:, q * 4:(q + 1) * 4, t0:t0 + G], tl)
                    p.dma("sync", bf["stab"][:, :, :], tabs[1].rearrange("(dc p) t -> p dc t", p=128)[:, q * 4:(q + 1) * 4, t0:t0 + G], tl)
                    wsl = wdb.use(fw)
                    ins = None
                    for dto in range(4):
                        par = dto
                        pq = banks4[par]
                        if q == 0 and grp == 0 and dto <= 3:
                            p.wait("tensor", sv)
                        qpp.producer_wait(p, "tensor", sv, par)
                        for dc in range(DC):
                            ins = p.do("tensor", "matmul", pq[:, :], wsl[:, dc, dto * 128:(dto + 1) * 128],
                                       bf["x_ln2"][:, grp, dc, :], start=(dc == 0), stop=(dc == DC - 1))
                        mm_done = p.inc("tensor", ins, st)
                        p.wait("vector", st, mm_done)
                        if dto == 0:
                            p.wait("vector", tl)
                            ob_g.producer_wait(p, "vector", so, 0)
                        qk_g.producer_wait(p, "vector", sv, par)
                        p.do("vector", "tensor_copy", bf["qk_t"][:, par, :], pq[:, :])
                        p.do("vector", "drain")
                        ins = p.do("vector", "stream_shuffle", bf["qk_r"][:, par, :], bf["qk_t"][:, par, :], shuffle_mask)
                        p.inc("vector", ins, sv)
                        qpp.consumed(sv, par)
                        ins = p.do("vector", "tensor_tensor", bf["qk_a"][:, par, :], bf["qk_t"][:, par, :], bf["ctab"][:, dto, :], ALU.mult)
                        p.do("vector", "drain")
                        p.do("vector", "tensor_tensor", bf["qk_r"][:, par, :], bf["qk_r"][:, par, :], bf["stab"][:, dto, :], ALU.mult)
                        p.do("vector", "drain")
                        ins = p.do("vector", "tensor_tensor", bf["qk_out"][:, dto, :], bf["qk_r"][:, par, :], bf["qk_a"][:, par, :], ALU.add)
                        a_done = p.inc("vector", ins, sv)
                        if dto == 3:
                            tab_g.consumed(sv, 0)
                        qk_g.consumed(sv, par)
                    p.wait("sync", sv)
                    if wname == "W_Q":
                        p.dma("sync", qt_dst.rearrange("(dc p) t -> p dc t", p=128)[:, q * 4:(q + 1) * 4, grp * G:grp * G + G],
                              bf["qk_out"][:, 0:4, :], so)
                        ob_g.consumed(so, 0)
                    else:
                        ksem = kv_store("k", grp, q, bf["qk_out"][:, 0:4, :], None)
                        ob_g.consumed(ksem if ksem is not None else so, 0)
                else:
                    wsl = wdb.use(fw)
                    for tb in range(4):
                        par = tb % 2
                        if q == 0 and grp == 0 and tb <= 1:
                            p.wait("tensor", sv)
                        vps.producer_wait(p, "tensor", sv, par)
                        ins = None
                        for dc in range(DC):
                            ins = p.do("tensor", "matmul", bf["ps_dn0"][par][:, :],
                                       bf["x_ln2"][:, grp, dc, tb * 128:(tb + 1) * 128],
                                       wsl[:, dc, :],
                                       start=(dc == 0), stop=(dc == DC - 1))
                        v_done = p.inc("tensor", ins, st)
                        p.wait("vector", st, v_done)
                        ob_g.producer_wait(p, "vector", so, 0)
                        ins = p.do("vector", "tensor_copy", bf["v_out"][:, 0:512], bf["ps_dn0"][par][:, :])
                        ve = p.inc("vector", ins, sv)
                        vps.consumed(sv, par)
                        p.wait("sync", sv, ve)
                        vsem = kv_store("v", grp, q, bf["v_out"][:, 0:512], tb)
                        ob_g.consumed(vsem if vsem is not None else so, 0)
            wdb.done()
    p.wait("sync", so)


def run_phases_roled(nc, sems, regs, phases):
    items = [(k, c) for k, c in sems.items() if isinstance(c, Ctr)]
    state = {k: c.v for k, c in items}
    rstate = None
    for kind, fn in phases:
        if kind == "common":
            snap = dict(state)
            final = None
            with nc.Block() as blk:
                for eng in ENGS:
                    for k, c in items:
                        c.v = snap[k]
                    def body(e, eng=eng, fn=fn):
                        fn(P(nc, eng, sems))
                    getattr(blk, eng)(body)
                    f = {k: c.v for k, c in items}
                    assert final is None or f == final, f"lockstep div {eng}"
                    final = f
            state = final
            assert rstate is None, "common phase after fork unsupported"
        else:
            if rstate is None:
                rstate = {0: dict(state), 1: dict(state)}
            finals = {0: None, 1: None}
            with nc.Block() as blk:
                for eng in ENGS:
                    def body(e, eng=eng, fn=fn):
                        for r in (0, 1):
                            for k, c in items:
                                c.v = rstate[r][k]
                            geng = getattr(nc, eng)
                            with geng.If_eq(regs[eng]["role"], r):
                                fn(P(nc, eng, sems), r)
                            geng.end_ifs()
                            f = {k: c.v for k, c in items}
                            assert finals[r] is None or f == finals[r], f"lockstep div {eng} role {r}: " + str(
                                {k: (finals[r][k], f[k]) for k in f if f[k] != finals[r][k]})
                            finals[r] = f
                    getattr(blk, eng)(body)
            rstate = finals


def build_fused():
    nc = _mk_nc()
    stack = contextlib.ExitStack()
    x0 = nc.dram_tensor("x0", [D, TOK], F32, kind="ExternalInput").ap()
    x0f = nc.dram_tensor("x0f", [D, TOK], F32, kind="ExternalInput").ap()
    w = ffn_weight_aps(nc, "ExternalInput")
    w["W_Q"] = nc.dram_tensor("W_Q", [4, 128, DC, 512], F32, kind="ExternalInput").ap()
    w["W_K"] = nc.dram_tensor("W_K", [4, 128, DC, 512], F32, kind="ExternalInput").ap()
    w["W_V"] = nc.dram_tensor("W_V", [4, 128, DC, 512], F32, kind="ExternalInput").ap()
    ident = nc.dram_tensor("ident", [128, 128], F32, kind="ExternalInput").ap()
    onehot = nc.dram_tensor("onehot", [8, NE * 128], BF16, kind="ExternalInput").ap()
    cq = nc.dram_tensor("cq", [2, D, TOK], BF16, kind="ExternalInput").ap()
    ckv = nc.dram_tensor("ckv", [2, D, 2 * TOK], BF16, kind="ExternalInput").ap()
    dtab = nc.dram_tensor("dtab", [S, TOK], BF16, kind="ExternalInput").ap()
    meta = nc.dram_tensor("meta", [1, 4], mybir.dt.int32, kind="ExternalInput").ap()
    xout = nc.dram_tensor("xout", [D, TOK], F32, kind="ExternalOutput").ap()
    qt_d = nc.dram_tensor("qt_d", [D, TOK], BF16, kind="Internal").ap()
    kt_sh = nc.dram_tensor("kt_sh", [D, S], BF16, kind="Internal", addr_space="Shared").ap()
    v_sh = nc.dram_tensor("v_sh", [S, D], BF16, kind="Internal", addr_space="Shared").ap()
    flag_sh = nc.dram_tensor("flag_sh", [2, 1, 4], mybir.dt.int32, kind="Internal", addr_space="Shared").ap()

    bf = alloc_core_bufs(nc, stack)
    bf["ident_dram"] = ident
    bf["onehot_dram"] = onehot
    bf["meta_sb"] = stack.enter_context(nc.sbuf_tensor("meta_sb", [1, 4], mybir.dt.int32))
    sems = alloc_sems(nc, stack)
    for n in ("kvw", "flg", "pol"):
        sems[n] = Ctr(stack.enter_context(nc.semaphore(n)))
    regs = {}
    for eng in ENGS:
        regs[eng] = {"role": getattr(nc, eng).alloc_register(name=f"role_{eng}")}
    regs["sync"]["half"] = nc.sync.alloc_register(name="half_sync")
    regs["sync"]["nonce"] = nc.sync.alloc_register(name="nonce_sync")
    regs["sync"]["pollcond"] = nc.sync.alloc_register(name="pollcond")
    regs["sync"]["pollfv"] = nc.sync.alloc_register(name="pollfv")
    bf["regs"] = regs
    ind = {"x0": x0, "x0f": x0f, "cq": cq, "ckv": ckv, "dtab": dtab, "qt_d": qt_d,
           "kt_sh": kt_sh, "v_sh": v_sh, "flag_sh": flag_sh, "xout": xout}

    def kv_store_factory(p, role, wv_half_holder):
        half_reg = regs["sync"]["half"]

        def kv_store(kind, grp, q, buf, tb):
            for h in (0, 1):
                gbase = (h * TOK) if (role == 0 or bf["cur_pass"] == "foreign") else (RA + h * TOK)
                if p.active == "sync":
                    e = nc.sync
                    with e.If_eq(half_reg, h):
                        if kind == "k":
                            dst = kt_sh.rearrange("(dc p) m -> p dc m", p=128)[:, q * 4:(q + 1) * 4,
                                                                              gbase + grp * G:gbase + grp * G + G]
                        else:
                            r0 = gbase + grp * G + tb * 128
                            dst = v_sh[r0:r0 + 128, q * 512:(q + 1) * 512]
                        e.dma_start(out=dst, in_=buf).then_inc(sems["kvw"].h, 16)
                    e.end_ifs()
            sems["kvw"].v += 16
            return sems["kvw"]
        return kv_store

    def p0(p):
        emit_const_init(p, bf, sems)
        p.dma("sync", bf["meta_sb"][:, :], meta[:, :], sems["xload"])
        for eng in ENGS:
            p.wait(eng, sems["xload"])
            if p.active == eng:
                ge = getattr(nc, eng)
                ge.reg_load(regs[eng]["role"], bf["meta_sb"][0:1, 0:1])
                if eng == "sync":
                    ge.reg_load(regs[eng]["half"], bf["meta_sb"][0:1, 1:2])
                    ge.reg_load(regs[eng]["nonce"], bf["meta_sb"][0:1, 2:3])

    def mk_ffn_phase(src_ap, tag):
        def f(p, role):
            if src_ap is not None:
                if role == 0 and tag == "foreign":
                    return
                ap = x0f if tag == "foreign" else x0
                p.wait("sync", sems["v"])
                p.wait("sync", sems["t"])
                p.dma("sync", bf["x_res"][:, :, :], ap.rearrange("(dc p) t -> p dc t", p=128), sems["xload"])
                for eng in ("vector", "tensor", "scalar", "gpsimd"):
                    p.wait(eng, sems["xload"])
            emit_ffn_preamble(p, bf, sems, w)
            bf["wgu_db"] = DBuf(p, list(zip(bf["wga"], bf["wua"])), sems["wga_l"], sems["t"])
            bf["wda_db"] = DBuf(p, bf["wda"], sems["wda_l"], sems["t"])
            bf["pp_slot"] = Slot(4)
            bf["qq_slot"] = Slot(2)
            bf["gu_pairs"] = [(bf["ps_a"], bf["ps_b"]), (bf["ps_c"], bf["ps_d"]),
                              (bf["ps_dn0"][0], bf["ps_dn0"][1]), (bf["ps_dn0"][2], bf["ps_dn0"][3])]
            bf["silu_guard"] = Slot(4)
            bf["gtmp_guard"] = Slot(4)
            bf["xsq_guard"] = Slot(1)
            for grp in range(NGRP):
                pre = (sems["v"], sems["v"].v) if grp == 0 else None
                emit_ffn_group(p, bf, sems, w, grp, pre_t=pre)
        return f

    def mk_qkv_phase(tag):
        def f(p, role):
            if role == 0 and tag == "foreign":
                return
            bf["cur_pass"] = tag
            kv_store = kv_store_factory(p, role, None)
            if tag == "foreign":
                emit_qkv2(p, bf, sems, w, None, (ckv[0][:, 0:TOK], ckv[1][:, 0:TOK]), NGRP,
                          do_q=False, qt_dst=None, kv_store=kv_store, tok_base=0)
            else:
                emit_qkv2(p, bf, sems, w, (cq[0], cq[1]), (ckv[0][:, TOK:2 * TOK], ckv[1][:, TOK:2 * TOK]),
                          NGRP, do_q=True, qt_dst=qt_d, kv_store=kv_store, tok_base=0)
        return f

    def p_flags(p, role):
        half_reg = regs["sync"]["half"]
        nonce_reg = regs["sync"]["nonce"]
        p.wait("sync", sems["kvw"])
        for h in (0, 1):
            if p.active == "sync":
                with nc.sync.If_eq(half_reg, h):
                    nc.sync.dma_start(out=flag_sh[h, :, 0:1], in_=bf["meta_sb"][0:1, 2:3]).then_inc(sems["flg"].h, 16)
                nc.sync.end_ifs()
        sems["flg"].v += 16
        p.wait("sync", sems["flg"])
        if p.active == "sync":
            e = nc.sync
            cond = regs["sync"]["pollcond"]
            fv = regs["sync"]["pollfv"]
            for ph in (0, 1):
                e.reg_mov(cond, 1)
                with e.While(cond):
                    e.sem_clear(sems["pol"].h)
                    e.dma_start(out=bf["meta_sb"][0:1, 3:4], in_=flag_sh[ph, :, 0:1]).then_inc(sems["pol"].h, 16)
                    e.wait_ge(sems["pol"].h, 16)
                    e.reg_load(fv, bf["meta_sb"][0:1, 3:4])
                    e.reg_alu(cond, fv, nonce_reg, ALU.is_equal)
                    e.reg_alu(cond, cond, 1, ALU.bitwise_xor)
                e.sem_clear(sems["pol"].h)
        if p.active == "sync":
            nc.sync.sem_inc(sems["g"].h, 1)
        sems["g"].v += 1
        for eng in ("tensor", "gpsimd", "vector"):
            p.wait(eng, sems["g"])

    def p_attn(p, role):
        p.dma("sync", bf["qt_sb"][:, :, :], qt_d.rearrange("(dc p) t -> p dc t", p=128), sems["qtload"])
        for e2 in ("tensor", "vector", "gpsimd"):
            p.wait(e2, sems["qtload"])
        mx = RA if role == 0 else 4096
        emit_attention(p, bf, sems, {"kt_full": kt_sh[:, 0:mx], "v_full": v_sh[0:mx, :], "dtab": dtab[0:mx, :]}, m_ext=mx)

    def p_store(p, role):
        p.wait("sync", sems["v"])
        p.dma("sync", xout.rearrange("(dc p) t -> p dc t", p=128), bf["x_res"][:, :, :], sems["o"])
        p.wait("sync", sems["o"])

    run_phases_roled(nc, sems, regs, [("common", p0)])

    def roled_group(alloc_fns, fns):
        st2 = contextlib.ExitStack()
        for af in alloc_fns:
            for k2, v2 in af(nc, st2).items():
                bf[k2] = v2
        run2 = [("roled", fn) for fn in fns]
        _run_roled_cont(nc, sems, regs, run2)
        st2.close()

    global _ROLE_STATE
    _ROLE_STATE = None

    def _run_roled_cont(nc2, sems2, regs2, phs):
        global _ROLE_STATE
        items = [(k, c) for k, c in sems2.items() if isinstance(c, Ctr)]
        if _ROLE_STATE is None:
            base = {k: c.v for k, c in items}
            _ROLE_STATE = {0: dict(base), 1: dict(base)}
        for kind, fn in phs:
            finals = {0: None, 1: None}
            with nc2.Block() as blk:
                for eng in ENGS:
                    def body(e, eng=eng, fn=fn):
                        for r in (0, 1):
                            for k, c in items:
                                c.v = _ROLE_STATE[r][k]
                            geng = getattr(nc2, eng)
                            with geng.If_eq(regs2[eng]["role"], r):
                                fn(P(nc2, eng, sems2), r)
                            geng.end_ifs()
                            f = {k: c.v for k, c in items}
                            assert finals[r] is None or f == finals[r], f"lockstep div {eng} role {r}: " + str(
                                {k: (finals[r][k], f[k]) for k in (finals[r] or f) if f.get(k) != (finals[r] or {}).get(k)})
                            finals[r] = f
                    getattr(blk, eng)(body)
            _ROLE_STATE = finals

    roled_group([_mk_ffn_alloc("_1")], [mk_ffn_phase(x0f, "foreign")])
    roled_group([_mk_qkv_alloc("_1")], [mk_qkv_phase("foreign")])
    roled_group([_mk_ffn_alloc("_2")], [mk_ffn_phase(x0, "own")])
    roled_group([_mk_qkv_alloc("_2")], [mk_qkv_phase("own")])
    roled_group([], [p_flags])
    roled_group([alloc_attn_bufs], [p_attn])
    roled_group([_mk_ffn_alloc("_3")], [mk_ffn_phase(None, "ffn2")])
    roled_group([], [p_store])

    stack.close()
    return nc


def _mk_ffn_alloc(tagname):
    def f(nc, stack):
        bf = {}
        def S_(name, shape, dt):
            return stack.enter_context(nc.sbuf_tensor(name + tagname, shape, dt))
        bf["x_ln"] = S_("x_ln", [128, DC, G], BF16)
        bf["xsq"] = S_("xsq", [128, DC, G], BF16)
        bf["grnd"] = S_("grnd", [128, 2, HIE, G], BF16)
        bf["gsh"] = bf["grnd"].rearrange("p a b t -> p (a b) t")
        bf["comb"] = S_("comb", [128, NE, G], BF16)
        bf["comb_f32"] = S_("comb_f32", [128, 4, 8], F32)
        bf["scores_tmp"] = S_("scores_tmp", [128, 4, 8], F32)
        bf["comb8"] = S_("comb8", [8, G], BF16)
        bf["silu_sb"] = S_("silu_sb", [128, 4, G], BF16)
        bf["gtmp"] = S_("gtmp", [128, 4, G], BF16)
        bf["wga"] = [S_(f"wga{i}", [128, 2, DC, 128], BF16) for i in range(2)]
        bf["wua"] = [S_(f"wua{i}", [128, 2, DC, 128], BF16) for i in range(2)]
        bf["wda"] = [S_(f"wda{i}", [128, HIE, G], BF16) for i in range(2)]
        bf["ident"] = S_("ident_sb", [128, 128], F32)
        bf["onehot"] = S_("onehot_sb", [8, NE * 128], BF16)
        bf["gw"] = S_("gw_sb", [128, DC, NE], BF16)
        return bf
    return f


def _mk_qkv_alloc(tagname):
    def f(nc, stack):
        bf = {}
        def S_(name, shape, dt):
            return stack.enter_context(nc.sbuf_tensor(name + tagname, shape, dt))
        bf["x_ln2"] = S_("x_ln2", [128, NGRP, DC, G], BF16)
        bf["xsq"] = S_("xsq_q", [128, DC, G], BF16)
        bf["wq4"] = [S_(f"wq4_{i}", [128, DC, 512], BF16) for i in range(2)]
        bf["ctab"] = S_("ctab", [128, 4, G], BF16)
        bf["stab"] = S_("stab", [128, 4, G], BF16)
        bf["qk_t"] = S_("qk_t", [128, 4, G], BF16)
        bf["qk_r"] = S_("qk_r", [128, 4, G], BF16)
        bf["qk_a"] = S_("qk_a", [128, 4, G], BF16)
        bf["qk_out"] = S_("qk_out", [128, 4, G], BF16)
        bf["v_out"] = bf["qk_out"].rearrange("p a t -> p (a t)")[:, 0:512]
        return bf
    return f


alloc_ffn_bufs2 = _mk_ffn_alloc("_2")
alloc_ffn_bufs3 = _mk_ffn_alloc("_3")
alloc_qkv_bufs2 = _mk_qkv_alloc("_2")


def fused_core_inputs(c, X, wmap, WQ, WK, WV, nonce):
    b, role, half = c // 4, (c % 4) // 2, c % 2
    own0 = (RA if role else 0) + half * TOK
    ident, onehot = _consts()
    own_rows = np.arange(own0, own0 + TOK)
    for_rows = np.arange(half * TOK, half * TOK + TOK)
    cq = np.stack(host_xpos_tables(own_rows, False))
    ck_own = host_xpos_tables(own_rows, True)
    if role:
        ck_for = host_xpos_tables(for_rows, True)
    else:
        z = np.zeros((D, TOK), dtype=ml_dtypes.bfloat16)
        ck_for = (z, z)
    ckv = np.stack([np.concatenate([ck_for[0], ck_own[0]], axis=1),
                    np.concatenate([ck_for[1], ck_own[1]], axis=1)])
    m = dict(wmap)
    m.update({
        "x0": fm(X[b, own0:own0 + TOK, :]),
        "x0f": fm(X[b, half * TOK:half * TOK + TOK, :]) if role else np.zeros((D, TOK), np.float32),
        "W_Q": WQ, "W_K": WK, "W_V": WV,
        "ident": ident, "onehot": onehot, "cq": cq, "ckv": ckv,
        "dtab": host_decay(own_rows, S).astype(ml_dtypes.bfloat16),
        "meta": np.array([[role, half, nonce, 0]], dtype=np.int32),
    })
    return m


def kernel(L0_attn_X, L0_attn_W_Q, L0_attn_W_K, L0_attn_W_V, L0_ffn_gate_w,
           L0_ffn_w_gate, L0_ffn_w_up, L0_ffn_w_down, L0_ffn_sw_gate,
           L0_ffn_sw_up, L0_ffn_sw_down):
    X = np.asarray(L0_attn_X)
    wmap = {
        "w_gate": np.ascontiguousarray(L0_ffn_w_gate),
        "w_up": np.ascontiguousarray(L0_ffn_w_up),
        "w_down": np.ascontiguousarray(L0_ffn_w_down),
        "sw_gate": np.ascontiguousarray(L0_ffn_sw_gate),
        "sw_up": np.ascontiguousarray(L0_ffn_sw_up),
        "sw_down": np.ascontiguousarray(L0_ffn_sw_down),
        "gate_w": np.ascontiguousarray(L0_ffn_gate_w),
    }
    wmap = stage_weights(wmap)
    WQ = stage_wqkv(L0_attn_W_Q)
    WK = stage_wqkv(L0_attn_W_K)
    WV = stage_wqkv(L0_attn_W_V)
    nonce = int(np.random.randint(1, 2**31 - 1))
    nc = build_fused()
    ins = [fused_core_inputs(c, X, wmap, WQ, WK, WV, nonce) for c in range(8)]
    r = run_bass_kernel_spmd(nc, ins, core_ids=list(range(8)))
    out = np.empty((B, S, D), dtype=np.float32)
    for c in range(8):
        b, role, half = c // 4, (c % 4) // 2, c % 2
        own0 = (RA if role else 0) + half * TOK
        out[b, own0:own0 + TOK, :] = r.results[c]["xout"].T
    return out


# version 10
# speedup vs baseline: 1.6742x; 1.6677x over previous
import numpy as np
import ml_dtypes
import concourse.bass as bass
import concourse.mybir as mybir
from concourse.bass_utils import run_bass_kernel_spmd

F32 = mybir.dt.float32
BF16 = mybir.dt.bfloat16
AF = mybir.ActivationFunctionType
ALU = mybir.AluOpType
AX = mybir.AxisListType.X

D = 2048
DC = 16
HE = 1408
HIE = 11
HS = 2816
HIS = 22
NE = 8
GAMMA = 0.96875
EPS = 1e-5
G = 512
TOK = 1024
NGRP = TOK // G
S = 4096
B = 2
ENGS = ("sync", "gpsimd", "tensor", "vector", "scalar")
ROUNDS = [(0, 1), (2, 3), (4, 5), (6, 7), ("sh",)]


def host_xpos_tables(rows, downscale):
    rows = np.asarray(rows, dtype=np.float64)
    half = D // 2
    j = np.arange(half, dtype=np.float64)
    scale_vec = (2.0 * j + 0.4 * D) / (1.4 * D)
    scale = scale_vec[None, :] ** (rows / 512.0)[:, None]
    if downscale:
        scale = 1.0 / scale
    inv_freq = 1.0 / 10000.0 ** (j / half)
    sinusoid = rows[:, None] * inv_freq[None, :]
    sin = np.sin(sinusoid) * scale
    cos = np.cos(sinusoid) * scale
    C = np.empty((D, len(rows)), dtype=np.float64)
    Sg = np.empty((D, len(rows)), dtype=np.float64)
    C[0::2, :] = cos.T
    C[1::2, :] = cos.T
    Sg[0::2, :] = -sin.T
    Sg[1::2, :] = sin.T
    return C.astype(ml_dtypes.bfloat16), Sg.astype(ml_dtypes.bfloat16)


def host_decay(row_ids, m_ext):
    n = np.asarray(row_ids, dtype=np.float64)[None, :]
    m = np.arange(m_ext, dtype=np.float64)[:, None]
    diff = n - m
    with np.errstate(over="ignore", under="ignore"):
        Dm = np.where(diff >= 0, np.exp(np.log(GAMMA) * np.maximum(diff, 0.0)), 0.0)
    return Dm.astype(np.float32)


def fm(a):
    return np.ascontiguousarray(np.asarray(a).T)



class Ctr:
    __slots__ = ("h", "v")

    def __init__(self, handle):
        self.h = handle
        self.v = 0


class P:
    def __init__(self, nc, active, sems):
        self.nc = nc
        self.active = active
        self.sems = sems
        self.pend = {e: [] for e in ENGS}

    def defer_inc(self, eng, sem, n=1):
        self.pend[eng].append((sem, n))

    def flush_pend(self, eng, ins, limit=1):
        k = 0
        while self.pend[eng] and k < limit:
            sem, n = self.pend[eng].pop(0)
            sem.v += n
            if self.active == eng and ins is not None:
                ins.then_inc(sem.h, n)
            k += 1

    def do(self, eng, fn, *args, **kw):
        if self.active == eng:
            return getattr(getattr(self.nc, eng), fn)(*args, **kw)
        return None

    def inc(self, eng, ins, sem, n=1):
        sem.v += n
        if self.active == eng:
            assert ins is not None, f"inc on missing ins ({eng})"
            ins.then_inc(sem.h, n)
        return sem.v

    def wait(self, eng, sem, val=None):
        if val is None:
            val = sem.v
        assert val <= sem.v, f"wait {val} > emitted {sem.v}"
        if self.active == eng and val > 0:
            getattr(self.nc, eng).wait_ge(sem.h, val)

    def dma(self, eng, out, in_, sem, **kw):
        ins = self.do(eng, "dma_start", out=out, in_=in_, **kw)
        return self.inc(eng, ins, sem, 16)


class DBuf:

    def __init__(self, p, bufs, loaded, prog, feng="gpsimd"):
        self.p = p
        self.bufs = bufs
        self.n = len(bufs)
        self.loaded = loaded
        self.prog = prog
        self.feng = feng
        self.fills = 0
        self.loaded_at = {}
        self.done_at = []

    def fill(self, srcs):
        if self.fills >= self.n:
            assert len(self.done_at) > self.fills - self.n, "DBuf.done() missing"
            self.p.wait(self.feng, self.prog, self.done_at[self.fills - self.n])
        slot = self.bufs[self.fills % self.n]
        for dst_fn, src in srcs:
            self.p.dma(self.feng, dst_fn(slot), src, self.loaded)
        self.loaded_at[self.fills] = self.loaded.v
        self.fills += 1
        return self.fills - 1

    def use(self, i, eng="tensor"):
        self.p.wait(eng, self.loaded, self.loaded_at[i])
        return self.bufs[i % self.n]

    def done(self):
        self.done_at.append(self.prog.v)


class Slot:

    def __init__(self, nslots):
        self.readers = [None] * nslots

    def producer_wait(self, p, eng, sem, slot):
        r = self.readers[slot]
        if r is not None:
            p.wait(eng, r[0], r[1])

    def consumed(self, sem, slot):
        self.readers[slot] = (sem, sem.v)



def emit_ln(p, bf, sm, grp, pre_t=None, dst=None):
    sv, st, sa = sm["v"], sm["t"], sm["a"]
    g0 = grp * G
    x_ln = bf["x_ln"] if dst is None else dst
    rows = bf["rows"]
    mu, ex2, var, lnv, rstd = (rows[:, i * G:(i + 1) * G] for i in range(5))
    ins = None
    for dc in range(DC):
        ins = p.do("vector", "tensor_copy", x_ln[:, dc, :], bf["x_res"][:, dc, g0:g0 + G])
    cast_done = p.inc("vector", ins, sv)
    ins = None
    bf["xsq_guard"].producer_wait(p, "scalar", st, 0)
    for dc in range(DC):
        ins = p.do("scalar", "activation", bf["xsq"][:, dc, :], bf["x_res"][:, dc, g0:g0 + G], AF.Square)
    sq_done = p.inc("scalar", ins, sa)

    p.wait("tensor", sv, cast_done)
    p.wait("tensor", sa, sq_done)
    if pre_t is not None:
        p.wait("tensor", pre_t[0], pre_t[1])
    for dc in range(DC):
        ins = p.do("tensor", "matmul", bf["ps_b"][0:1, :], bf["ones_bf"][:, :], x_ln[:, dc, :], start=(dc == 0), stop=(dc == DC - 1))
        if dc == 0:
            p.flush_pend("tensor", ins, limit=1)
    ins = None
    for dc in range(DC):
        ins = p.do("tensor", "matmul", bf["ps_d"][0:1, :], bf["ones_bf"][:, :], bf["xsq"][:, dc, :], start=(dc == 0), stop=(dc == DC - 1))
    stats_done = p.inc("tensor", ins, st)
    bf["xsq_guard"].consumed(st, 0)

    p.wait("vector", st, stats_done)
    p.do("vector", "tensor_scalar", mu, bf["ps_b"][0:1, :], 1.0 / D, None, ALU.mult)
    p.do("vector", "tensor_scalar", ex2, bf["ps_d"][0:1, :], 1.0 / D, None, ALU.mult)
    p.do("vector", "drain")
    p.do("vector", "tensor_tensor", var, mu, mu, ALU.mult)
    p.do("vector", "drain")
    ins = p.do("vector", "tensor_tensor", var, ex2, var, ALU.subtract)
    var_done = p.inc("vector", ins, sv)
    p.wait("scalar", sv, var_done)
    p.do("scalar", "activation", lnv, var, AF.Ln, bias=bf["rows"][0:1, 5 * G:5 * G + 1])
    p.do("scalar", "drain")
    ins = p.do("scalar", "activation", rstd, lnv, AF.Exp, scale=-0.5)
    rstd_done = p.inc("scalar", ins, sa)

    p.wait("tensor", sa, rstd_done)
    p.do("tensor", "matmul", bf["ps_a"][:, :], bf["ones_f32"][:, :], mu, start=True, stop=True)
    ins = p.do("tensor", "matmul", bf["ps_c"][:, :], bf["ones_f32"][:, :], rstd, start=True, stop=True)
    bcast_done = p.inc("tensor", ins, st)

    p.wait("vector", st, bcast_done)
    ins = None
    for dc in range(DC):
        p.do("vector", "tensor_tensor", bf["ln_tmp"][:, dc % 2, :], bf["x_res"][:, dc, g0:g0 + G], bf["ps_a"][:, :], ALU.subtract)
        if dc >= 1:
            ins = p.do("vector", "tensor_tensor", x_ln[:, dc - 1, :], bf["ln_tmp"][:, (dc - 1) % 2, :], bf["ps_c"][:, :], ALU.mult)
    p.do("vector", "drain")
    ins = p.do("vector", "tensor_tensor", x_ln[:, DC - 1, :], bf["ln_tmp"][:, (DC - 1) % 2, :], bf["ps_c"][:, :], ALU.mult)
    return p.inc("vector", ins, sv)


def emit_gating(p, bf, sm, xln_done):
    sv, st, sa = sm["v"], sm["t"], sm["a"]
    p.wait("tensor", sv, xln_done)
    ins = None
    for tb in range(4):
        for dc in range(DC):
            ins = p.do("tensor", "matmul", bf["ps_b"][:, 32 + tb * 8:40 + tb * 8],
                       bf["x_ln"][:, dc, tb * 128:(tb + 1) * 128],
                       bf["gw"][:, dc, :], start=(dc == 0), stop=(dc == DC - 1))
    logits_done = p.inc("tensor", ins, st)

    comb_sb = bf["comb_f32"]
    stmp = bf["scores_tmp"]
    srows = bf["srows"]
    mx = lambda tb: srows[:, tb, 0:1]
    smr = lambda tb: srows[:, tb, 1:2]
    rcp = lambda tb: srows[:, tb, 2:3]
    m1 = lambda tb: srows[:, tb, 3:4]
    m2 = lambda tb: srows[:, tb, 4:5]
    lg = lambda tb: bf["ps_b"][:, 32 + tb * 8:40 + tb * 8]

    p.wait("vector", st, logits_done)
    for tb in range(4):
        p.do("vector", "tensor_reduce", mx(tb), lg(tb), AX, ALU.max, negate=True)
    ins = None
    for tb in range(4):
        ins = p.do("vector", "tensor_copy", stmp[:, tb, :], lg(tb))
    pre_exp = p.inc("vector", ins, sv)
    p.wait("scalar", sv, pre_exp)
    ins = None
    for tb in range(4):
        ins = p.do("scalar", "activation", comb_sb[:, tb, :], stmp[:, tb, :], AF.Exp, bias=mx(tb))
    exp_done = p.inc("scalar", ins, sa)
    p.wait("vector", sa, exp_done)

    def step(fn):
        for tb in range(4):
            fn(tb)
        p.do("vector", "drain")

    step(lambda tb: p.do("vector", "tensor_reduce", smr(tb), comb_sb[:, tb, :], AX, ALU.add))
    step(lambda tb: p.do("vector", "reciprocal", rcp(tb), smr(tb)))
    step(lambda tb: p.do("vector", "tensor_scalar", comb_sb[:, tb, :], comb_sb[:, tb, :], rcp(tb), None, ALU.mult))
    step(lambda tb: p.do("vector", "tensor_reduce", m1(tb), comb_sb[:, tb, :], AX, ALU.max))
    step(lambda tb: p.do("vector", "tensor_scalar", stmp[:, tb, :], comb_sb[:, tb, :], m1(tb), None, ALU.is_ge))
    step(lambda tb: p.do("vector", "scalar_tensor_tensor", stmp[:, tb, :], stmp[:, tb, :], -2.0, comb_sb[:, tb, :], ALU.mult, ALU.add))
    step(lambda tb: p.do("vector", "tensor_reduce", m2(tb), stmp[:, tb, :], AX, ALU.max))
    step(lambda tb: p.do("vector", "tensor_scalar", stmp[:, tb, :], comb_sb[:, tb, :], m2(tb), None, ALU.is_ge))
    ins = None
    for tb in range(4):
        ins = p.do("vector", "tensor_tensor", comb_sb[:, tb, :], comb_sb[:, tb, :], stmp[:, tb, :], ALU.mult)
    comb_done = p.inc("vector", ins, sv)

    p.wait("tensor", sv, comb_done)
    ins = None
    for tb in range(4):
        ins = p.do("tensor", "transpose", bf["ps_d"][0:8, tb * 128:(tb + 1) * 128], comb_sb[:, tb, :], bf["ident"][:, :])
    tr_done = p.inc("tensor", ins, st)
    p.wait("vector", st, tr_done)
    ins = p.do("vector", "tensor_copy", bf["comb8"][:, :], bf["ps_d"][0:8, 0:G])
    c8_done = p.inc("vector", ins, sv)

    banks = [bf["ps_a"], bf["ps_b"], bf["ps_c"], bf["ps_d"]] + bf["ps_dn0"]
    p.wait("tensor", sv, c8_done)
    ins = None
    for e in range(NE):
        ins = p.do("tensor", "matmul", banks[e][:, :], bf["onehot"][:, e * 128:(e + 1) * 128], bf["comb8"][:, :], start=True, stop=True)
    be = p.inc("tensor", ins, st)
    p.wait("vector", st, be)
    ins = None
    for e in range(NE):
        ins = p.do("vector", "tensor_copy", bf["comb"][:, e, :], banks[e][:, :])
    p.inc("vector", ins, sv)
    return sv.v


def emit_ffn_group(p, bf, sm, w, grp, pre_t=None):
    sv, st, sa, sg = sm["v"], sm["t"], sm["a"], sm["g"]
    g0 = grp * G
    xln_done = emit_ln(p, bf, sm, grp, pre_t=pre_t)
    comb_done = emit_gating(p, bf, sm, xln_done)

    wgu, wda = bf["wgu_db"], bf["wda_db"]
    pp = bf["pp_slot"]
    qq = bf["qq_slot"]
    silu_guard = bf["silu_guard"]
    gtmp_guard = bf["gtmp_guard"]

    for ri, rnd in enumerate(ROUNDS):
        shared = rnd == ("sh",)
        n_hi = HIS if shared else HIE
        gdst = bf["gsh"] if shared else bf["grnd"]
        hb_sizes = []
        left = n_hi
        while left > 0:
            hb_sizes.append(min(2, left))
            left -= 2
        elist = [None] if shared else list(rnd)
        for ei, e in enumerate(elist):
            for hbi, hbsz in enumerate(hb_sizes):
                h0 = hbi * 2 * 128
                if shared:
                    gsrc = w["sw_gate"][hbi * 2:hbi * 2 + hbsz]
                    usrc = w["sw_up"][hbi * 2:hbi * 2 + hbsz]
                else:
                    gsrc = w["w_gate"][e, hbi * 2:hbi * 2 + hbsz]
                    usrc = w["w_up"][e, hbi * 2:hbi * 2 + hbsz]
                fgu = wgu.fill([
                    (lambda sl, n=hbsz: sl[0][:, 0:n, :, :], gsrc.rearrange("hi p dc h -> p hi dc h")),
                    (lambda sl, n=hbsz: sl[1][:, 0:n, :, :], usrc.rearrange("hi p dc h -> p hi dc h")),
                ])
                for hj in range(hbsz):
                    hi = hbi * 2 + hj
                    par = hi % 4
                    pg, pu = bf["gu_pairs"][par]
                    if ri == 0 and ei == 0 and hi <= 3:
                        p.wait("tensor", sv, comb_done)
                    else:
                        pp.producer_wait(p, "tensor", sv, par)
                        qq.producer_wait(p, "tensor", sv, 0)
                        qq.producer_wait(p, "tensor", sv, 1)
                    gsl, usl = wgu.use(fgu)
                    ins = None
                    for dc in range(DC):
                        ins = p.do("tensor", "matmul", pg[:, :], gsl[:, hj, dc, :], bf["x_ln"][:, dc, :], start=(dc == 0), stop=(dc == DC - 1))
                    for dc in range(DC):
                        ins = p.do("tensor", "matmul", pu[:, :], usl[:, hj, dc, :], bf["x_ln"][:, dc, :], start=(dc == 0), stop=(dc == DC - 1))
                    gu_done = p.inc("tensor", ins, st)
                    if hj == hbsz - 1:
                        wgu.done()

                    p.wait("scalar", st, gu_done)
                    silu_guard.producer_wait(p, "scalar", sv, par)
                    ins = p.do("scalar", "activation", bf["silu_sb"][:, par, :], pg[:, :], AF.Silu)
                    silu_done = p.inc("scalar", ins, sa)
                    p.wait("vector", sa, silu_done)
                    if shared:
                        ins = p.do("vector", "tensor_tensor", gdst[:, hi, :], pu[:, :], bf["silu_sb"][:, par, :], ALU.mult)
                        mul_done = p.inc("vector", ins, sv)
                        silu_guard.consumed(sv, par)
                        pp.consumed(sv, par)
                    else:
                        p.do("vector", "tensor_tensor", bf["gtmp"][:, par, :], pu[:, :], bf["silu_sb"][:, par, :], ALU.mult)
                        p.do("vector", "drain")
                        ins = p.do("vector", "tensor_tensor", gdst[:, ei, hi, :], bf["gtmp"][:, par, :], bf["comb"][:, e, :], ALU.mult)
                        mul_done = p.inc("vector", ins, sv)
                        silu_guard.consumed(sv, par)
                        pp.consumed(sv, par)
        g_ready_sem = sv
        g_ready = g_ready_sem.v

        for dq in range(4):
            d0 = dq * 512
            qset = 0 if dq % 2 == 0 else 1
            pq = bf["ps_dn0"] if qset == 0 else bf["ps_dn1"]
            subs = [0, 1] if shared else list(rnd)
            ins = None
            first_mm = True
            for si, sub in enumerate(subs):
                if shared:
                    srcw = w["sw_down"][sub, dq]
                else:
                    srcw = w["w_down"][sub, dq]
                fd = wda.fill([(lambda sl: sl[:, :, :], srcw)])
                if si == 0:
                    p.wait("tensor", g_ready_sem, g_ready)
                    qq.producer_wait(p, "tensor", sv, qset)
                dsl = wda.use(fd)
                last_si = si == len(subs) - 1
                for hi in range(HIE):
                    gsrc_t = bf["gsh"][:, (sub * HIE + hi) if shared else 0, :] if shared else bf["grnd"][:, si, hi, :]
                    for dt in range(4):
                        ins = p.do("tensor", "matmul", pq[dt][:, :], dsl[:, hi, dt * 128:(dt + 1) * 128], gsrc_t,
                                   start=first_mm, stop=(last_si and hi == HIE - 1))
                    first_mm = False
                p.inc("tensor", ins, st)
                wda.done()
            dq_done = st.v

            p.wait("vector", st, dq_done)
            ins = None
            for dt in range(4):
                dcg = dq * 4 + dt
                ins = p.do("vector", "scalar_tensor_tensor", bf["x_res"][:, dcg, g0:g0 + G],
                           pq[dt][:, :], 0.5, bf["x_res"][:, dcg, g0:g0 + G], ALU.mult, ALU.add)
            ev = p.inc("vector", ins, sv)
            qq.consumed(sv, qset)
            if qset == 0:
                pp.consumed(sv, 2)
                pp.consumed(sv, 3)
            else:
                pp.consumed(sv, 0)
                pp.consumed(sv, 1)
    return sv.v



def emit_qkv(p, bf, sm, w, outs):
    sv, st, sa, sg, so = sm["v"], sm["t"], sm["a"], sm["g"], sm["o"]
    wload, tl = sm["wload"], sm["tabload"]

    bf["xsq_guard"] = Slot(1)
    for grp in range(NGRP):
        emit_ln(p, bf, sm, grp, dst=bf["x_ln2"][:, grp])

    qk_g = Slot(2)
    ob_g = Slot(1)
    qpp = Slot(2)
    vps = Slot(1)
    shuffle_mask = [i ^ 1 for i in range(32)]

    for wname in ("W_Q", "W_K", "W_V"):
        for half in range(2):
            p.wait("gpsimd", st)
            p.dma("gpsimd", bf["wfull"][:, :, :],
                  w[wname][:, half * 1024:(half + 1) * 1024].rearrange("(dc p) o -> p dc o", p=128), wload)
            for grp in range(NGRP):
                if wname in ("W_Q", "W_K"):
                    tabs = bf["cs_q"] if wname == "W_Q" else bf["cs_k"]
                    p.wait("sync", sg)
                    p.wait("sync", sv)
                    p.dma("sync", bf["ctab"][:, :, :], tabs[0].rearrange("(dc p) t -> p dc t", p=128)[:, :, grp * G:grp * G + G], tl)
                    p.dma("sync", bf["stab"][:, :, :], tabs[1].rearrange("(dc p) t -> p dc t", p=128)[:, :, grp * G:grp * G + G], tl)
                    p.wait("tensor", wload)
                    for dto in range(8):
                        dt = half * 8 + dto
                        par = dt % 2
                        pq = bf["ps_a"] if par == 0 else bf["ps_c"]
                        if dto <= 1:
                            p.wait("tensor", sv)
                        qpp.producer_wait(p, "tensor", sv, par)
                        ins = None
                        for dc in range(DC):
                            ins = p.do("tensor", "matmul", pq[:, :], bf["wfull"][:, dc, dto * 128:(dto + 1) * 128],
                                       bf["x_ln2"][:, grp, dc, :], start=(dc == 0), stop=(dc == DC - 1))
                            if dc == 0:
                                p.flush_pend("tensor", ins, limit=1)
                        mm_done = p.inc("tensor", ins, st)
                        p.wait("vector", st, mm_done)
                        if dto == 0:
                            p.wait("vector", tl)
                            ob_g.producer_wait(p, "vector", so, 0)
                        qk_g.producer_wait(p, "vector", sv, par)
                        p.do("vector", "tensor_copy", bf["qk_t"][:, par, :], pq[:, :])
                        p.do("vector", "drain")
                        ins = p.do("vector", "stream_shuffle", bf["qk_r"][:, par, :], bf["qk_t"][:, par, :], shuffle_mask)
                        sh_done = p.inc("vector", ins, sv)
                        qpp.consumed(sv, par)
                        p.wait("gpsimd", sv, sh_done)
                        ins = p.do("gpsimd", "tensor_tensor", bf["qk_a"][:, par, :], bf["qk_t"][:, par, :], bf["ctab"][:, dt, :], ALU.mult)
                        a_done = p.inc("gpsimd", ins, sg)
                        p.wait("vector", sg, a_done)
                        p.do("vector", "tensor_tensor", bf["qk_r"][:, par, :], bf["qk_r"][:, par, :], bf["stab"][:, dt, :], ALU.mult)
                        p.do("vector", "drain")
                        ins = p.do("vector", "tensor_tensor", bf["qk_out"][:, dto, :], bf["qk_r"][:, par, :], bf["qk_a"][:, par, :], ALU.add)
                        p.inc("vector", ins, sv)
                        qk_g.consumed(sv, par)
                    p.wait("sync", sv)
                    dst = outs["qt"] if wname == "W_Q" else outs["kt"]
                    p.dma("sync", dst.rearrange("(dc p) t -> p dc t", p=128)[:, half * 8:(half + 1) * 8, grp * G:grp * G + G],
                          bf["qk_out"][:, 0:8, :], so)
                    ob_g.consumed(so, 0)
                else:
                    p.wait("tensor", wload)
                    for tb in range(4):
                        if tb == 0:
                            p.wait("tensor", sv)
                        vps.producer_wait(p, "tensor", sv, 0)
                        ins = None
                        for dc in range(DC):
                            for ds in range(2):
                                ins = p.do("tensor", "matmul", bf["ps_dn0"][ds][:, :],
                                           bf["x_ln2"][:, grp, dc, tb * 128:(tb + 1) * 128],
                                           bf["wfull"][:, dc, ds * 512:(ds + 1) * 512],
                                           start=(dc == 0), stop=(dc == DC - 1))
                        v_done = p.inc("tensor", ins, st)
                        p.wait("vector", st, v_done)
                        ob_g.producer_wait(p, "vector", so, 0)
                        ins = None
                        for ds in range(2):
                            ins = p.do("vector", "tensor_copy", bf["v_out"][:, ds * 512:(ds + 1) * 512], bf["ps_dn0"][ds][:, :])
                        ve = p.inc("vector", ins, sv)
                        vps.consumed(sv, 0)
                        p.wait("sync", sv, ve)
                        p.dma("sync", outs["v"][grp * G + tb * 128:grp * G + (tb + 1) * 128, half * 1024:(half + 1) * 1024],
                              bf["v_out"][:, 0:1024], so)
                        ob_g.consumed(so, 0)
    p.wait("sync", so)



def emit_attention(p, bf, sm, ind, m_ext=None):
    sv, st = sm["v"], sm["t"]
    kload, vload, dload = sm["kload"], sm["vload"], sm["dload"]
    MT = (m_ext if m_ext is not None else S) // 128
    KB = 4
    pband = bf["pband"]
    kdb = DBuf(p, bf["kbufs"], kload, sm["t"], feng="gpsimd")
    ddb = DBuf(p, bf["dbufs"], dload, sm["v"], feng="sync")
    vdb = DBuf(p, bf["vbufs"], vload, sm["t"], feng="gpsimd")
    app = Slot(2)
    aq = Slot(1)

    for grp in range(NGRP):
        n0 = grp * G
        for mtb in range(MT // KB):
            fk = kdb.fill([(lambda sl: sl[:, :, :],
                            ind["kt_full"].rearrange("(dc p) m -> p dc m", p=128)[:, :, mtb * KB * 128:(mtb + 1) * KB * 128])])
            fdt = ddb.fill([(lambda sl: sl[:, :, :],
                             ind["dtab"][mtb * KB * 128:(mtb + 1) * KB * 128, n0:n0 + G].rearrange("(mc p) n -> p mc n", p=128))])
            for mi in range(KB):
                mt = mtb * KB + mi
                par = mt % 2
                pq = pband[par]
                app.producer_wait(p, "tensor", sv, par)
                if grp == 0 and mt <= 1:
                    p.wait("tensor", sv)
                if mt <= 1:
                    aq.producer_wait(p, "tensor", sv, 0)
                ksl = kdb.use(fk)
                ins = None
                for dc in range(DC):
                    ins = p.do("tensor", "matmul", pq[:, :], ksl[:, dc, mi * 128:(mi + 1) * 128],
                               bf["qt_sb"][:, dc, n0:n0 + G], start=(dc == 0), stop=(dc == DC - 1))
                qk_done = p.inc("tensor", ins, st)
                if mi == KB - 1:
                    kdb.done()
                p.wait("vector", st, qk_done)
                if mi == 0:
                    p.wait("vector", dload, ddb.loaded_at[fdt])
                dsl = bf["dbufs"][fdt % 2]
                ins = p.do("vector", "tensor_tensor", bf["att_sb"][:, mt, :], pq[:, :], dsl[:, mi, :], ALU.mult)
                p.inc("vector", ins, sv)
                app.consumed(sv, par)
                if mi == KB - 1:
                    ddb.done()
        att_all = sv.v
        for dh in range(2):
            ins = None
            for mtb in range(MT // KB):
                fv = vdb.fill([(lambda sl: sl[:, :, :],
                                ind["v_full"][mtb * KB * 128:(mtb + 1) * KB * 128, dh * 1024:(dh + 1) * 1024].rearrange("(mc p) d -> p mc d", p=128))])
                if mtb == 0:
                    p.wait("tensor", sv, att_all)
                    aq.producer_wait(p, "tensor", sv, 0)
                    app.producer_wait(p, "tensor", sv, 0)
                    app.producer_wait(p, "tensor", sv, 1)
                vsl = vdb.use(fv)
                for mi in range(KB):
                    mt = mtb * KB + mi
                    for dt in range(8):
                        ins = p.do("tensor", "matmul", pband[dt][:, :], vsl[:, mi, dt * 128:(dt + 1) * 128],
                                   bf["att_sb"][:, mt, :], start=(mt == 0), stop=(mt == MT - 1))
                p.inc("tensor", ins, st)
                vdb.done()
            oh_done = st.v
            p.wait("vector", st, oh_done)
            ins = None
            for dt in range(8):
                dcg = dh * 8 + dt
                ins = p.do("vector", "tensor_tensor", bf["x_res"][:, dcg, n0:n0 + G], pband[dt][:, :],
                           bf["x_res"][:, dcg, n0:n0 + G], ALU.add)
            ev = p.inc("vector", ins, sv)
            aq.consumed(sv, 0)
            app.consumed(sv, 0)
            app.consumed(sv, 1)



import contextlib


def run_phases(nc, sems, phases):
    items = [(k, c) for k, c in sems.items() if isinstance(c, Ctr)]
    for emitfn in phases:
        snap = {k: c.v for k, c in items}
        final = None
        with nc.Block() as blk:
            for eng in ENGS:
                for k, c in items:
                    c.v = snap[k]
                def body(e, eng=eng, fn=emitfn):
                    fn(P(nc, eng, sems))
                getattr(blk, eng)(body)
                f = {k: c.v for k, c in items}
                if final is None:
                    final = f
                else:
                    assert f == final, f"lockstep divergence in pass {eng}: " + str(
                        {k: (final[k], f[k]) for k in f if f[k] != final[k]})


def alloc_sems(nc, stack):
    names = ["v", "t", "a", "g", "o", "wload", "tabload", "kload", "vload", "dload",
             "kfree", "vfree", "dfree", "xload", "gload", "qtload", "att_q", "vq_free",
             "wga_l", "wga_f", "wua_l", "wua_f", "wda_l", "wda_f"]
    sems = {n: Ctr(stack.enter_context(nc.semaphore(n))) for n in names}
    for pair in ["pp_free", "qq_free", "att_pp"]:
        for i in range(2):
            sems[f"{pair}{i}"] = Ctr(stack.enter_context(nc.semaphore(f"{pair}{i}")))
    sems["pp_free"] = [sems["pp_free0"], sems["pp_free1"]]
    sems["qq_free"] = [sems["qq_free0"], sems["qq_free1"]]
    sems["att_pp"] = [sems["att_pp0"], sems["att_pp1"]]
    return sems


def alloc_core_bufs(nc, stack):
    bf = {}
    def S_(name, shape, dt):
        return stack.enter_context(nc.sbuf_tensor(name, shape, dt))
    def PS(name, shape):
        return stack.enter_context(nc.psum_tensor(name, shape, F32))
    bf["x_res"] = S_("x_res", [128, DC, TOK], F32)
    bf["ln_tmp"] = S_("ln_tmp", [128, 2, G], F32)
    bf["rows"] = S_("rows", [1, 5 * G + 8], F32)
    bf["srows"] = S_("srows", [128, 4, 8], F32)
    bf["ones_bf"] = S_("ones_bf", [128, 1], BF16)
    bf["ones_f32"] = S_("ones_f32", [1, 128], F32)
    bf["ps_a"] = PS("ps_a", [128, G])
    bf["ps_b"] = PS("ps_b", [128, G])
    bf["ps_c"] = PS("ps_c", [128, G])
    bf["ps_d"] = PS("ps_d", [128, G])
    bf["ps_dn0"] = [PS(f"ps_dn0_{i}", [128, G]) for i in range(4)]
    bf["ps_dn1"] = [bf["ps_a"], bf["ps_b"], bf["ps_c"], bf["ps_d"]]
    bf["pband"] = [bf["ps_a"], bf["ps_b"], bf["ps_c"], bf["ps_d"]] + bf["ps_dn0"]
    return bf


def alloc_ffn_bufs(nc, stack):
    bf = {}
    def S_(name, shape, dt):
        return stack.enter_context(nc.sbuf_tensor(name, shape, dt))
    bf["x_ln"] = S_("x_ln", [128, DC, G], BF16)
    bf["xsq"] = S_("xsq", [128, DC, G], BF16)
    bf["grnd"] = S_("grnd", [128, 2, HIE, G], BF16)
    bf["gsh"] = bf["grnd"].rearrange("p a b t -> p (a b) t")
    bf["comb"] = S_("comb", [128, NE, G], BF16)
    bf["comb_f32"] = S_("comb_f32", [128, 4, 8], F32)
    bf["scores_tmp"] = S_("scores_tmp", [128, 4, 8], F32)
    bf["comb8"] = S_("comb8", [8, G], BF16)
    bf["silu_sb"] = S_("silu_sb", [128, 4, G], BF16)
    bf["gtmp"] = S_("gtmp", [128, 4, G], BF16)
    bf["wga"] = [S_(f"wga{i}", [128, 2, DC, 128], BF16) for i in range(2)]
    bf["wua"] = [S_(f"wua{i}", [128, 2, DC, 128], BF16) for i in range(2)]
    bf["wda"] = [S_(f"wda{i}", [128, HIE, G], BF16) for i in range(2)]
    return bf


def alloc_qkv_bufs(nc, stack):
    bf = {}
    def S_(name, shape, dt):
        return stack.enter_context(nc.sbuf_tensor(name, shape, dt))
    bf["x_ln2"] = S_("x_ln2", [128, NGRP, DC, G], BF16)
    bf["xsq"] = S_("xsq_q", [128, DC, G], BF16)
    bf["wfull"] = S_("wfull", [128, DC, D // 2], BF16)
    bf["ctab"] = S_("ctab", [128, DC, G], BF16)
    bf["stab"] = S_("stab", [128, DC, G], BF16)
    bf["qk_t"] = S_("qk_t", [128, 2, G], BF16)
    bf["qk_r"] = S_("qk_r", [128, 2, G], BF16)
    bf["qk_a"] = S_("qk_a", [128, 2, G], BF16)
    bf["qk_out"] = S_("qk_out", [128, 8, G], BF16)
    bf["v_out"] = bf["qk_out"].rearrange("p a t -> p (a t)")[:, 0:D]
    return bf


def alloc_attn_bufs(nc, stack):
    bf = {}
    def S_(name, shape, dt):
        return stack.enter_context(nc.sbuf_tensor(name, shape, dt))
    bf["qt_sb"] = S_("qt_sb", [128, DC, TOK], BF16)
    bf["att_sb"] = S_("att_sb", [128, S // 128, G], BF16)
    bf["kbufs"] = [S_(f"kbuf{i}", [128, DC, 512], BF16) for i in range(2)]
    bf["vbufs"] = [S_(f"vbuf{i}", [128, 4, 1024], BF16) for i in range(2)]
    bf["dbufs"] = [S_(f"dbuf{i}", [128, 4, G], BF16) for i in range(2)]
    return bf


def emit_const_init(p, bf, sm):
    p.do("vector", "memset", bf["ones_bf"][:, :], 1.0)
    p.do("vector", "memset", bf["rows"][0:1, 5 * G:5 * G + 8], EPS)
    ins = p.do("vector", "memset", bf["ones_f32"][:, :], 1.0)
    return p.inc("vector", ins, sm["v"])


def ffn_weight_aps(nc, kind):
    w = {}
    w["w_gate"] = nc.dram_tensor("w_gate", [NE, HIE, 128, DC, 128], F32, kind=kind).ap()
    w["w_up"] = nc.dram_tensor("w_up", [NE, HIE, 128, DC, 128], F32, kind=kind).ap()
    w["w_down"] = nc.dram_tensor("w_down", [NE, 4, 128, HIE, 512], F32, kind=kind).ap()
    w["sw_gate"] = nc.dram_tensor("sw_gate", [HIS, 128, DC, 128], F32, kind=kind).ap()
    w["sw_up"] = nc.dram_tensor("sw_up", [HIS, 128, DC, 128], F32, kind=kind).ap()
    w["sw_down"] = nc.dram_tensor("sw_down", [2, 4, 128, HIE, 512], F32, kind=kind).ap()
    w["gate_w"] = nc.dram_tensor("gate_w", [D, NE], F32, kind=kind).ap()
    return w


def stage_weights(wmap):
    o = {}
    wg = np.ascontiguousarray(wmap["w_gate"]).reshape(NE, DC, 128, HIE, 128)
    o["w_gate"] = np.ascontiguousarray(wg.transpose(0, 3, 2, 1, 4))
    wu = np.ascontiguousarray(wmap["w_up"]).reshape(NE, DC, 128, HIE, 128)
    o["w_up"] = np.ascontiguousarray(wu.transpose(0, 3, 2, 1, 4))
    wd = np.ascontiguousarray(wmap["w_down"]).reshape(NE, HIE, 128, 4, 512)
    o["w_down"] = np.ascontiguousarray(wd.transpose(0, 3, 2, 1, 4))
    sg_ = np.ascontiguousarray(wmap["sw_gate"]).reshape(DC, 128, HIS, 128)
    o["sw_gate"] = np.ascontiguousarray(sg_.transpose(2, 1, 0, 3))
    su = np.ascontiguousarray(wmap["sw_up"]).reshape(DC, 128, HIS, 128)
    o["sw_up"] = np.ascontiguousarray(su.transpose(2, 1, 0, 3))
    sd = np.ascontiguousarray(wmap["sw_down"]).reshape(2, HIE, 128, 4, 512)
    o["sw_down"] = np.ascontiguousarray(sd.transpose(0, 3, 2, 1, 4))
    o["gate_w"] = wmap["gate_w"]
    return o


def stage_wqkv(W):
    wq = np.ascontiguousarray(W).reshape(DC, 128, 4, 512)
    return np.ascontiguousarray(wq.transpose(2, 1, 0, 3))


def emit_barrier(p, sems):
    for k, c in sems.items():
        if isinstance(c, Ctr):
            p.wait(p.active, c)


def emit_ffn_preamble(p, bf, sm, w):
    p.dma("sync", bf["ident"][:, :], bf["ident_dram"][:, :], sm["xload"])
    p.dma("sync", bf["onehot"][:, :], bf["onehot_dram"][:, :], sm["xload"])
    p.dma("gpsimd", bf["gw"][:, :, :], w["gate_w"].rearrange("(dc p) e -> p dc e", p=128), sm["gload"])
    for e in ENGS:
        p.wait(e, sm["xload"])
        p.wait(e, sm["gload"])


def alloc_gating_consts(nc, stack, bf):
    bf["ident"] = stack.enter_context(nc.sbuf_tensor("ident_sb", [128, 128], F32))
    bf["onehot"] = stack.enter_context(nc.sbuf_tensor("onehot_sb", [8, NE * 128], BF16))
    bf["gw"] = stack.enter_context(nc.sbuf_tensor("gw_sb", [128, DC, NE], BF16))


def _mk_nc():
    return bass.Bass("TRN2", target_bir_lowering=False, detect_race_conditions=False)


def build_l1(ffn_only=False):
    nc = _mk_nc()
    stack = contextlib.ExitStack()
    x0 = nc.dram_tensor("x0", [D, TOK], F32, kind="ExternalInput").ap()
    w = ffn_weight_aps(nc, "ExternalInput")
    w["W_Q"] = nc.dram_tensor("W_Q", [D, D], F32, kind="ExternalInput").ap()
    w["W_K"] = nc.dram_tensor("W_K", [D, D], F32, kind="ExternalInput").ap()
    w["W_V"] = nc.dram_tensor("W_V", [D, D], F32, kind="ExternalInput").ap()
    ident = nc.dram_tensor("ident", [128, 128], F32, kind="ExternalInput").ap()
    onehot = nc.dram_tensor("onehot", [8, NE * 128], BF16, kind="ExternalInput").ap()
    cs_q0 = nc.dram_tensor("cq", [2, D, TOK], BF16, kind="ExternalInput").ap()
    cs_k0 = nc.dram_tensor("ck", [2, D, TOK], BF16, kind="ExternalInput").ap()
    x1_out = nc.dram_tensor("x1", [D, TOK], F32, kind="ExternalOutput").ap()
    qt_out = nc.dram_tensor("qt", [D, TOK], BF16, kind="ExternalOutput").ap()
    kt_out = nc.dram_tensor("kt", [D, TOK], BF16, kind="ExternalOutput").ap()
    v_out = nc.dram_tensor("v", [TOK, D], BF16, kind="ExternalOutput").ap()

    bf = alloc_core_bufs(nc, stack)
    bf["ident_dram"] = ident
    bf["onehot_dram"] = onehot
    bf["cs_q"] = (cs_q0[0], cs_q0[1])
    bf["cs_k"] = (cs_k0[0], cs_k0[1])
    sems = alloc_sems(nc, stack)

    def phase_load(p):
        emit_const_init(p, bf, sems)
        p.dma("sync", bf["x_res"][:, :, :], x0.rearrange("(dc p) t -> p dc t", p=128), sems["xload"])
        emit_ffn_preamble(p, bf, sems, w)
        p.wait("tensor", sems["v"])

    def phase_ffn(p):
        bf["silu_guard"] = Slot(4)
        bf["gtmp_guard"] = Slot(4)
        bf["xsq_guard"] = Slot(1)
        bf["wgu_db"] = DBuf(p, list(zip(bf["wga"], bf["wua"])), sems["wga_l"], sems["t"])
        bf["wda_db"] = DBuf(p, bf["wda"], sems["wda_l"], sems["t"])
        bf["pp_slot"] = Slot(4)
        bf["qq_slot"] = Slot(2)
        bf["gu_pairs"] = [(bf["ps_a"], bf["ps_b"]), (bf["ps_c"], bf["ps_d"]),
                          (bf["ps_dn0"][0], bf["ps_dn0"][1]), (bf["ps_dn0"][2], bf["ps_dn0"][3])]
        for grp in range(NGRP):
            emit_ffn_group(p, bf, sems, w, grp)

    with contextlib.ExitStack() as fstack:
        alloc_gating_consts(nc, fstack, bf)
        for k, v_ in alloc_ffn_bufs(nc, fstack).items():
            bf[k] = v_
        run_phases(nc, sems, [phase_load, phase_ffn])

    if not ffn_only:
        def phase_qkv(p):
            emit_barrier(p, sems)
            emit_qkv(p, bf, sems, w, {"qt": qt_out, "kt": kt_out, "v": v_out})
        with contextlib.ExitStack() as qstack:
            for k, v_ in alloc_qkv_bufs(nc, qstack).items():
                bf[k] = v_
            run_phases(nc, sems, [phase_qkv])

    def phase_store(p):
        emit_barrier(p, sems)
        p.dma("sync", x1_out.rearrange("(dc p) t -> p dc t", p=128), bf["x_res"][:, :, :], sems["o"])
        p.wait("sync", sems["o"])

    run_phases(nc, sems, [phase_store])
    stack.close()
    return nc


def build_l2(attn_only=False):
    nc = _mk_nc()
    stack = contextlib.ExitStack()
    x1 = nc.dram_tensor("x1", [D, TOK], F32, kind="ExternalInput").ap()
    qt_in = nc.dram_tensor("qt", [D, TOK], BF16, kind="ExternalInput").ap()
    kt_full = nc.dram_tensor("kt_full", [D, S], BF16, kind="ExternalInput").ap()
    v_full = nc.dram_tensor("v_full", [S, D], BF16, kind="ExternalInput").ap()
    dtab = nc.dram_tensor("dtab", [S, TOK], BF16, kind="ExternalInput").ap()
    w = ffn_weight_aps(nc, "ExternalInput")
    ident = nc.dram_tensor("ident", [128, 128], F32, kind="ExternalInput").ap()
    onehot = nc.dram_tensor("onehot", [8, NE * 128], BF16, kind="ExternalInput").ap()
    xout = nc.dram_tensor("xout", [D, TOK], F32, kind="ExternalOutput").ap()

    bf = alloc_core_bufs(nc, stack)
    bf["ident_dram"] = ident
    bf["onehot_dram"] = onehot
    sems = alloc_sems(nc, stack)

    def phase_load(p):
        emit_const_init(p, bf, sems)
        p.dma("sync", bf["x_res"][:, :, :], x1.rearrange("(dc p) t -> p dc t", p=128), sems["xload"])
        p.dma("sync", bf["qt_sb"][:, :, :], qt_in.rearrange("(dc p) t -> p dc t", p=128), sems["xload"])
        for e in ENGS:
            p.wait(e, sems["xload"])
        p.wait("tensor", sems["v"])

    def phase_attn(p):
        emit_attention(p, bf, sems, {"kt_full": kt_full, "v_full": v_full, "dtab": dtab})

    with contextlib.ExitStack() as astack:
        for k, v_ in alloc_attn_bufs(nc, astack).items():
            bf[k] = v_
        run_phases(nc, sems, [phase_load, phase_attn])

    if not attn_only:
        def phase_ffn(p):
            emit_barrier(p, sems)
            emit_ffn_preamble(p, bf, sems, w)
            bf["silu_guard"] = Slot(4)
            bf["gtmp_guard"] = Slot(4)
            bf["xsq_guard"] = Slot(1)
            bf["wgu_db"] = DBuf(p, list(zip(bf["wga"], bf["wua"])), sems["wga_l"], sems["t"])
            bf["wda_db"] = DBuf(p, bf["wda"], sems["wda_l"], sems["t"])
            bf["pp_slot"] = Slot(4)
            bf["qq_slot"] = Slot(2)
            bf["gu_pairs"] = [(bf["ps_a"], bf["ps_b"]), (bf["ps_c"], bf["ps_d"]),
                              (bf["ps_dn0"][0], bf["ps_dn0"][1]), (bf["ps_dn0"][2], bf["ps_dn0"][3])]
            for grp in range(NGRP):
                pre = (sems["v"], sems["v"].v) if grp == 0 else None
                emit_ffn_group(p, bf, sems, w, grp, pre_t=pre)
        with contextlib.ExitStack() as fstack:
            alloc_gating_consts(nc, fstack, bf)
            for k, v_ in alloc_ffn_bufs(nc, fstack).items():
                bf[k] = v_
            run_phases(nc, sems, [phase_ffn])

    def phase_store(p):
        emit_barrier(p, sems)
        p.dma("sync", xout.rearrange("(dc p) t -> p dc t", p=128), bf["x_res"][:, :, :], sems["o"])
        p.wait("sync", sems["o"])

    run_phases(nc, sems, [phase_store])
    stack.close()
    return nc



def _consts():
    ident = np.eye(128, dtype=np.float32)
    onehot = np.zeros((8, NE * 128), dtype=ml_dtypes.bfloat16)
    for e in range(NE):
        onehot[e, e * 128:(e + 1) * 128] = 1.0
    return ident, onehot


def kernel_unfused(L0_attn_X, L0_attn_W_Q, L0_attn_W_K, L0_attn_W_V, L0_ffn_gate_w,
           L0_ffn_w_gate, L0_ffn_w_up, L0_ffn_w_down, L0_ffn_sw_gate,
           L0_ffn_sw_up, L0_ffn_sw_down):
    X = np.asarray(L0_attn_X)
    ident, onehot = _consts()
    wmap = {
        "w_gate": np.ascontiguousarray(L0_ffn_w_gate),
        "w_up": np.ascontiguousarray(L0_ffn_w_up),
        "w_down": np.ascontiguousarray(L0_ffn_w_down),
        "sw_gate": np.ascontiguousarray(L0_ffn_sw_gate),
        "sw_up": np.ascontiguousarray(L0_ffn_sw_up),
        "sw_down": np.ascontiguousarray(L0_ffn_sw_down),
        "gate_w": np.ascontiguousarray(L0_ffn_gate_w),
    }

    nc1 = build_l1()
    in1 = []
    for c in range(8):
        b, r0 = c // 4, (c % 4) * TOK
        rows = np.arange(r0, r0 + TOK)
        cq = np.stack(host_xpos_tables(rows, downscale=False))
        ck = np.stack(host_xpos_tables(rows, downscale=True))
        m = dict(wmap)
        m.update({
            "x0": fm(X[b, r0:r0 + TOK, :]),
            "W_Q": np.ascontiguousarray(L0_attn_W_Q),
            "W_K": np.ascontiguousarray(L0_attn_W_K),
            "W_V": np.ascontiguousarray(L0_attn_W_V),
            "ident": ident, "onehot": onehot, "cq": cq, "ck": ck,
        })
        in1.append(m)
    r1 = run_bass_kernel_spmd(nc1, in1, core_ids=list(range(8)))

    ktb = [np.concatenate([r1.results[b * 4 + i]["kt"] for i in range(4)], axis=1) for b in range(B)]
    vtb = [np.concatenate([r1.results[b * 4 + i]["v"] for i in range(4)], axis=0) for b in range(B)]

    nc2 = build_l2()
    in2 = []
    for c in range(8):
        b, r0 = c // 4, (c % 4) * TOK
        rows = np.arange(r0, r0 + TOK)
        m = dict(wmap)
        m.update({
            "x1": r1.results[c]["x1"],
            "qt": r1.results[c]["qt"],
            "kt_full": ktb[b],
            "v_full": vtb[b],
            "dtab": host_decay(rows, S).astype(ml_dtypes.bfloat16),
            "ident": ident, "onehot": onehot,
        })
        in2.append(m)
    r2 = run_bass_kernel_spmd(nc2, in2, core_ids=list(range(8)))

    out = np.empty((B, S, D), dtype=np.float32)
    for c in range(8):
        b, r0 = c // 4, (c % 4) * TOK
        out[b, r0:r0 + TOK, :] = r2.results[c]["xout"].T
    return out



RA = 2048


def emit_qkv2(p, bf, sm, w, tabs_c, tabs_k, n_groups, do_q, qt_dst, kv_store, tok_base):
    sv, st, sa, sg, so = sm["v"], sm["t"], sm["a"], sm["g"], sm["o"]
    wload, tl = sm["wload"], sm["tabload"]

    bf["xsq_guard"] = Slot(1)
    for grp in range(n_groups):
        emit_ln(p, bf, sm, grp, dst=bf["x_ln2"][:, grp])

    qk_g = Slot(4)
    ob_g = Slot(1)
    qpp = Slot(4)
    vps = Slot(2)
    tab_g = Slot(1)
    wdb = DBuf(p, bf["wq4"], wload, sm["t"], feng="gpsimd")
    shuffle_mask = [i ^ 1 for i in range(32)]
    wnames = (("W_Q", "W_K", "W_V") if do_q else ("W_K", "W_V"))
    banks4 = [bf["ps_a"], bf["ps_b"], bf["ps_c"], bf["ps_d"]]

    fills = {}
    order = [(wn, q) for wn in wnames for q in range(4)]
    fills[order[0]] = wdb.fill([(lambda sl: sl[:, :, :], w[order[0][0]][0])])
    for oi, (wname, q) in enumerate(order):
        if True:
            fw = fills[(wname, q)]
            if oi + 1 < len(order):
                wn2, q2 = order[oi + 1]
                fills[(wn2, q2)] = wdb.fill([(lambda sl, wn2=wn2, q2=q2: sl[:, :, :], w[wn2][q2])])
            for grp in range(n_groups):
                t0 = tok_base + grp * G
                if wname in ("W_Q", "W_K"):
                    tabs = (tabs_c if wname == "W_Q" else tabs_k)
                    tab_g.producer_wait(p, "sync", sv, 0)
                    p.wait("sync", sv)
                    p.dma("sync", bf["ctab"][:, :, :], tabs[0].rearrange("(dc p) t -> p dc t", p=128)[:, q * 4:(q + 1) * 4, t0:t0 + G], tl)
                    p.dma("sync", bf["stab"][:, :, :], tabs[1].rearrange("(dc p) t -> p dc t", p=128)[:, q * 4:(q + 1) * 4, t0:t0 + G], tl)
                    wsl = wdb.use(fw)
                    ins = None
                    for dto in range(4):
                        par = dto
                        pq = banks4[par]
                        if q == 0 and grp == 0 and dto <= 3:
                            p.wait("tensor", sv)
                        qpp.producer_wait(p, "tensor", sv, par)
                        for dc in range(DC):
                            ins = p.do("tensor", "matmul", pq[:, :], wsl[:, dc, dto * 128:(dto + 1) * 128],
                                       bf["x_ln2"][:, grp, dc, :], start=(dc == 0), stop=(dc == DC - 1))
                        mm_done = p.inc("tensor", ins, st)
                        p.wait("vector", st, mm_done)
                        if dto == 0:
                            p.wait("vector", tl)
                            ob_g.producer_wait(p, "vector", so, 0)
                        qk_g.producer_wait(p, "vector", sv, par)
                        p.do("vector", "tensor_copy", bf["qk_t"][:, par, :], pq[:, :])
                        p.do("vector", "drain")
                        ins = p.do("vector", "stream_shuffle", bf["qk_r"][:, par, :], bf["qk_t"][:, par, :], shuffle_mask)
                        p.inc("vector", ins, sv)
                        qpp.consumed(sv, par)
                        ins = p.do("vector", "tensor_tensor", bf["qk_a"][:, par, :], bf["qk_t"][:, par, :], bf["ctab"][:, dto, :], ALU.mult)
                        p.do("vector", "drain")
                        p.do("vector", "tensor_tensor", bf["qk_r"][:, par, :], bf["qk_r"][:, par, :], bf["stab"][:, dto, :], ALU.mult)
                        p.do("vector", "drain")
                        ins = p.do("vector", "tensor_tensor", bf["qk_out"][:, dto, :], bf["qk_r"][:, par, :], bf["qk_a"][:, par, :], ALU.add)
                        a_done = p.inc("vector", ins, sv)
                        if dto == 3:
                            tab_g.consumed(sv, 0)
                        qk_g.consumed(sv, par)
                    p.wait("sync", sv)
                    if wname == "W_Q":
                        p.dma("sync", qt_dst.rearrange("(dc p) t -> p dc t", p=128)[:, q * 4:(q + 1) * 4, grp * G:grp * G + G],
                              bf["qk_out"][:, 0:4, :], so)
                        ob_g.consumed(so, 0)
                    else:
                        ksem = kv_store("k", grp, q, bf["qk_out"][:, 0:4, :], None)
                        ob_g.consumed(ksem if ksem is not None else so, 0)
                else:
                    wsl = wdb.use(fw)
                    for tb in range(4):
                        par = tb % 2
                        if q == 0 and grp == 0 and tb <= 1:
                            p.wait("tensor", sv)
                        vps.producer_wait(p, "tensor", sv, par)
                        ins = None
                        for dc in range(DC):
                            ins = p.do("tensor", "matmul", bf["ps_dn0"][par][:, :],
                                       bf["x_ln2"][:, grp, dc, tb * 128:(tb + 1) * 128],
                                       wsl[:, dc, :],
                                       start=(dc == 0), stop=(dc == DC - 1))
                        v_done = p.inc("tensor", ins, st)
                        p.wait("vector", st, v_done)
                        ob_g.producer_wait(p, "vector", so, 0)
                        ins = p.do("vector", "tensor_copy", bf["v_out"][:, 0:512], bf["ps_dn0"][par][:, :])
                        ve = p.inc("vector", ins, sv)
                        vps.consumed(sv, par)
                        p.wait("sync", sv, ve)
                        vsem = kv_store("v", grp, q, bf["v_out"][:, 0:512], tb)
                        ob_g.consumed(vsem if vsem is not None else so, 0)
            wdb.done()
    p.wait("sync", so)


def run_phases_roled(nc, sems, regs, phases):
    items = [(k, c) for k, c in sems.items() if isinstance(c, Ctr)]
    state = {k: c.v for k, c in items}
    rstate = None
    for kind, fn in phases:
        if kind == "common":
            snap = dict(state)
            final = None
            with nc.Block() as blk:
                for eng in ENGS:
                    for k, c in items:
                        c.v = snap[k]
                    def body(e, eng=eng, fn=fn):
                        fn(P(nc, eng, sems))
                    getattr(blk, eng)(body)
                    f = {k: c.v for k, c in items}
                    assert final is None or f == final, f"lockstep div {eng}"
                    final = f
            state = final
            assert rstate is None, "common phase after fork unsupported"
        else:
            if rstate is None:
                rstate = {0: dict(state), 1: dict(state)}
            finals = {0: None, 1: None}
            with nc.Block() as blk:
                for eng in ENGS:
                    def body(e, eng=eng, fn=fn):
                        for r in (0, 1):
                            for k, c in items:
                                c.v = rstate[r][k]
                            geng = getattr(nc, eng)
                            with geng.If_eq(regs[eng]["role"], r):
                                fn(P(nc, eng, sems), r)
                            geng.end_ifs()
                            f = {k: c.v for k, c in items}
                            assert finals[r] is None or f == finals[r], f"lockstep div {eng} role {r}: " + str(
                                {k: (finals[r][k], f[k]) for k in f if f[k] != finals[r][k]})
                            finals[r] = f
                    getattr(blk, eng)(body)
            rstate = finals


def build_fused():
    nc = _mk_nc()
    stack = contextlib.ExitStack()
    x0 = nc.dram_tensor("x0", [D, TOK], F32, kind="ExternalInput").ap()
    x0f = nc.dram_tensor("x0f", [D, TOK], F32, kind="ExternalInput").ap()
    w = ffn_weight_aps(nc, "ExternalInput")
    w["W_Q"] = nc.dram_tensor("W_Q", [4, 128, DC, 512], F32, kind="ExternalInput").ap()
    w["W_K"] = nc.dram_tensor("W_K", [4, 128, DC, 512], F32, kind="ExternalInput").ap()
    w["W_V"] = nc.dram_tensor("W_V", [4, 128, DC, 512], F32, kind="ExternalInput").ap()
    ident = nc.dram_tensor("ident", [128, 128], F32, kind="ExternalInput").ap()
    onehot = nc.dram_tensor("onehot", [8, NE * 128], BF16, kind="ExternalInput").ap()
    cq = nc.dram_tensor("cq", [2, D, TOK], BF16, kind="ExternalInput").ap()
    ckv = nc.dram_tensor("ckv", [2, D, 2 * TOK], BF16, kind="ExternalInput").ap()
    dtab = nc.dram_tensor("dtab", [S, TOK], BF16, kind="ExternalInput").ap()
    meta = nc.dram_tensor("meta", [1, 4], mybir.dt.int32, kind="ExternalInput").ap()
    xout = nc.dram_tensor("xout", [D, TOK], F32, kind="ExternalOutput").ap()
    qt_d = nc.dram_tensor("qt_d", [D, TOK], BF16, kind="Internal").ap()
    kt_sh = nc.dram_tensor("kt_sh", [D, S], BF16, kind="Internal", addr_space="Shared").ap()
    v_sh = nc.dram_tensor("v_sh", [S, D], BF16, kind="Internal", addr_space="Shared").ap()
    flag_sh = nc.dram_tensor("flag_sh", [2, 1, 4], mybir.dt.int32, kind="Internal", addr_space="Shared").ap()

    bf = alloc_core_bufs(nc, stack)
    bf["ident_dram"] = ident
    bf["onehot_dram"] = onehot
    bf["meta_sb"] = stack.enter_context(nc.sbuf_tensor("meta_sb", [1, 4], mybir.dt.int32))
    sems = alloc_sems(nc, stack)
    for n in ("kvw", "flg", "pol"):
        sems[n] = Ctr(stack.enter_context(nc.semaphore(n)))
    regs = {}
    for eng in ENGS:
        regs[eng] = {"role": getattr(nc, eng).alloc_register(name=f"role_{eng}")}
    regs["sync"]["half"] = nc.sync.alloc_register(name="half_sync")
    regs["sync"]["nonce"] = nc.sync.alloc_register(name="nonce_sync")
    regs["sync"]["pollcond"] = nc.sync.alloc_register(name="pollcond")
    regs["sync"]["pollfv"] = nc.sync.alloc_register(name="pollfv")
    bf["regs"] = regs
    ind = {"x0": x0, "x0f": x0f, "cq": cq, "ckv": ckv, "dtab": dtab, "qt_d": qt_d,
           "kt_sh": kt_sh, "v_sh": v_sh, "flag_sh": flag_sh, "xout": xout}

    def kv_store_factory(p, role, wv_half_holder):
        half_reg = regs["sync"]["half"]

        def kv_store(kind, grp, q, buf, tb):
            for h in (0, 1):
                gbase = (h * TOK) if (role == 0 or bf["cur_pass"] == "foreign") else (RA + h * TOK)
                if p.active == "sync":
                    e = nc.sync
                    with e.If_eq(half_reg, h):
                        if kind == "k":
                            dst = kt_sh.rearrange("(dc p) m -> p dc m", p=128)[:, q * 4:(q + 1) * 4,
                                                                              gbase + grp * G:gbase + grp * G + G]
                        else:
                            r0 = gbase + grp * G + tb * 128
                            dst = v_sh[r0:r0 + 128, q * 512:(q + 1) * 512]
                        e.dma_start(out=dst, in_=buf).then_inc(sems["kvw"].h, 16)
                    e.end_ifs()
            sems["kvw"].v += 16
            return sems["kvw"]
        return kv_store

    def p0(p):
        emit_const_init(p, bf, sems)
        p.dma("sync", bf["meta_sb"][:, :], meta[:, :], sems["xload"])
        for eng in ENGS:
            p.wait(eng, sems["xload"])
            if p.active == eng:
                ge = getattr(nc, eng)
                ge.reg_load(regs[eng]["role"], bf["meta_sb"][0:1, 0:1])
                if eng == "sync":
                    ge.reg_load(regs[eng]["half"], bf["meta_sb"][0:1, 1:2])
                    ge.reg_load(regs[eng]["nonce"], bf["meta_sb"][0:1, 2:3])

    def mk_ffn_phase(src_ap, tag):
        def f(p, role):
            if src_ap is not None:
                if role == 0 and tag == "foreign":
                    return
                ap = x0f if tag == "foreign" else x0
                p.wait("sync", sems["v"])
                p.wait("sync", sems["t"])
                p.dma("sync", bf["x_res"][:, :, :], ap.rearrange("(dc p) t -> p dc t", p=128), sems["xload"])
                for eng in ("vector", "tensor", "scalar", "gpsimd"):
                    p.wait(eng, sems["xload"])
            emit_ffn_preamble(p, bf, sems, w)
            bf["wgu_db"] = DBuf(p, list(zip(bf["wga"], bf["wua"])), sems["wga_l"], sems["t"])
            bf["wda_db"] = DBuf(p, bf["wda"], sems["wda_l"], sems["t"])
            bf["pp_slot"] = Slot(4)
            bf["qq_slot"] = Slot(2)
            bf["gu_pairs"] = [(bf["ps_a"], bf["ps_b"]), (bf["ps_c"], bf["ps_d"]),
                              (bf["ps_dn0"][0], bf["ps_dn0"][1]), (bf["ps_dn0"][2], bf["ps_dn0"][3])]
            bf["silu_guard"] = Slot(4)
            bf["gtmp_guard"] = Slot(4)
            bf["xsq_guard"] = Slot(1)
            for grp in range(NGRP):
                pre = (sems["v"], sems["v"].v) if grp == 0 else None
                emit_ffn_group(p, bf, sems, w, grp, pre_t=pre)
        return f

    def mk_qkv_phase(tag):
        def f(p, role):
            if role == 0 and tag == "foreign":
                return
            bf["cur_pass"] = tag
            kv_store = kv_store_factory(p, role, None)
            if tag == "foreign":
                emit_qkv2(p, bf, sems, w, None, (ckv[0][:, 0:TOK], ckv[1][:, 0:TOK]), NGRP,
                          do_q=False, qt_dst=None, kv_store=kv_store, tok_base=0)
            else:
                emit_qkv2(p, bf, sems, w, (cq[0], cq[1]), (ckv[0][:, TOK:2 * TOK], ckv[1][:, TOK:2 * TOK]),
                          NGRP, do_q=True, qt_dst=qt_d, kv_store=kv_store, tok_base=0)
        return f

    def p_flags(p, role):
        half_reg = regs["sync"]["half"]
        nonce_reg = regs["sync"]["nonce"]
        p.wait("sync", sems["kvw"])
        for h in (0, 1):
            if p.active == "sync":
                with nc.sync.If_eq(half_reg, h):
                    nc.sync.dma_start(out=flag_sh[h, :, 0:1], in_=bf["meta_sb"][0:1, 2:3]).then_inc(sems["flg"].h, 16)
                nc.sync.end_ifs()
        sems["flg"].v += 16
        p.wait("sync", sems["flg"])
        if p.active == "sync":
            e = nc.sync
            cond = regs["sync"]["pollcond"]
            fv = regs["sync"]["pollfv"]
            for ph in (0, 1):
                e.reg_mov(cond, 1)
                with e.While(cond):
                    e.sem_clear(sems["pol"].h)
                    e.dma_start(out=bf["meta_sb"][0:1, 3:4], in_=flag_sh[ph, :, 0:1]).then_inc(sems["pol"].h, 16)
                    e.wait_ge(sems["pol"].h, 16)
                    e.reg_load(fv, bf["meta_sb"][0:1, 3:4])
                    e.reg_alu(cond, fv, nonce_reg, ALU.is_equal)
                    e.reg_alu(cond, cond, 1, ALU.bitwise_xor)
                e.sem_clear(sems["pol"].h)
        if p.active == "sync":
            nc.sync.sem_inc(sems["g"].h, 1)
        sems["g"].v += 1
        for eng in ("tensor", "gpsimd", "vector"):
            p.wait(eng, sems["g"])

    def p_attn(p, role):
        p.dma("sync", bf["qt_sb"][:, :, :], qt_d.rearrange("(dc p) t -> p dc t", p=128), sems["qtload"])
        for e2 in ("tensor", "vector", "gpsimd"):
            p.wait(e2, sems["qtload"])
        mx = RA if role == 0 else 4096
        emit_attention(p, bf, sems, {"kt_full": kt_sh[:, 0:mx], "v_full": v_sh[0:mx, :], "dtab": dtab[0:mx, :]}, m_ext=mx)

    def p_store(p, role):
        p.wait("sync", sems["v"])
        p.dma("sync", xout.rearrange("(dc p) t -> p dc t", p=128), bf["x_res"][:, :, :], sems["o"])
        p.wait("sync", sems["o"])

    run_phases_roled(nc, sems, regs, [("common", p0)])

    def roled_group(alloc_fns, fns):
        st2 = contextlib.ExitStack()
        for af in alloc_fns:
            for k2, v2 in af(nc, st2).items():
                bf[k2] = v2
        run2 = [("roled", fn) for fn in fns]
        _run_roled_cont(nc, sems, regs, run2)
        st2.close()

    global _ROLE_STATE
    _ROLE_STATE = None

    def _run_roled_cont(nc2, sems2, regs2, phs):
        global _ROLE_STATE
        items = [(k, c) for k, c in sems2.items() if isinstance(c, Ctr)]
        if _ROLE_STATE is None:
            base = {k: c.v for k, c in items}
            _ROLE_STATE = {0: dict(base), 1: dict(base)}
        for kind, fn in phs:
            finals = {0: None, 1: None}
            with nc2.Block() as blk:
                for eng in ENGS:
                    def body(e, eng=eng, fn=fn):
                        for r in (0, 1):
                            for k, c in items:
                                c.v = _ROLE_STATE[r][k]
                            geng = getattr(nc2, eng)
                            with geng.If_eq(regs2[eng]["role"], r):
                                fn(P(nc2, eng, sems2), r)
                            geng.end_ifs()
                            f = {k: c.v for k, c in items}
                            assert finals[r] is None or f == finals[r], f"lockstep div {eng} role {r}: " + str(
                                {k: (finals[r][k], f[k]) for k in (finals[r] or f) if f.get(k) != (finals[r] or {}).get(k)})
                            finals[r] = f
                    getattr(blk, eng)(body)
            _ROLE_STATE = finals

    roled_group([_mk_ffn_alloc("_1")], [mk_ffn_phase(x0f, "foreign")])
    roled_group([_mk_qkv_alloc("_1")], [mk_qkv_phase("foreign")])
    roled_group([_mk_ffn_alloc("_2")], [mk_ffn_phase(x0, "own")])
    roled_group([_mk_qkv_alloc("_2")], [mk_qkv_phase("own")])
    roled_group([], [p_flags])
    roled_group([alloc_attn_bufs], [p_attn])
    roled_group([_mk_ffn_alloc("_3")], [mk_ffn_phase(None, "ffn2")])
    roled_group([], [p_store])

    stack.close()
    return nc


def _mk_ffn_alloc(tagname):
    def f(nc, stack):
        bf = {}
        def S_(name, shape, dt):
            return stack.enter_context(nc.sbuf_tensor(name + tagname, shape, dt))
        bf["x_ln"] = S_("x_ln", [128, DC, G], BF16)
        bf["xsq"] = S_("xsq", [128, DC, G], BF16)
        bf["grnd"] = S_("grnd", [128, 2, HIE, G], BF16)
        bf["gsh"] = bf["grnd"].rearrange("p a b t -> p (a b) t")
        bf["comb"] = S_("comb", [128, NE, G], BF16)
        bf["comb_f32"] = S_("comb_f32", [128, 4, 8], F32)
        bf["scores_tmp"] = S_("scores_tmp", [128, 4, 8], F32)
        bf["comb8"] = S_("comb8", [8, G], BF16)
        bf["silu_sb"] = S_("silu_sb", [128, 4, G], BF16)
        bf["gtmp"] = S_("gtmp", [128, 4, G], BF16)
        bf["wga"] = [S_(f"wga{i}", [128, 2, DC, 128], BF16) for i in range(2)]
        bf["wua"] = [S_(f"wua{i}", [128, 2, DC, 128], BF16) for i in range(2)]
        bf["wda"] = [S_(f"wda{i}", [128, HIE, G], BF16) for i in range(2)]
        bf["ident"] = S_("ident_sb", [128, 128], F32)
        bf["onehot"] = S_("onehot_sb", [8, NE * 128], BF16)
        bf["gw"] = S_("gw_sb", [128, DC, NE], BF16)
        return bf
    return f


def _mk_qkv_alloc(tagname):
    def f(nc, stack):
        bf = {}
        def S_(name, shape, dt):
            return stack.enter_context(nc.sbuf_tensor(name + tagname, shape, dt))
        bf["x_ln2"] = S_("x_ln2", [128, NGRP, DC, G], BF16)
        bf["xsq"] = S_("xsq_q", [128, DC, G], BF16)
        bf["wq4"] = [S_(f"wq4_{i}", [128, DC, 512], BF16) for i in range(2)]
        bf["ctab"] = S_("ctab", [128, 4, G], BF16)
        bf["stab"] = S_("stab", [128, 4, G], BF16)
        bf["qk_t"] = S_("qk_t", [128, 4, G], BF16)
        bf["qk_r"] = S_("qk_r", [128, 4, G], BF16)
        bf["qk_a"] = S_("qk_a", [128, 4, G], BF16)
        bf["qk_out"] = S_("qk_out", [128, 4, G], BF16)
        bf["v_out"] = bf["qk_out"].rearrange("p a t -> p (a t)")[:, 0:512]
        return bf
    return f


alloc_ffn_bufs2 = _mk_ffn_alloc("_2")
alloc_ffn_bufs3 = _mk_ffn_alloc("_3")
alloc_qkv_bufs2 = _mk_qkv_alloc("_2")


def fused_core_inputs(c, X, wmap, WQ, WK, WV, nonce):
    b, role, half = c // 4, (c % 4) // 2, c % 2
    own0 = (RA if role else 0) + half * TOK
    ident, onehot = _consts()
    own_rows = np.arange(own0, own0 + TOK)
    for_rows = np.arange(half * TOK, half * TOK + TOK)
    cq = np.stack(host_xpos_tables(own_rows, False))
    ck_own = host_xpos_tables(own_rows, True)
    if role:
        ck_for = host_xpos_tables(for_rows, True)
    else:
        z = np.zeros((D, TOK), dtype=ml_dtypes.bfloat16)
        ck_for = (z, z)
    ckv = np.stack([np.concatenate([ck_for[0], ck_own[0]], axis=1),
                    np.concatenate([ck_for[1], ck_own[1]], axis=1)])
    m = dict(wmap)
    m.update({
        "x0": fm(X[b, own0:own0 + TOK, :]),
        "x0f": fm(X[b, half * TOK:half * TOK + TOK, :]) if role else np.zeros((D, TOK), np.float32),
        "W_Q": WQ, "W_K": WK, "W_V": WV,
        "ident": ident, "onehot": onehot, "cq": cq, "ckv": ckv,
        "dtab": host_decay(own_rows, S).astype(ml_dtypes.bfloat16),
        "meta": np.array([[role, half, nonce, 0]], dtype=np.int32),
    })
    return m


def kernel(L0_attn_X, L0_attn_W_Q, L0_attn_W_K, L0_attn_W_V, L0_ffn_gate_w,
           L0_ffn_w_gate, L0_ffn_w_up, L0_ffn_w_down, L0_ffn_sw_gate,
           L0_ffn_sw_up, L0_ffn_sw_down):
    X = np.asarray(L0_attn_X)
    wmap = {
        "w_gate": np.ascontiguousarray(L0_ffn_w_gate),
        "w_up": np.ascontiguousarray(L0_ffn_w_up),
        "w_down": np.ascontiguousarray(L0_ffn_w_down),
        "sw_gate": np.ascontiguousarray(L0_ffn_sw_gate),
        "sw_up": np.ascontiguousarray(L0_ffn_sw_up),
        "sw_down": np.ascontiguousarray(L0_ffn_sw_down),
        "gate_w": np.ascontiguousarray(L0_ffn_gate_w),
    }
    wmap = stage_weights(wmap)
    WQ = stage_wqkv(L0_attn_W_Q)
    WK = stage_wqkv(L0_attn_W_K)
    WV = stage_wqkv(L0_attn_W_V)
    nonce = int(np.random.randint(1, 2**31 - 1))
    nc = build_fused()
    ins = [fused_core_inputs(c, X, wmap, WQ, WK, WV, nonce) for c in range(8)]
    r = run_bass_kernel_spmd(nc, ins, core_ids=list(range(8)))
    out = np.empty((B, S, D), dtype=np.float32)
    for c in range(8):
        b, role, half = c // 4, (c % 4) // 2, c % 2
        own0 = (RA if role else 0) + half * TOK
        out[b, own0:own0 + TOK, :] = r.results[c]["xout"].T
    return out


# version 11
# speedup vs baseline: 1.8128x; 1.0828x over previous
import numpy as np
import ml_dtypes
import concourse.bass as bass
import concourse.mybir as mybir
from concourse.bass_utils import run_bass_kernel_spmd

F32 = mybir.dt.float32
BF16 = mybir.dt.bfloat16
AF = mybir.ActivationFunctionType
ALU = mybir.AluOpType
AX = mybir.AxisListType.X

D = 2048
DC = 16
HE = 1408
HIE = 11
HS = 2816
HIS = 22
NE = 8
GAMMA = 0.96875
EPS = 1e-5
G = 512
TOK = 1024
NGRP = TOK // G
S = 4096
B = 2
ENGS = ("sync", "gpsimd", "tensor", "vector", "scalar")
ROUNDS = [(0, 1), (2, 3), (4, 5), (6, 7), ("sh",)]


def host_xpos_tables(rows, downscale):
    rows = np.asarray(rows, dtype=np.float64)
    half = D // 2
    j = np.arange(half, dtype=np.float64)
    scale_vec = (2.0 * j + 0.4 * D) / (1.4 * D)
    scale = scale_vec[None, :] ** (rows / 512.0)[:, None]
    if downscale:
        scale = 1.0 / scale
    inv_freq = 1.0 / 10000.0 ** (j / half)
    sinusoid = rows[:, None] * inv_freq[None, :]
    sin = np.sin(sinusoid) * scale
    cos = np.cos(sinusoid) * scale
    C = np.empty((D, len(rows)), dtype=np.float64)
    Sg = np.empty((D, len(rows)), dtype=np.float64)
    C[0::2, :] = cos.T
    C[1::2, :] = cos.T
    Sg[0::2, :] = -sin.T
    Sg[1::2, :] = sin.T
    return C.astype(ml_dtypes.bfloat16), Sg.astype(ml_dtypes.bfloat16)


def host_decay(row_ids, m_ext):
    n = np.asarray(row_ids, dtype=np.float64)[None, :]
    m = np.arange(m_ext, dtype=np.float64)[:, None]
    diff = n - m
    with np.errstate(over="ignore", under="ignore"):
        Dm = np.where(diff >= 0, np.exp(np.log(GAMMA) * np.maximum(diff, 0.0)), 0.0)
    return Dm.astype(np.float32)


def fm(a):
    return np.ascontiguousarray(np.asarray(a).T)



class Ctr:
    __slots__ = ("h", "v")

    def __init__(self, handle):
        self.h = handle
        self.v = 0


class P:
    def __init__(self, nc, active, sems):
        self.nc = nc
        self.active = active
        self.sems = sems
        self.pend = {e: [] for e in ENGS}

    def defer_inc(self, eng, sem, n=1):
        self.pend[eng].append((sem, n))

    def flush_pend(self, eng, ins, limit=1):
        k = 0
        while self.pend[eng] and k < limit:
            sem, n = self.pend[eng].pop(0)
            sem.v += n
            if self.active == eng and ins is not None:
                ins.then_inc(sem.h, n)
            k += 1

    def do(self, eng, fn, *args, **kw):
        if self.active == eng:
            return getattr(getattr(self.nc, eng), fn)(*args, **kw)
        return None

    def inc(self, eng, ins, sem, n=1):
        sem.v += n
        if self.active == eng:
            assert ins is not None, f"inc on missing ins ({eng})"
            ins.then_inc(sem.h, n)
        return sem.v

    def wait(self, eng, sem, val=None):
        if val is None:
            val = sem.v
        assert val <= sem.v, f"wait {val} > emitted {sem.v}"
        if self.active == eng and val > 0:
            getattr(self.nc, eng).wait_ge(sem.h, val)

    def dma(self, eng, out, in_, sem, **kw):
        ins = self.do(eng, "dma_start", out=out, in_=in_, **kw)
        return self.inc(eng, ins, sem, 16)


class DBuf:

    def __init__(self, p, bufs, loaded, prog, feng="gpsimd", wbsem=None, wbeng="scalar"):
        self.p = p
        self.bufs = bufs
        self.n = len(bufs)
        self.loaded = loaded
        self.prog = prog
        self.feng = feng
        self.fills = 0
        self.loaded_at = {}
        self.done_at = []
        self.wbsem = wbsem
        self.wbeng = wbeng
        self.wb_at = {}

    def fill(self, srcs, wb=None):
        if self.fills >= self.n:
            assert len(self.done_at) > self.fills - self.n, "DBuf.done() missing"
            old = self.fills - self.n
            self.p.wait(self.feng, self.prog, self.done_at[old])
            if old in self.wb_at:
                self.p.wait(self.feng, self.wbsem, self.wb_at[old])
        slot = self.bufs[self.fills % self.n]
        for dst_fn, src in srcs:
            self.p.dma(self.feng, dst_fn(slot), src, self.loaded)
        self.loaded_at[self.fills] = self.loaded.v
        if wb:
            self.p.wait(self.wbeng, self.loaded, self.loaded.v)
            for dst, src_fn in wb:
                self.p.dma(self.wbeng, dst, src_fn(slot), self.wbsem)
            self.wb_at[self.fills] = self.wbsem.v
        self.fills += 1
        return self.fills - 1

    def use(self, i, eng="tensor"):
        self.p.wait(eng, self.loaded, self.loaded_at[i])
        return self.bufs[i % self.n]

    def done(self):
        self.done_at.append(self.prog.v)


class Slot:

    def __init__(self, nslots):
        self.readers = [None] * nslots

    def producer_wait(self, p, eng, sem, slot):
        r = self.readers[slot]
        if r is not None:
            p.wait(eng, r[0], r[1])

    def consumed(self, sem, slot):
        self.readers[slot] = (sem, sem.v)



def emit_ln(p, bf, sm, grp, pre_t=None, dst=None):
    sv, st, sa = sm["v"], sm["t"], sm["a"]
    g0 = grp * G
    x_ln = bf["x_ln"] if dst is None else dst
    rows = bf["rows"]
    mu, ex2, var, lnv, rstd = (rows[:, i * G:(i + 1) * G] for i in range(5))
    ins = None
    for dc in range(DC):
        ins = p.do("vector", "tensor_copy", x_ln[:, dc, :], bf["x_res"][:, dc, g0:g0 + G])
    cast_done = p.inc("vector", ins, sv)
    ins = None
    bf["xsq_guard"].producer_wait(p, "scalar", st, 0)
    for dc in range(DC):
        ins = p.do("scalar", "activation", bf["xsq"][:, dc, :], bf["x_res"][:, dc, g0:g0 + G], AF.Square)
    sq_done = p.inc("scalar", ins, sa)

    p.wait("tensor", sv, cast_done)
    p.wait("tensor", sa, sq_done)
    if pre_t is not None:
        p.wait("tensor", pre_t[0], pre_t[1])
    for dc in range(DC):
        ins = p.do("tensor", "matmul", bf["ps_b"][0:1, :], bf["ones_bf"][:, :], x_ln[:, dc, :], start=(dc == 0), stop=(dc == DC - 1))
        if dc == 0:
            p.flush_pend("tensor", ins, limit=1)
    ins = None
    for dc in range(DC):
        ins = p.do("tensor", "matmul", bf["ps_d"][0:1, :], bf["ones_bf"][:, :], bf["xsq"][:, dc, :], start=(dc == 0), stop=(dc == DC - 1))
    stats_done = p.inc("tensor", ins, st)
    bf["xsq_guard"].consumed(st, 0)

    p.wait("vector", st, stats_done)
    p.do("vector", "tensor_scalar", mu, bf["ps_b"][0:1, :], 1.0 / D, None, ALU.mult)
    p.do("vector", "tensor_scalar", ex2, bf["ps_d"][0:1, :], 1.0 / D, None, ALU.mult)
    p.do("vector", "drain")
    p.do("vector", "tensor_tensor", var, mu, mu, ALU.mult)
    p.do("vector", "drain")
    ins = p.do("vector", "tensor_tensor", var, ex2, var, ALU.subtract)
    var_done = p.inc("vector", ins, sv)
    p.wait("scalar", sv, var_done)
    p.do("scalar", "activation", lnv, var, AF.Ln, bias=bf["rows"][0:1, 5 * G:5 * G + 1])
    p.do("scalar", "drain")
    ins = p.do("scalar", "activation", rstd, lnv, AF.Exp, scale=-0.5)
    rstd_done = p.inc("scalar", ins, sa)

    p.wait("tensor", sa, rstd_done)
    p.do("tensor", "matmul", bf["ps_a"][:, :], bf["ones_f32"][:, :], mu, start=True, stop=True)
    ins = p.do("tensor", "matmul", bf["ps_c"][:, :], bf["ones_f32"][:, :], rstd, start=True, stop=True)
    bcast_done = p.inc("tensor", ins, st)

    p.wait("vector", st, bcast_done)
    ins = None
    for dc in range(DC):
        p.do("vector", "tensor_tensor", bf["ln_tmp"][:, dc % 2, :], bf["x_res"][:, dc, g0:g0 + G], bf["ps_a"][:, :], ALU.subtract)
        if dc >= 1:
            ins = p.do("vector", "tensor_tensor", x_ln[:, dc - 1, :], bf["ln_tmp"][:, (dc - 1) % 2, :], bf["ps_c"][:, :], ALU.mult)
    p.do("vector", "drain")
    ins = p.do("vector", "tensor_tensor", x_ln[:, DC - 1, :], bf["ln_tmp"][:, (DC - 1) % 2, :], bf["ps_c"][:, :], ALU.mult)
    return p.inc("vector", ins, sv)


def emit_gating(p, bf, sm, xln_done):
    sv, st, sa = sm["v"], sm["t"], sm["a"]
    p.wait("tensor", sv, xln_done)
    ins = None
    for tb in range(4):
        for dc in range(DC):
            ins = p.do("tensor", "matmul", bf["ps_b"][:, 32 + tb * 8:40 + tb * 8],
                       bf["x_ln"][:, dc, tb * 128:(tb + 1) * 128],
                       bf["gw"][:, dc, :], start=(dc == 0), stop=(dc == DC - 1))
    logits_done = p.inc("tensor", ins, st)

    comb_sb = bf["comb_f32"]
    stmp = bf["scores_tmp"]
    srows = bf["srows"]
    mx = lambda tb: srows[:, tb, 0:1]
    smr = lambda tb: srows[:, tb, 1:2]
    rcp = lambda tb: srows[:, tb, 2:3]
    m1 = lambda tb: srows[:, tb, 3:4]
    m2 = lambda tb: srows[:, tb, 4:5]
    lg = lambda tb: bf["ps_b"][:, 32 + tb * 8:40 + tb * 8]

    p.wait("vector", st, logits_done)
    for tb in range(4):
        p.do("vector", "tensor_reduce", mx(tb), lg(tb), AX, ALU.max, negate=True)
    ins = None
    for tb in range(4):
        ins = p.do("vector", "tensor_copy", stmp[:, tb, :], lg(tb))
    pre_exp = p.inc("vector", ins, sv)
    p.wait("scalar", sv, pre_exp)
    ins = None
    for tb in range(4):
        ins = p.do("scalar", "activation", comb_sb[:, tb, :], stmp[:, tb, :], AF.Exp, bias=mx(tb))
    exp_done = p.inc("scalar", ins, sa)
    p.wait("vector", sa, exp_done)

    def step(fn):
        for tb in range(4):
            fn(tb)
        p.do("vector", "drain")

    step(lambda tb: p.do("vector", "tensor_reduce", smr(tb), comb_sb[:, tb, :], AX, ALU.add))
    step(lambda tb: p.do("vector", "reciprocal", rcp(tb), smr(tb)))
    step(lambda tb: p.do("vector", "tensor_scalar", comb_sb[:, tb, :], comb_sb[:, tb, :], rcp(tb), None, ALU.mult))
    step(lambda tb: p.do("vector", "tensor_reduce", m1(tb), comb_sb[:, tb, :], AX, ALU.max))
    step(lambda tb: p.do("vector", "tensor_scalar", stmp[:, tb, :], comb_sb[:, tb, :], m1(tb), None, ALU.is_ge))
    step(lambda tb: p.do("vector", "scalar_tensor_tensor", stmp[:, tb, :], stmp[:, tb, :], -2.0, comb_sb[:, tb, :], ALU.mult, ALU.add))
    step(lambda tb: p.do("vector", "tensor_reduce", m2(tb), stmp[:, tb, :], AX, ALU.max))
    step(lambda tb: p.do("vector", "tensor_scalar", stmp[:, tb, :], comb_sb[:, tb, :], m2(tb), None, ALU.is_ge))
    ins = None
    for tb in range(4):
        ins = p.do("vector", "tensor_tensor", comb_sb[:, tb, :], comb_sb[:, tb, :], stmp[:, tb, :], ALU.mult)
    comb_done = p.inc("vector", ins, sv)

    p.wait("tensor", sv, comb_done)
    ins = None
    for tb in range(4):
        ins = p.do("tensor", "transpose", bf["ps_d"][0:8, tb * 128:(tb + 1) * 128], comb_sb[:, tb, :], bf["ident"][:, :])
    tr_done = p.inc("tensor", ins, st)
    p.wait("vector", st, tr_done)
    ins = p.do("vector", "tensor_copy", bf["comb8"][:, :], bf["ps_d"][0:8, 0:G])
    c8_done = p.inc("vector", ins, sv)

    banks = [bf["ps_a"], bf["ps_b"], bf["ps_c"], bf["ps_d"]] + bf["ps_dn0"]
    p.wait("tensor", sv, c8_done)
    ins = None
    for e in range(NE):
        ins = p.do("tensor", "matmul", banks[e][:, :], bf["onehot"][:, e * 128:(e + 1) * 128], bf["comb8"][:, :], start=True, stop=True)
    be = p.inc("tensor", ins, st)
    p.wait("vector", st, be)
    ins = None
    for e in range(NE):
        ins = p.do("vector", "tensor_copy", bf["comb"][:, e, :], banks[e][:, :])
    p.inc("vector", ins, sv)
    return sv.v


def emit_ffn_group(p, bf, sm, w, grp, pre_t=None, wcfg=None):
    sv, st, sa, sg = sm["v"], sm["t"], sm["a"], sm["g"]
    g0 = grp * G
    xln_done = emit_ln(p, bf, sm, grp, pre_t=pre_t)
    comb_done = emit_gating(p, bf, sm, xln_done)

    wgu, wda = bf["wgu_db"], bf["wda_db"]
    pp = bf["pp_slot"]
    qq = bf["qq_slot"]
    silu_guard = bf["silu_guard"]
    gtmp_guard = bf["gtmp_guard"]

    for ri, rnd in enumerate(ROUNDS):
        shared = rnd == ("sh",)
        n_hi = HIS if shared else HIE
        gdst = bf["gsh"] if shared else bf["grnd"]
        hb_sizes = []
        left = n_hi
        while left > 0:
            hb_sizes.append(min(2, left))
            left -= 2
        elist = [None] if shared else list(rnd)
        for ei, e in enumerate(elist):
            for hbi, hbsz in enumerate(hb_sizes):
                h0 = hbi * 2 * 128
                wb = None
                if wcfg is not None and wcfg[0] == "read":
                    wc = wcfg[1]
                    if shared:
                        gsrc = wc["sw_gate"][hbi, :, 0:hbsz]
                        usrc = wc["sw_up"][hbi, :, 0:hbsz]
                    else:
                        gsrc = wc["w_gate"][e, hbi, :, 0:hbsz]
                        usrc = wc["w_up"][e, hbi, :, 0:hbsz]
                else:
                    if shared:
                        gsrc = w["sw_gate"][hbi * 2:hbi * 2 + hbsz].rearrange("hi p dc h -> p hi dc h")
                        usrc = w["sw_up"][hbi * 2:hbi * 2 + hbsz].rearrange("hi p dc h -> p hi dc h")
                    else:
                        gsrc = w["w_gate"][e, hbi * 2:hbi * 2 + hbsz].rearrange("hi p dc h -> p hi dc h")
                        usrc = w["w_up"][e, hbi * 2:hbi * 2 + hbsz].rearrange("hi p dc h -> p hi dc h")
                    if wcfg is not None:
                        wc = wcfg[1]
                        if shared:
                            gwd, uwd = wc["sw_gate"][hbi, :, 0:hbsz], wc["sw_up"][hbi, :, 0:hbsz]
                        else:
                            gwd, uwd = wc["w_gate"][e, hbi, :, 0:hbsz], wc["w_up"][e, hbi, :, 0:hbsz]
                        wb = [(gwd, lambda sl, n=hbsz: sl[0][:, 0:n, :, :]),
                              (uwd, lambda sl, n=hbsz: sl[1][:, 0:n, :, :])]
                fgu = wgu.fill([
                    (lambda sl, n=hbsz: sl[0][:, 0:n, :, :], gsrc),
                    (lambda sl, n=hbsz: sl[1][:, 0:n, :, :], usrc),
                ], wb=wb)
                for hj in range(hbsz):
                    hi = hbi * 2 + hj
                    par = hi % 4
                    pg, pu = bf["gu_pairs"][par]
                    if ri == 0 and ei == 0 and hi <= 3:
                        p.wait("tensor", sv, comb_done)
                    else:
                        pp.producer_wait(p, "tensor", sv, par)
                        qq.producer_wait(p, "tensor", sv, 0)
                        qq.producer_wait(p, "tensor", sv, 1)
                    gsl, usl = wgu.use(fgu)
                    ins = None
                    for dc in range(DC):
                        ins = p.do("tensor", "matmul", pg[:, :], gsl[:, hj, dc, :], bf["x_ln"][:, dc, :], start=(dc == 0), stop=(dc == DC - 1))
                    for dc in range(DC):
                        ins = p.do("tensor", "matmul", pu[:, :], usl[:, hj, dc, :], bf["x_ln"][:, dc, :], start=(dc == 0), stop=(dc == DC - 1))
                    gu_done = p.inc("tensor", ins, st)
                    if hj == hbsz - 1:
                        wgu.done()

                    p.wait("scalar", st, gu_done)
                    silu_guard.producer_wait(p, "scalar", sv, par)
                    ins = p.do("scalar", "activation", bf["silu_sb"][:, par, :], pg[:, :], AF.Silu)
                    silu_done = p.inc("scalar", ins, sa)
                    p.wait("vector", sa, silu_done)
                    if shared:
                        ins = p.do("vector", "tensor_tensor", gdst[:, hi, :], pu[:, :], bf["silu_sb"][:, par, :], ALU.mult)
                        mul_done = p.inc("vector", ins, sv)
                        silu_guard.consumed(sv, par)
                        pp.consumed(sv, par)
                    else:
                        p.do("vector", "tensor_tensor", bf["gtmp"][:, par, :], pu[:, :], bf["silu_sb"][:, par, :], ALU.mult)
                        p.do("vector", "drain")
                        ins = p.do("vector", "tensor_tensor", gdst[:, ei, hi, :], bf["gtmp"][:, par, :], bf["comb"][:, e, :], ALU.mult)
                        mul_done = p.inc("vector", ins, sv)
                        silu_guard.consumed(sv, par)
                        pp.consumed(sv, par)
        g_ready_sem = sv
        g_ready = g_ready_sem.v

        for dq in range(4):
            d0 = dq * 512
            qset = 0 if dq % 2 == 0 else 1
            pq = bf["ps_dn0"] if qset == 0 else bf["ps_dn1"]
            subs = [0, 1] if shared else list(rnd)
            ins = None
            first_mm = True
            for si, sub in enumerate(subs):
                wb = None
                if wcfg is not None and wcfg[0] == "read":
                    srcw = (wcfg[1]["sw_down"] if shared else wcfg[1]["w_down"])[sub, dq]
                else:
                    srcw = (w["sw_down"] if shared else w["w_down"])[sub, dq]
                    if wcfg is not None:
                        ddst = (wcfg[1]["sw_down"] if shared else wcfg[1]["w_down"])[sub, dq]
                        wb = [(ddst, lambda sl: sl[:, :, :])]
                fd = wda.fill([(lambda sl: sl[:, :, :], srcw)], wb=wb)
                if si == 0:
                    p.wait("tensor", g_ready_sem, g_ready)
                    qq.producer_wait(p, "tensor", sv, qset)
                dsl = wda.use(fd)
                last_si = si == len(subs) - 1
                for hi in range(HIE):
                    gsrc_t = bf["gsh"][:, (sub * HIE + hi) if shared else 0, :] if shared else bf["grnd"][:, si, hi, :]
                    for dt in range(4):
                        ins = p.do("tensor", "matmul", pq[dt][:, :], dsl[:, hi, dt * 128:(dt + 1) * 128], gsrc_t,
                                   start=first_mm, stop=(last_si and hi == HIE - 1))
                    first_mm = False
                p.inc("tensor", ins, st)
                wda.done()
            dq_done = st.v

            p.wait("vector", st, dq_done)
            ins = None
            for dt in range(4):
                dcg = dq * 4 + dt
                ins = p.do("vector", "scalar_tensor_tensor", bf["x_res"][:, dcg, g0:g0 + G],
                           pq[dt][:, :], 0.5, bf["x_res"][:, dcg, g0:g0 + G], ALU.mult, ALU.add)
            ev = p.inc("vector", ins, sv)
            qq.consumed(sv, qset)
            if qset == 0:
                pp.consumed(sv, 2)
                pp.consumed(sv, 3)
            else:
                pp.consumed(sv, 0)
                pp.consumed(sv, 1)
    return sv.v



def emit_qkv(p, bf, sm, w, outs):
    sv, st, sa, sg, so = sm["v"], sm["t"], sm["a"], sm["g"], sm["o"]
    wload, tl = sm["wload"], sm["tabload"]

    bf["xsq_guard"] = Slot(1)
    for grp in range(NGRP):
        emit_ln(p, bf, sm, grp, dst=bf["x_ln2"][:, grp])

    qk_g = Slot(2)
    ob_g = Slot(1)
    qpp = Slot(2)
    vps = Slot(1)
    shuffle_mask = [i ^ 1 for i in range(32)]

    for wname in ("W_Q", "W_K", "W_V"):
        for half in range(2):
            p.wait("gpsimd", st)
            p.dma("gpsimd", bf["wfull"][:, :, :],
                  w[wname][:, half * 1024:(half + 1) * 1024].rearrange("(dc p) o -> p dc o", p=128), wload)
            for grp in range(NGRP):
                if wname in ("W_Q", "W_K"):
                    tabs = bf["cs_q"] if wname == "W_Q" else bf["cs_k"]
                    p.wait("sync", sg)
                    p.wait("sync", sv)
                    p.dma("sync", bf["ctab"][:, :, :], tabs[0].rearrange("(dc p) t -> p dc t", p=128)[:, :, grp * G:grp * G + G], tl)
                    p.dma("sync", bf["stab"][:, :, :], tabs[1].rearrange("(dc p) t -> p dc t", p=128)[:, :, grp * G:grp * G + G], tl)
                    p.wait("tensor", wload)
                    for dto in range(8):
                        dt = half * 8 + dto
                        par = dt % 2
                        pq = bf["ps_a"] if par == 0 else bf["ps_c"]
                        if dto <= 1:
                            p.wait("tensor", sv)
                        qpp.producer_wait(p, "tensor", sv, par)
                        ins = None
                        for dc in range(DC):
                            ins = p.do("tensor", "matmul", pq[:, :], bf["wfull"][:, dc, dto * 128:(dto + 1) * 128],
                                       bf["x_ln2"][:, grp, dc, :], start=(dc == 0), stop=(dc == DC - 1))
                            if dc == 0:
                                p.flush_pend("tensor", ins, limit=1)
                        mm_done = p.inc("tensor", ins, st)
                        p.wait("vector", st, mm_done)
                        if dto == 0:
                            p.wait("vector", tl)
                            ob_g.producer_wait(p, "vector", so, 0)
                        qk_g.producer_wait(p, "vector", sv, par)
                        p.do("vector", "tensor_copy", bf["qk_t"][:, par, :], pq[:, :])
                        p.do("vector", "drain")
                        ins = p.do("vector", "stream_shuffle", bf["qk_r"][:, par, :], bf["qk_t"][:, par, :], shuffle_mask)
                        sh_done = p.inc("vector", ins, sv)
                        qpp.consumed(sv, par)
                        p.wait("gpsimd", sv, sh_done)
                        ins = p.do("gpsimd", "tensor_tensor", bf["qk_a"][:, par, :], bf["qk_t"][:, par, :], bf["ctab"][:, dt, :], ALU.mult)
                        a_done = p.inc("gpsimd", ins, sg)
                        p.wait("vector", sg, a_done)
                        p.do("vector", "tensor_tensor", bf["qk_r"][:, par, :], bf["qk_r"][:, par, :], bf["stab"][:, dt, :], ALU.mult)
                        p.do("vector", "drain")
                        ins = p.do("vector", "tensor_tensor", bf["qk_out"][:, dto, :], bf["qk_r"][:, par, :], bf["qk_a"][:, par, :], ALU.add)
                        p.inc("vector", ins, sv)
                        qk_g.consumed(sv, par)
                    p.wait("sync", sv)
                    dst = outs["qt"] if wname == "W_Q" else outs["kt"]
                    p.dma("sync", dst.rearrange("(dc p) t -> p dc t", p=128)[:, half * 8:(half + 1) * 8, grp * G:grp * G + G],
                          bf["qk_out"][:, 0:8, :], so)
                    ob_g.consumed(so, 0)
                else:
                    p.wait("tensor", wload)
                    for tb in range(4):
                        if tb == 0:
                            p.wait("tensor", sv)
                        vps.producer_wait(p, "tensor", sv, 0)
                        ins = None
                        for dc in range(DC):
                            for ds in range(2):
                                ins = p.do("tensor", "matmul", bf["ps_dn0"][ds][:, :],
                                           bf["x_ln2"][:, grp, dc, tb * 128:(tb + 1) * 128],
                                           bf["wfull"][:, dc, ds * 512:(ds + 1) * 512],
                                           start=(dc == 0), stop=(dc == DC - 1))
                        v_done = p.inc("tensor", ins, st)
                        p.wait("vector", st, v_done)
                        ob_g.producer_wait(p, "vector", so, 0)
                        ins = None
                        for ds in range(2):
                            ins = p.do("vector", "tensor_copy", bf["v_out"][:, ds * 512:(ds + 1) * 512], bf["ps_dn0"][ds][:, :])
                        ve = p.inc("vector", ins, sv)
                        vps.consumed(sv, 0)
                        p.wait("sync", sv, ve)
                        p.dma("sync", outs["v"][grp * G + tb * 128:grp * G + (tb + 1) * 128, half * 1024:(half + 1) * 1024],
                              bf["v_out"][:, 0:1024], so)
                        ob_g.consumed(so, 0)
    p.wait("sync", so)



def emit_attention(p, bf, sm, ind, m_ext=None):
    sv, st = sm["v"], sm["t"]
    kload, vload, dload = sm["kload"], sm["vload"], sm["dload"]
    MT = (m_ext if m_ext is not None else S) // 128
    KB = 4
    pband = bf["pband"]
    kdb = DBuf(p, bf["kbufs"], kload, sm["t"], feng="gpsimd")
    ddb = DBuf(p, bf["dbufs"], dload, sm["v"], feng="sync")
    vdb = DBuf(p, bf["vbufs"], vload, sm["t"], feng="gpsimd")
    app = Slot(2)
    aq = Slot(1)

    for grp in range(NGRP):
        n0 = grp * G
        for mtb in range(MT // KB):
            fk = kdb.fill([(lambda sl: sl[:, :, :],
                            ind["kt_full"].rearrange("(dc p) m -> p dc m", p=128)[:, :, mtb * KB * 128:(mtb + 1) * KB * 128])])
            fdt = ddb.fill([(lambda sl: sl[:, :, :],
                             ind["dtab"][mtb * KB * 128:(mtb + 1) * KB * 128, n0:n0 + G].rearrange("(mc p) n -> p mc n", p=128))])
            for mi in range(KB):
                mt = mtb * KB + mi
                par = mt % 2
                pq = pband[par]
                app.producer_wait(p, "tensor", sv, par)
                if grp == 0 and mt <= 1:
                    p.wait("tensor", sv)
                if mt <= 1:
                    aq.producer_wait(p, "tensor", sv, 0)
                ksl = kdb.use(fk)
                ins = None
                for dc in range(DC):
                    ins = p.do("tensor", "matmul", pq[:, :], ksl[:, dc, mi * 128:(mi + 1) * 128],
                               bf["qt_sb"][:, dc, n0:n0 + G], start=(dc == 0), stop=(dc == DC - 1))
                qk_done = p.inc("tensor", ins, st)
                if mi == KB - 1:
                    kdb.done()
                p.wait("vector", st, qk_done)
                if mi == 0:
                    p.wait("vector", dload, ddb.loaded_at[fdt])
                dsl = bf["dbufs"][fdt % 2]
                ins = p.do("vector", "tensor_tensor", bf["att_sb"][:, mt, :], pq[:, :], dsl[:, mi, :], ALU.mult)
                p.inc("vector", ins, sv)
                app.consumed(sv, par)
                if mi == KB - 1:
                    ddb.done()
        att_all = sv.v
        for dh in range(2):
            ins = None
            for mtb in range(MT // KB):
                fv = vdb.fill([(lambda sl: sl[:, :, :],
                                ind["v_full"][mtb * KB * 128:(mtb + 1) * KB * 128, dh * 1024:(dh + 1) * 1024].rearrange("(mc p) d -> p mc d", p=128))])
                if mtb == 0:
                    p.wait("tensor", sv, att_all)
                    aq.producer_wait(p, "tensor", sv, 0)
                    app.producer_wait(p, "tensor", sv, 0)
                    app.producer_wait(p, "tensor", sv, 1)
                vsl = vdb.use(fv)
                for mi in range(KB):
                    mt = mtb * KB + mi
                    for dt in range(8):
                        ins = p.do("tensor", "matmul", pband[dt][:, :], vsl[:, mi, dt * 128:(dt + 1) * 128],
                                   bf["att_sb"][:, mt, :], start=(mt == 0), stop=(mt == MT - 1))
                p.inc("tensor", ins, st)
                vdb.done()
            oh_done = st.v
            p.wait("vector", st, oh_done)
            ins = None
            for dt in range(8):
                dcg = dh * 8 + dt
                ins = p.do("vector", "tensor_tensor", bf["x_res"][:, dcg, n0:n0 + G], pband[dt][:, :],
                           bf["x_res"][:, dcg, n0:n0 + G], ALU.add)
            ev = p.inc("vector", ins, sv)
            aq.consumed(sv, 0)
            app.consumed(sv, 0)
            app.consumed(sv, 1)



import contextlib


def run_phases(nc, sems, phases):
    items = [(k, c) for k, c in sems.items() if isinstance(c, Ctr)]
    for emitfn in phases:
        snap = {k: c.v for k, c in items}
        final = None
        with nc.Block() as blk:
            for eng in ENGS:
                for k, c in items:
                    c.v = snap[k]
                def body(e, eng=eng, fn=emitfn):
                    fn(P(nc, eng, sems))
                getattr(blk, eng)(body)
                f = {k: c.v for k, c in items}
                if final is None:
                    final = f
                else:
                    assert f == final, f"lockstep divergence in pass {eng}: " + str(
                        {k: (final[k], f[k]) for k in f if f[k] != final[k]})


def alloc_sems(nc, stack):
    names = ["v", "t", "a", "g", "o", "wload", "tabload", "kload", "vload", "dload",
             "kfree", "vfree", "dfree", "xload", "gload", "qtload", "att_q", "vq_free",
             "wga_l", "wga_f", "wua_l", "wua_f", "wda_l", "wda_f", "wbk"]
    sems = {n: Ctr(stack.enter_context(nc.semaphore(n))) for n in names}
    for pair in ["pp_free", "qq_free", "att_pp"]:
        for i in range(2):
            sems[f"{pair}{i}"] = Ctr(stack.enter_context(nc.semaphore(f"{pair}{i}")))
    sems["pp_free"] = [sems["pp_free0"], sems["pp_free1"]]
    sems["qq_free"] = [sems["qq_free0"], sems["qq_free1"]]
    sems["att_pp"] = [sems["att_pp0"], sems["att_pp1"]]
    return sems


def alloc_core_bufs(nc, stack):
    bf = {}
    def S_(name, shape, dt):
        return stack.enter_context(nc.sbuf_tensor(name, shape, dt))
    def PS(name, shape):
        return stack.enter_context(nc.psum_tensor(name, shape, F32))
    bf["x_res"] = S_("x_res", [128, DC, TOK], F32)
    bf["ln_tmp"] = S_("ln_tmp", [128, 2, G], F32)
    bf["rows"] = S_("rows", [1, 5 * G + 8], F32)
    bf["srows"] = S_("srows", [128, 4, 8], F32)
    bf["ones_bf"] = S_("ones_bf", [128, 1], BF16)
    bf["ones_f32"] = S_("ones_f32", [1, 128], F32)
    bf["ps_a"] = PS("ps_a", [128, G])
    bf["ps_b"] = PS("ps_b", [128, G])
    bf["ps_c"] = PS("ps_c", [128, G])
    bf["ps_d"] = PS("ps_d", [128, G])
    bf["ps_dn0"] = [PS(f"ps_dn0_{i}", [128, G]) for i in range(4)]
    bf["ps_dn1"] = [bf["ps_a"], bf["ps_b"], bf["ps_c"], bf["ps_d"]]
    bf["pband"] = [bf["ps_a"], bf["ps_b"], bf["ps_c"], bf["ps_d"]] + bf["ps_dn0"]
    return bf


def alloc_ffn_bufs(nc, stack):
    bf = {}
    def S_(name, shape, dt):
        return stack.enter_context(nc.sbuf_tensor(name, shape, dt))
    bf["x_ln"] = S_("x_ln", [128, DC, G], BF16)
    bf["xsq"] = S_("xsq", [128, DC, G], BF16)
    bf["grnd"] = S_("grnd", [128, 2, HIE, G], BF16)
    bf["gsh"] = bf["grnd"].rearrange("p a b t -> p (a b) t")
    bf["comb"] = S_("comb", [128, NE, G], BF16)
    bf["comb_f32"] = S_("comb_f32", [128, 4, 8], F32)
    bf["scores_tmp"] = S_("scores_tmp", [128, 4, 8], F32)
    bf["comb8"] = S_("comb8", [8, G], BF16)
    bf["silu_sb"] = S_("silu_sb", [128, 4, G], BF16)
    bf["gtmp"] = S_("gtmp", [128, 4, G], BF16)
    bf["wga"] = [S_(f"wga{i}", [128, 2, DC, 128], BF16) for i in range(2)]
    bf["wua"] = [S_(f"wua{i}", [128, 2, DC, 128], BF16) for i in range(2)]
    bf["wda"] = [S_(f"wda{i}", [128, HIE, G], BF16) for i in range(2)]
    return bf


def alloc_qkv_bufs(nc, stack):
    bf = {}
    def S_(name, shape, dt):
        return stack.enter_context(nc.sbuf_tensor(name, shape, dt))
    bf["x_ln2"] = S_("x_ln2", [128, NGRP, DC, G], BF16)
    bf["xsq"] = S_("xsq_q", [128, DC, G], BF16)
    bf["wfull"] = S_("wfull", [128, DC, D // 2], BF16)
    bf["ctab"] = S_("ctab", [128, DC, G], BF16)
    bf["stab"] = S_("stab", [128, DC, G], BF16)
    bf["qk_t"] = S_("qk_t", [128, 2, G], BF16)
    bf["qk_r"] = S_("qk_r", [128, 2, G], BF16)
    bf["qk_a"] = S_("qk_a", [128, 2, G], BF16)
    bf["qk_out"] = S_("qk_out", [128, 8, G], BF16)
    bf["v_out"] = bf["qk_out"].rearrange("p a t -> p (a t)")[:, 0:D]
    return bf


def alloc_attn_bufs(nc, stack):
    bf = {}
    def S_(name, shape, dt):
        return stack.enter_context(nc.sbuf_tensor(name, shape, dt))
    bf["qt_sb"] = S_("qt_sb", [128, DC, TOK], BF16)
    bf["att_sb"] = S_("att_sb", [128, S // 128, G], BF16)
    bf["kbufs"] = [S_(f"kbuf{i}", [128, DC, 512], BF16) for i in range(2)]
    bf["vbufs"] = [S_(f"vbuf{i}", [128, 4, 1024], BF16) for i in range(2)]
    bf["dbufs"] = [S_(f"dbuf{i}", [128, 4, G], BF16) for i in range(2)]
    return bf


def emit_const_init(p, bf, sm):
    p.do("vector", "memset", bf["ones_bf"][:, :], 1.0)
    p.do("vector", "memset", bf["rows"][0:1, 5 * G:5 * G + 8], EPS)
    ins = p.do("vector", "memset", bf["ones_f32"][:, :], 1.0)
    return p.inc("vector", ins, sm["v"])


def ffn_weight_aps(nc, kind):
    w = {}
    w["w_gate"] = nc.dram_tensor("w_gate", [NE, HIE, 128, DC, 128], F32, kind=kind).ap()
    w["w_up"] = nc.dram_tensor("w_up", [NE, HIE, 128, DC, 128], F32, kind=kind).ap()
    w["w_down"] = nc.dram_tensor("w_down", [NE, 4, 128, HIE, 512], F32, kind=kind).ap()
    w["sw_gate"] = nc.dram_tensor("sw_gate", [HIS, 128, DC, 128], F32, kind=kind).ap()
    w["sw_up"] = nc.dram_tensor("sw_up", [HIS, 128, DC, 128], F32, kind=kind).ap()
    w["sw_down"] = nc.dram_tensor("sw_down", [2, 4, 128, HIE, 512], F32, kind=kind).ap()
    w["gate_w"] = nc.dram_tensor("gate_w", [D, NE], F32, kind=kind).ap()
    return w


def ffn_cache_aps(nc):
    c = {}
    c["w_gate"] = nc.dram_tensor("wc_gate", [NE, 6, 128, 2, DC, 128], BF16, kind="Internal").ap()
    c["w_up"] = nc.dram_tensor("wc_up", [NE, 6, 128, 2, DC, 128], BF16, kind="Internal").ap()
    c["w_down"] = nc.dram_tensor("wc_down", [NE, 4, 128, HIE, 512], BF16, kind="Internal").ap()
    c["sw_gate"] = nc.dram_tensor("wc_sgate", [HIS // 2, 128, 2, DC, 128], BF16, kind="Internal").ap()
    c["sw_up"] = nc.dram_tensor("wc_sup", [HIS // 2, 128, 2, DC, 128], BF16, kind="Internal").ap()
    c["sw_down"] = nc.dram_tensor("wc_sdown", [2, 4, 128, HIE, 512], BF16, kind="Internal").ap()
    return c


def stage_weights(wmap):
    o = {}
    wg = np.ascontiguousarray(wmap["w_gate"]).reshape(NE, DC, 128, HIE, 128)
    o["w_gate"] = np.ascontiguousarray(wg.transpose(0, 3, 2, 1, 4))
    wu = np.ascontiguousarray(wmap["w_up"]).reshape(NE, DC, 128, HIE, 128)
    o["w_up"] = np.ascontiguousarray(wu.transpose(0, 3, 2, 1, 4))
    wd = np.ascontiguousarray(wmap["w_down"]).reshape(NE, HIE, 128, 4, 512)
    o["w_down"] = np.ascontiguousarray(wd.transpose(0, 3, 2, 1, 4))
    sg_ = np.ascontiguousarray(wmap["sw_gate"]).reshape(DC, 128, HIS, 128)
    o["sw_gate"] = np.ascontiguousarray(sg_.transpose(2, 1, 0, 3))
    su = np.ascontiguousarray(wmap["sw_up"]).reshape(DC, 128, HIS, 128)
    o["sw_up"] = np.ascontiguousarray(su.transpose(2, 1, 0, 3))
    sd = np.ascontiguousarray(wmap["sw_down"]).reshape(2, HIE, 128, 4, 512)
    o["sw_down"] = np.ascontiguousarray(sd.transpose(0, 3, 2, 1, 4))
    o["gate_w"] = wmap["gate_w"]
    return o


def stage_wqkv(W):
    wq = np.ascontiguousarray(W).reshape(DC, 128, 4, 512)
    return np.ascontiguousarray(wq.transpose(2, 1, 0, 3))


def emit_barrier(p, sems):
    for k, c in sems.items():
        if isinstance(c, Ctr):
            p.wait(p.active, c)


def emit_ffn_preamble(p, bf, sm, w):
    p.dma("sync", bf["ident"][:, :], bf["ident_dram"][:, :], sm["xload"])
    p.dma("sync", bf["onehot"][:, :], bf["onehot_dram"][:, :], sm["xload"])
    p.dma("gpsimd", bf["gw"][:, :, :], w["gate_w"].rearrange("(dc p) e -> p dc e", p=128), sm["gload"])
    for e in ENGS:
        p.wait(e, sm["xload"])
        p.wait(e, sm["gload"])


def alloc_gating_consts(nc, stack, bf):
    bf["ident"] = stack.enter_context(nc.sbuf_tensor("ident_sb", [128, 128], F32))
    bf["onehot"] = stack.enter_context(nc.sbuf_tensor("onehot_sb", [8, NE * 128], BF16))
    bf["gw"] = stack.enter_context(nc.sbuf_tensor("gw_sb", [128, DC, NE], BF16))


def _mk_nc():
    return bass.Bass("TRN2", target_bir_lowering=False, detect_race_conditions=False)


def build_l1(ffn_only=False):
    nc = _mk_nc()
    stack = contextlib.ExitStack()
    x0 = nc.dram_tensor("x0", [D, TOK], F32, kind="ExternalInput").ap()
    w = ffn_weight_aps(nc, "ExternalInput")
    w["W_Q"] = nc.dram_tensor("W_Q", [D, D], F32, kind="ExternalInput").ap()
    w["W_K"] = nc.dram_tensor("W_K", [D, D], F32, kind="ExternalInput").ap()
    w["W_V"] = nc.dram_tensor("W_V", [D, D], F32, kind="ExternalInput").ap()
    ident = nc.dram_tensor("ident", [128, 128], F32, kind="ExternalInput").ap()
    onehot = nc.dram_tensor("onehot", [8, NE * 128], BF16, kind="ExternalInput").ap()
    cs_q0 = nc.dram_tensor("cq", [2, D, TOK], BF16, kind="ExternalInput").ap()
    cs_k0 = nc.dram_tensor("ck", [2, D, TOK], BF16, kind="ExternalInput").ap()
    x1_out = nc.dram_tensor("x1", [D, TOK], F32, kind="ExternalOutput").ap()
    qt_out = nc.dram_tensor("qt", [D, TOK], BF16, kind="ExternalOutput").ap()
    kt_out = nc.dram_tensor("kt", [D, TOK], BF16, kind="ExternalOutput").ap()
    v_out = nc.dram_tensor("v", [TOK, D], BF16, kind="ExternalOutput").ap()

    bf = alloc_core_bufs(nc, stack)
    bf["ident_dram"] = ident
    bf["onehot_dram"] = onehot
    bf["cs_q"] = (cs_q0[0], cs_q0[1])
    bf["cs_k"] = (cs_k0[0], cs_k0[1])
    sems = alloc_sems(nc, stack)

    def phase_load(p):
        emit_const_init(p, bf, sems)
        p.dma("sync", bf["x_res"][:, :, :], x0.rearrange("(dc p) t -> p dc t", p=128), sems["xload"])
        emit_ffn_preamble(p, bf, sems, w)
        p.wait("tensor", sems["v"])

    def phase_ffn(p):
        bf["silu_guard"] = Slot(4)
        bf["gtmp_guard"] = Slot(4)
        bf["xsq_guard"] = Slot(1)
        bf["wgu_db"] = DBuf(p, list(zip(bf["wga"], bf["wua"])), sems["wga_l"], sems["t"])
        bf["wda_db"] = DBuf(p, bf["wda"], sems["wda_l"], sems["t"])
        bf["pp_slot"] = Slot(4)
        bf["qq_slot"] = Slot(2)
        bf["gu_pairs"] = [(bf["ps_a"], bf["ps_b"]), (bf["ps_c"], bf["ps_d"]),
                          (bf["ps_dn0"][0], bf["ps_dn0"][1]), (bf["ps_dn0"][2], bf["ps_dn0"][3])]
        for grp in range(NGRP):
            emit_ffn_group(p, bf, sems, w, grp)

    with contextlib.ExitStack() as fstack:
        alloc_gating_consts(nc, fstack, bf)
        for k, v_ in alloc_ffn_bufs(nc, fstack).items():
            bf[k] = v_
        run_phases(nc, sems, [phase_load, phase_ffn])

    if not ffn_only:
        def phase_qkv(p):
            emit_barrier(p, sems)
            emit_qkv(p, bf, sems, w, {"qt": qt_out, "kt": kt_out, "v": v_out})
        with contextlib.ExitStack() as qstack:
            for k, v_ in alloc_qkv_bufs(nc, qstack).items():
                bf[k] = v_
            run_phases(nc, sems, [phase_qkv])

    def phase_store(p):
        emit_barrier(p, sems)
        p.dma("sync", x1_out.rearrange("(dc p) t -> p dc t", p=128), bf["x_res"][:, :, :], sems["o"])
        p.wait("sync", sems["o"])

    run_phases(nc, sems, [phase_store])
    stack.close()
    return nc


def build_l2(attn_only=False):
    nc = _mk_nc()
    stack = contextlib.ExitStack()
    x1 = nc.dram_tensor("x1", [D, TOK], F32, kind="ExternalInput").ap()
    qt_in = nc.dram_tensor("qt", [D, TOK], BF16, kind="ExternalInput").ap()
    kt_full = nc.dram_tensor("kt_full", [D, S], BF16, kind="ExternalInput").ap()
    v_full = nc.dram_tensor("v_full", [S, D], BF16, kind="ExternalInput").ap()
    dtab = nc.dram_tensor("dtab", [S, TOK], BF16, kind="ExternalInput").ap()
    w = ffn_weight_aps(nc, "ExternalInput")
    ident = nc.dram_tensor("ident", [128, 128], F32, kind="ExternalInput").ap()
    onehot = nc.dram_tensor("onehot", [8, NE * 128], BF16, kind="ExternalInput").ap()
    xout = nc.dram_tensor("xout", [D, TOK], F32, kind="ExternalOutput").ap()

    bf = alloc_core_bufs(nc, stack)
    bf["ident_dram"] = ident
    bf["onehot_dram"] = onehot
    sems = alloc_sems(nc, stack)

    def phase_load(p):
        emit_const_init(p, bf, sems)
        p.dma("sync", bf["x_res"][:, :, :], x1.rearrange("(dc p) t -> p dc t", p=128), sems["xload"])
        p.dma("sync", bf["qt_sb"][:, :, :], qt_in.rearrange("(dc p) t -> p dc t", p=128), sems["xload"])
        for e in ENGS:
            p.wait(e, sems["xload"])
        p.wait("tensor", sems["v"])

    def phase_attn(p):
        emit_attention(p, bf, sems, {"kt_full": kt_full, "v_full": v_full, "dtab": dtab})

    with contextlib.ExitStack() as astack:
        for k, v_ in alloc_attn_bufs(nc, astack).items():
            bf[k] = v_
        run_phases(nc, sems, [phase_load, phase_attn])

    if not attn_only:
        def phase_ffn(p):
            emit_barrier(p, sems)
            emit_ffn_preamble(p, bf, sems, w)
            bf["silu_guard"] = Slot(4)
            bf["gtmp_guard"] = Slot(4)
            bf["xsq_guard"] = Slot(1)
            bf["wgu_db"] = DBuf(p, list(zip(bf["wga"], bf["wua"])), sems["wga_l"], sems["t"])
            bf["wda_db"] = DBuf(p, bf["wda"], sems["wda_l"], sems["t"])
            bf["pp_slot"] = Slot(4)
            bf["qq_slot"] = Slot(2)
            bf["gu_pairs"] = [(bf["ps_a"], bf["ps_b"]), (bf["ps_c"], bf["ps_d"]),
                              (bf["ps_dn0"][0], bf["ps_dn0"][1]), (bf["ps_dn0"][2], bf["ps_dn0"][3])]
            for grp in range(NGRP):
                pre = (sems["v"], sems["v"].v) if grp == 0 else None
                emit_ffn_group(p, bf, sems, w, grp, pre_t=pre)
        with contextlib.ExitStack() as fstack:
            alloc_gating_consts(nc, fstack, bf)
            for k, v_ in alloc_ffn_bufs(nc, fstack).items():
                bf[k] = v_
            run_phases(nc, sems, [phase_ffn])

    def phase_store(p):
        emit_barrier(p, sems)
        p.dma("sync", xout.rearrange("(dc p) t -> p dc t", p=128), bf["x_res"][:, :, :], sems["o"])
        p.wait("sync", sems["o"])

    run_phases(nc, sems, [phase_store])
    stack.close()
    return nc



def _consts():
    ident = np.eye(128, dtype=np.float32)
    onehot = np.zeros((8, NE * 128), dtype=ml_dtypes.bfloat16)
    for e in range(NE):
        onehot[e, e * 128:(e + 1) * 128] = 1.0
    return ident, onehot


def kernel_unfused(L0_attn_X, L0_attn_W_Q, L0_attn_W_K, L0_attn_W_V, L0_ffn_gate_w,
           L0_ffn_w_gate, L0_ffn_w_up, L0_ffn_w_down, L0_ffn_sw_gate,
           L0_ffn_sw_up, L0_ffn_sw_down):
    X = np.asarray(L0_attn_X)
    ident, onehot = _consts()
    wmap = {
        "w_gate": np.ascontiguousarray(L0_ffn_w_gate),
        "w_up": np.ascontiguousarray(L0_ffn_w_up),
        "w_down": np.ascontiguousarray(L0_ffn_w_down),
        "sw_gate": np.ascontiguousarray(L0_ffn_sw_gate),
        "sw_up": np.ascontiguousarray(L0_ffn_sw_up),
        "sw_down": np.ascontiguousarray(L0_ffn_sw_down),
        "gate_w": np.ascontiguousarray(L0_ffn_gate_w),
    }

    nc1 = build_l1()
    in1 = []
    for c in range(8):
        b, r0 = c // 4, (c % 4) * TOK
        rows = np.arange(r0, r0 + TOK)
        cq = np.stack(host_xpos_tables(rows, downscale=False))
        ck = np.stack(host_xpos_tables(rows, downscale=True))
        m = dict(wmap)
        m.update({
            "x0": fm(X[b, r0:r0 + TOK, :]),
            "W_Q": np.ascontiguousarray(L0_attn_W_Q),
            "W_K": np.ascontiguousarray(L0_attn_W_K),
            "W_V": np.ascontiguousarray(L0_attn_W_V),
            "ident": ident, "onehot": onehot, "cq": cq, "ck": ck,
        })
        in1.append(m)
    r1 = run_bass_kernel_spmd(nc1, in1, core_ids=list(range(8)))

    ktb = [np.concatenate([r1.results[b * 4 + i]["kt"] for i in range(4)], axis=1) for b in range(B)]
    vtb = [np.concatenate([r1.results[b * 4 + i]["v"] for i in range(4)], axis=0) for b in range(B)]

    nc2 = build_l2()
    in2 = []
    for c in range(8):
        b, r0 = c // 4, (c % 4) * TOK
        rows = np.arange(r0, r0 + TOK)
        m = dict(wmap)
        m.update({
            "x1": r1.results[c]["x1"],
            "qt": r1.results[c]["qt"],
            "kt_full": ktb[b],
            "v_full": vtb[b],
            "dtab": host_decay(rows, S).astype(ml_dtypes.bfloat16),
            "ident": ident, "onehot": onehot,
        })
        in2.append(m)
    r2 = run_bass_kernel_spmd(nc2, in2, core_ids=list(range(8)))

    out = np.empty((B, S, D), dtype=np.float32)
    for c in range(8):
        b, r0 = c // 4, (c % 4) * TOK
        out[b, r0:r0 + TOK, :] = r2.results[c]["xout"].T
    return out



RA = 2048


def emit_qkv2(p, bf, sm, w, tabs_c, tabs_k, n_groups, do_q, qt_dst, kv_store, tok_base):
    sv, st, sa, sg, so = sm["v"], sm["t"], sm["a"], sm["g"], sm["o"]
    wload, tl = sm["wload"], sm["tabload"]

    bf["xsq_guard"] = Slot(1)
    for grp in range(n_groups):
        emit_ln(p, bf, sm, grp, dst=bf["x_ln2"][:, grp])

    qk_g = Slot(4)
    ob_g = Slot(1)
    qpp = Slot(4)
    vps = Slot(2)
    tab_g = Slot(1)
    wdb = DBuf(p, bf["wq4"], wload, sm["t"], feng="gpsimd")
    shuffle_mask = [i ^ 1 for i in range(32)]
    wnames = (("W_Q", "W_K", "W_V") if do_q else ("W_K", "W_V"))
    banks4 = [bf["ps_a"], bf["ps_b"], bf["ps_c"], bf["ps_d"]]

    fills = {}
    order = [(wn, q) for wn in wnames for q in range(4)]
    fills[order[0]] = wdb.fill([(lambda sl: sl[:, :, :], w[order[0][0]][0])])
    for oi, (wname, q) in enumerate(order):
        if True:
            fw = fills[(wname, q)]
            if oi + 1 < len(order):
                wn2, q2 = order[oi + 1]
                fills[(wn2, q2)] = wdb.fill([(lambda sl, wn2=wn2, q2=q2: sl[:, :, :], w[wn2][q2])])
            for grp in range(n_groups):
                t0 = tok_base + grp * G
                if wname in ("W_Q", "W_K"):
                    tabs = (tabs_c if wname == "W_Q" else tabs_k)
                    tab_g.producer_wait(p, "sync", sv, 0)
                    p.wait("sync", sv)
                    p.dma("sync", bf["ctab"][:, :, :], tabs[0].rearrange("(dc p) t -> p dc t", p=128)[:, q * 4:(q + 1) * 4, t0:t0 + G], tl)
                    p.dma("sync", bf["stab"][:, :, :], tabs[1].rearrange("(dc p) t -> p dc t", p=128)[:, q * 4:(q + 1) * 4, t0:t0 + G], tl)
                    wsl = wdb.use(fw)
                    ins = None
                    for dto in range(4):
                        par = dto
                        pq = banks4[par]
                        if q == 0 and grp == 0 and dto <= 3:
                            p.wait("tensor", sv)
                        qpp.producer_wait(p, "tensor", sv, par)
                        for dc in range(DC):
                            ins = p.do("tensor", "matmul", pq[:, :], wsl[:, dc, dto * 128:(dto + 1) * 128],
                                       bf["x_ln2"][:, grp, dc, :], start=(dc == 0), stop=(dc == DC - 1))
                        mm_done = p.inc("tensor", ins, st)
                        p.wait("vector", st, mm_done)
                        if dto == 0:
                            p.wait("vector", tl)
                            ob_g.producer_wait(p, "vector", so, 0)
                        qk_g.producer_wait(p, "vector", sv, par)
                        p.do("vector", "tensor_copy", bf["qk_t"][:, par, :], pq[:, :])
                        p.do("vector", "drain")
                        ins = p.do("vector", "stream_shuffle", bf["qk_r"][:, par, :], bf["qk_t"][:, par, :], shuffle_mask)
                        p.inc("vector", ins, sv)
                        qpp.consumed(sv, par)
                        ins = p.do("vector", "tensor_tensor", bf["qk_a"][:, par, :], bf["qk_t"][:, par, :], bf["ctab"][:, dto, :], ALU.mult)
                        p.do("vector", "drain")
                        p.do("vector", "tensor_tensor", bf["qk_r"][:, par, :], bf["qk_r"][:, par, :], bf["stab"][:, dto, :], ALU.mult)
                        p.do("vector", "drain")
                        ins = p.do("vector", "tensor_tensor", bf["qk_out"][:, dto, :], bf["qk_r"][:, par, :], bf["qk_a"][:, par, :], ALU.add)
                        a_done = p.inc("vector", ins, sv)
                        if dto == 3:
                            tab_g.consumed(sv, 0)
                        qk_g.consumed(sv, par)
                    p.wait("sync", sv)
                    if wname == "W_Q":
                        p.dma("sync", qt_dst.rearrange("(dc p) t -> p dc t", p=128)[:, q * 4:(q + 1) * 4, grp * G:grp * G + G],
                              bf["qk_out"][:, 0:4, :], so)
                        ob_g.consumed(so, 0)
                    else:
                        ksem = kv_store("k", grp, q, bf["qk_out"][:, 0:4, :], None)
                        ob_g.consumed(ksem if ksem is not None else so, 0)
                else:
                    wsl = wdb.use(fw)
                    for tb in range(4):
                        par = tb % 2
                        if q == 0 and grp == 0 and tb <= 1:
                            p.wait("tensor", sv)
                        vps.producer_wait(p, "tensor", sv, par)
                        ins = None
                        for dc in range(DC):
                            ins = p.do("tensor", "matmul", bf["ps_dn0"][par][:, :],
                                       bf["x_ln2"][:, grp, dc, tb * 128:(tb + 1) * 128],
                                       wsl[:, dc, :],
                                       start=(dc == 0), stop=(dc == DC - 1))
                        v_done = p.inc("tensor", ins, st)
                        p.wait("vector", st, v_done)
                        ob_g.producer_wait(p, "vector", so, 0)
                        ins = p.do("vector", "tensor_copy", bf["v_out"][:, 0:512], bf["ps_dn0"][par][:, :])
                        ve = p.inc("vector", ins, sv)
                        vps.consumed(sv, par)
                        p.wait("sync", sv, ve)
                        vsem = kv_store("v", grp, q, bf["v_out"][:, 0:512], tb)
                        ob_g.consumed(vsem if vsem is not None else so, 0)
            wdb.done()
    p.wait("sync", so)


def run_phases_roled(nc, sems, regs, phases):
    items = [(k, c) for k, c in sems.items() if isinstance(c, Ctr)]
    state = {k: c.v for k, c in items}
    rstate = None
    for kind, fn in phases:
        if kind == "common":
            snap = dict(state)
            final = None
            with nc.Block() as blk:
                for eng in ENGS:
                    for k, c in items:
                        c.v = snap[k]
                    def body(e, eng=eng, fn=fn):
                        fn(P(nc, eng, sems))
                    getattr(blk, eng)(body)
                    f = {k: c.v for k, c in items}
                    assert final is None or f == final, f"lockstep div {eng}"
                    final = f
            state = final
            assert rstate is None, "common phase after fork unsupported"
        else:
            if rstate is None:
                rstate = {0: dict(state), 1: dict(state)}
            finals = {0: None, 1: None}
            with nc.Block() as blk:
                for eng in ENGS:
                    def body(e, eng=eng, fn=fn):
                        for r in (0, 1):
                            for k, c in items:
                                c.v = rstate[r][k]
                            geng = getattr(nc, eng)
                            with geng.If_eq(regs[eng]["role"], r):
                                fn(P(nc, eng, sems), r)
                            geng.end_ifs()
                            f = {k: c.v for k, c in items}
                            assert finals[r] is None or f == finals[r], f"lockstep div {eng} role {r}: " + str(
                                {k: (finals[r][k], f[k]) for k in f if f[k] != finals[r][k]})
                            finals[r] = f
                    getattr(blk, eng)(body)
            rstate = finals


def build_fused():
    nc = _mk_nc()
    stack = contextlib.ExitStack()
    x0 = nc.dram_tensor("x0", [D, TOK], F32, kind="ExternalInput").ap()
    x0f = nc.dram_tensor("x0f", [D, TOK], F32, kind="ExternalInput").ap()
    w = ffn_weight_aps(nc, "ExternalInput")
    w["W_Q"] = nc.dram_tensor("W_Q", [4, 128, DC, 512], F32, kind="ExternalInput").ap()
    w["W_K"] = nc.dram_tensor("W_K", [4, 128, DC, 512], F32, kind="ExternalInput").ap()
    w["W_V"] = nc.dram_tensor("W_V", [4, 128, DC, 512], F32, kind="ExternalInput").ap()
    ident = nc.dram_tensor("ident", [128, 128], F32, kind="ExternalInput").ap()
    onehot = nc.dram_tensor("onehot", [8, NE * 128], BF16, kind="ExternalInput").ap()
    cq = nc.dram_tensor("cq", [2, D, TOK], BF16, kind="ExternalInput").ap()
    ckv = nc.dram_tensor("ckv", [2, D, 2 * TOK], BF16, kind="ExternalInput").ap()
    dtab = nc.dram_tensor("dtab", [S, TOK], BF16, kind="ExternalInput").ap()
    meta = nc.dram_tensor("meta", [1, 4], mybir.dt.int32, kind="ExternalInput").ap()
    xout = nc.dram_tensor("xout", [D, TOK], F32, kind="ExternalOutput").ap()
    qt_d = nc.dram_tensor("qt_d", [D, TOK], BF16, kind="Internal").ap()
    wcache = ffn_cache_aps(nc)
    kt_sh = nc.dram_tensor("kt_sh", [D, S], BF16, kind="Internal", addr_space="Shared").ap()
    v_sh = nc.dram_tensor("v_sh", [S, D], BF16, kind="Internal", addr_space="Shared").ap()
    flag_sh = nc.dram_tensor("flag_sh", [2, 1, 4], mybir.dt.int32, kind="Internal", addr_space="Shared").ap()

    bf = alloc_core_bufs(nc, stack)
    bf["ident_dram"] = ident
    bf["onehot_dram"] = onehot
    bf["meta_sb"] = stack.enter_context(nc.sbuf_tensor("meta_sb", [1, 4], mybir.dt.int32))
    sems = alloc_sems(nc, stack)
    for n in ("kvw", "flg", "pol"):
        sems[n] = Ctr(stack.enter_context(nc.semaphore(n)))
    regs = {}
    for eng in ENGS:
        regs[eng] = {"role": getattr(nc, eng).alloc_register(name=f"role_{eng}")}
    regs["sync"]["half"] = nc.sync.alloc_register(name="half_sync")
    regs["sync"]["nonce"] = nc.sync.alloc_register(name="nonce_sync")
    regs["sync"]["pollcond"] = nc.sync.alloc_register(name="pollcond")
    regs["sync"]["pollfv"] = nc.sync.alloc_register(name="pollfv")
    bf["regs"] = regs
    ind = {"x0": x0, "x0f": x0f, "cq": cq, "ckv": ckv, "dtab": dtab, "qt_d": qt_d,
           "kt_sh": kt_sh, "v_sh": v_sh, "flag_sh": flag_sh, "xout": xout}

    def kv_store_factory(p, role, wv_half_holder):
        half_reg = regs["sync"]["half"]

        def kv_store(kind, grp, q, buf, tb):
            for h in (0, 1):
                gbase = (h * TOK) if (role == 0 or bf["cur_pass"] == "foreign") else (RA + h * TOK)
                if p.active == "sync":
                    e = nc.sync
                    with e.If_eq(half_reg, h):
                        if kind == "k":
                            dst = kt_sh.rearrange("(dc p) m -> p dc m", p=128)[:, q * 4:(q + 1) * 4,
                                                                              gbase + grp * G:gbase + grp * G + G]
                        else:
                            r0 = gbase + grp * G + tb * 128
                            dst = v_sh[r0:r0 + 128, q * 512:(q + 1) * 512]
                        e.dma_start(out=dst, in_=buf).then_inc(sems["kvw"].h, 16)
                    e.end_ifs()
            sems["kvw"].v += 16
            return sems["kvw"]
        return kv_store

    def p0(p):
        emit_const_init(p, bf, sems)
        p.dma("sync", bf["meta_sb"][:, :], meta[:, :], sems["xload"])
        for eng in ENGS:
            p.wait(eng, sems["xload"])
            if p.active == eng:
                ge = getattr(nc, eng)
                ge.reg_load(regs[eng]["role"], bf["meta_sb"][0:1, 0:1])
                if eng == "sync":
                    ge.reg_load(regs[eng]["half"], bf["meta_sb"][0:1, 1:2])
                    ge.reg_load(regs[eng]["nonce"], bf["meta_sb"][0:1, 2:3])

    def mk_ffn_phase(src_ap, tag):
        def f(p, role):
            if src_ap is not None:
                if role == 0 and tag == "foreign":
                    return
                ap = x0f if tag == "foreign" else x0
                p.wait("sync", sems["v"])
                p.wait("sync", sems["t"])
                p.dma("sync", bf["x_res"][:, :, :], ap.rearrange("(dc p) t -> p dc t", p=128), sems["xload"])
                for eng in ("vector", "tensor", "scalar", "gpsimd"):
                    p.wait(eng, sems["xload"])
            emit_ffn_preamble(p, bf, sems, w)
            bf["wgu_db"] = DBuf(p, list(zip(bf["wga"], bf["wua"])), sems["wga_l"], sems["t"], wbsem=sems["wbk"])
            bf["wda_db"] = DBuf(p, bf["wda"], sems["wda_l"], sems["t"], wbsem=sems["wbk"])
            bf["pp_slot"] = Slot(4)
            bf["qq_slot"] = Slot(2)
            bf["gu_pairs"] = [(bf["ps_a"], bf["ps_b"]), (bf["ps_c"], bf["ps_d"]),
                              (bf["ps_dn0"][0], bf["ps_dn0"][1]), (bf["ps_dn0"][2], bf["ps_dn0"][3])]
            bf["silu_guard"] = Slot(4)
            bf["gtmp_guard"] = Slot(4)
            bf["xsq_guard"] = Slot(1)
            first = (tag == "foreign") if role == 1 else (tag == "own")
            for grp in range(NGRP):
                pre = (sems["v"], sems["v"].v) if grp == 0 else None
                if first and grp == 0:
                    wcfg = ("write", wcache)
                else:
                    wcfg = ("read", wcache)
                    p.wait("gpsimd", sems["wbk"], sems["wbk"].v)
                emit_ffn_group(p, bf, sems, w, grp, pre_t=pre, wcfg=wcfg)
        return f

    def mk_qkv_phase(tag):
        def f(p, role):
            if role == 0 and tag == "foreign":
                return
            bf["cur_pass"] = tag
            kv_store = kv_store_factory(p, role, None)
            if tag == "foreign":
                emit_qkv2(p, bf, sems, w, None, (ckv[0][:, 0:TOK], ckv[1][:, 0:TOK]), NGRP,
                          do_q=False, qt_dst=None, kv_store=kv_store, tok_base=0)
            else:
                emit_qkv2(p, bf, sems, w, (cq[0], cq[1]), (ckv[0][:, TOK:2 * TOK], ckv[1][:, TOK:2 * TOK]),
                          NGRP, do_q=True, qt_dst=qt_d, kv_store=kv_store, tok_base=0)
        return f

    def p_flags(p, role):
        half_reg = regs["sync"]["half"]
        nonce_reg = regs["sync"]["nonce"]
        p.wait("sync", sems["kvw"])
        for h in (0, 1):
            if p.active == "sync":
                with nc.sync.If_eq(half_reg, h):
                    nc.sync.dma_start(out=flag_sh[h, :, 0:1], in_=bf["meta_sb"][0:1, 2:3]).then_inc(sems["flg"].h, 16)
                nc.sync.end_ifs()
        sems["flg"].v += 16
        p.wait("sync", sems["flg"])
        if p.active == "sync":
            e = nc.sync
            cond = regs["sync"]["pollcond"]
            fv = regs["sync"]["pollfv"]
            for ph in (0, 1):
                e.reg_mov(cond, 1)
                with e.While(cond):
                    e.sem_clear(sems["pol"].h)
                    e.dma_start(out=bf["meta_sb"][0:1, 3:4], in_=flag_sh[ph, :, 0:1]).then_inc(sems["pol"].h, 16)
                    e.wait_ge(sems["pol"].h, 16)
                    e.reg_load(fv, bf["meta_sb"][0:1, 3:4])
                    e.reg_alu(cond, fv, nonce_reg, ALU.is_equal)
                    e.reg_alu(cond, cond, 1, ALU.bitwise_xor)
                e.sem_clear(sems["pol"].h)
        if p.active == "sync":
            nc.sync.sem_inc(sems["g"].h, 1)
        sems["g"].v += 1
        for eng in ("tensor", "gpsimd", "vector"):
            p.wait(eng, sems["g"])

    def p_attn(p, role):
        p.dma("sync", bf["qt_sb"][:, :, :], qt_d.rearrange("(dc p) t -> p dc t", p=128), sems["qtload"])
        for e2 in ("tensor", "vector", "gpsimd"):
            p.wait(e2, sems["qtload"])
        mx = RA if role == 0 else 4096
        emit_attention(p, bf, sems, {"kt_full": kt_sh[:, 0:mx], "v_full": v_sh[0:mx, :], "dtab": dtab[0:mx, :]}, m_ext=mx)

    def p_store(p, role):
        p.wait("sync", sems["v"])
        p.dma("sync", xout.rearrange("(dc p) t -> p dc t", p=128), bf["x_res"][:, :, :], sems["o"])
        p.wait("sync", sems["o"])

    run_phases_roled(nc, sems, regs, [("common", p0)])

    def roled_group(alloc_fns, fns):
        st2 = contextlib.ExitStack()
        for af in alloc_fns:
            for k2, v2 in af(nc, st2).items():
                bf[k2] = v2
        run2 = [("roled", fn) for fn in fns]
        _run_roled_cont(nc, sems, regs, run2)
        st2.close()

    global _ROLE_STATE
    _ROLE_STATE = None

    def _run_roled_cont(nc2, sems2, regs2, phs):
        global _ROLE_STATE
        items = [(k, c) for k, c in sems2.items() if isinstance(c, Ctr)]
        if _ROLE_STATE is None:
            base = {k: c.v for k, c in items}
            _ROLE_STATE = {0: dict(base), 1: dict(base)}
        for kind, fn in phs:
            finals = {0: None, 1: None}
            with nc2.Block() as blk:
                for eng in ENGS:
                    def body(e, eng=eng, fn=fn):
                        for r in (0, 1):
                            for k, c in items:
                                c.v = _ROLE_STATE[r][k]
                            geng = getattr(nc2, eng)
                            with geng.If_eq(regs2[eng]["role"], r):
                                fn(P(nc2, eng, sems2), r)
                            geng.end_ifs()
                            f = {k: c.v for k, c in items}
                            assert finals[r] is None or f == finals[r], f"lockstep div {eng} role {r}: " + str(
                                {k: (finals[r][k], f[k]) for k in (finals[r] or f) if f.get(k) != (finals[r] or {}).get(k)})
                            finals[r] = f
                    getattr(blk, eng)(body)
            _ROLE_STATE = finals

    roled_group([_mk_ffn_alloc("_1")], [mk_ffn_phase(x0f, "foreign")])
    roled_group([_mk_qkv_alloc("_1")], [mk_qkv_phase("foreign")])
    roled_group([_mk_ffn_alloc("_2")], [mk_ffn_phase(x0, "own")])
    roled_group([_mk_qkv_alloc("_2")], [mk_qkv_phase("own")])
    roled_group([], [p_flags])
    roled_group([alloc_attn_bufs], [p_attn])
    roled_group([_mk_ffn_alloc("_3")], [mk_ffn_phase(None, "ffn2")])
    roled_group([], [p_store])

    stack.close()
    return nc


def _mk_ffn_alloc(tagname):
    def f(nc, stack):
        bf = {}
        def S_(name, shape, dt):
            return stack.enter_context(nc.sbuf_tensor(name + tagname, shape, dt))
        bf["x_ln"] = S_("x_ln", [128, DC, G], BF16)
        bf["xsq"] = S_("xsq", [128, DC, G], BF16)
        bf["grnd"] = S_("grnd", [128, 2, HIE, G], BF16)
        bf["gsh"] = bf["grnd"].rearrange("p a b t -> p (a b) t")
        bf["comb"] = S_("comb", [128, NE, G], BF16)
        bf["comb_f32"] = S_("comb_f32", [128, 4, 8], F32)
        bf["scores_tmp"] = S_("scores_tmp", [128, 4, 8], F32)
        bf["comb8"] = S_("comb8", [8, G], BF16)
        bf["silu_sb"] = S_("silu_sb", [128, 4, G], BF16)
        bf["gtmp"] = S_("gtmp", [128, 4, G], BF16)
        bf["wga"] = [S_(f"wga{i}", [128, 2, DC, 128], BF16) for i in range(2)]
        bf["wua"] = [S_(f"wua{i}", [128, 2, DC, 128], BF16) for i in range(2)]
        bf["wda"] = [S_(f"wda{i}", [128, HIE, G], BF16) for i in range(2)]
        bf["ident"] = S_("ident_sb", [128, 128], F32)
        bf["onehot"] = S_("onehot_sb", [8, NE * 128], BF16)
        bf["gw"] = S_("gw_sb", [128, DC, NE], BF16)
        return bf
    return f


def _mk_qkv_alloc(tagname):
    def f(nc, stack):
        bf = {}
        def S_(name, shape, dt):
            return stack.enter_context(nc.sbuf_tensor(name + tagname, shape, dt))
        bf["x_ln2"] = S_("x_ln2", [128, NGRP, DC, G], BF16)
        bf["xsq"] = S_("xsq_q", [128, DC, G], BF16)
        bf["wq4"] = [S_(f"wq4_{i}", [128, DC, 512], BF16) for i in range(2)]
        bf["ctab"] = S_("ctab", [128, 4, G], BF16)
        bf["stab"] = S_("stab", [128, 4, G], BF16)
        bf["qk_t"] = S_("qk_t", [128, 4, G], BF16)
        bf["qk_r"] = S_("qk_r", [128, 4, G], BF16)
        bf["qk_a"] = S_("qk_a", [128, 4, G], BF16)
        bf["qk_out"] = S_("qk_out", [128, 4, G], BF16)
        bf["v_out"] = bf["qk_out"].rearrange("p a t -> p (a t)")[:, 0:512]
        return bf
    return f


alloc_ffn_bufs2 = _mk_ffn_alloc("_2")
alloc_ffn_bufs3 = _mk_ffn_alloc("_3")
alloc_qkv_bufs2 = _mk_qkv_alloc("_2")


def fused_core_inputs(c, X, wmap, WQ, WK, WV, nonce):
    b, role, half = c // 4, (c % 4) // 2, c % 2
    own0 = (RA if role else 0) + half * TOK
    ident, onehot = _consts()
    own_rows = np.arange(own0, own0 + TOK)
    for_rows = np.arange(half * TOK, half * TOK + TOK)
    cq = np.stack(host_xpos_tables(own_rows, False))
    ck_own = host_xpos_tables(own_rows, True)
    if role:
        ck_for = host_xpos_tables(for_rows, True)
    else:
        z = np.zeros((D, TOK), dtype=ml_dtypes.bfloat16)
        ck_for = (z, z)
    ckv = np.stack([np.concatenate([ck_for[0], ck_own[0]], axis=1),
                    np.concatenate([ck_for[1], ck_own[1]], axis=1)])
    m = dict(wmap)
    m.update({
        "x0": fm(X[b, own0:own0 + TOK, :]),
        "x0f": fm(X[b, half * TOK:half * TOK + TOK, :]) if role else np.zeros((D, TOK), np.float32),
        "W_Q": WQ, "W_K": WK, "W_V": WV,
        "ident": ident, "onehot": onehot, "cq": cq, "ckv": ckv,
        "dtab": host_decay(own_rows, S).astype(ml_dtypes.bfloat16),
        "meta": np.array([[role, half, nonce, 0]], dtype=np.int32),
    })
    return m


def kernel(L0_attn_X, L0_attn_W_Q, L0_attn_W_K, L0_attn_W_V, L0_ffn_gate_w,
           L0_ffn_w_gate, L0_ffn_w_up, L0_ffn_w_down, L0_ffn_sw_gate,
           L0_ffn_sw_up, L0_ffn_sw_down):
    X = np.asarray(L0_attn_X)
    wmap = {
        "w_gate": np.ascontiguousarray(L0_ffn_w_gate),
        "w_up": np.ascontiguousarray(L0_ffn_w_up),
        "w_down": np.ascontiguousarray(L0_ffn_w_down),
        "sw_gate": np.ascontiguousarray(L0_ffn_sw_gate),
        "sw_up": np.ascontiguousarray(L0_ffn_sw_up),
        "sw_down": np.ascontiguousarray(L0_ffn_sw_down),
        "gate_w": np.ascontiguousarray(L0_ffn_gate_w),
    }
    wmap = stage_weights(wmap)
    WQ = stage_wqkv(L0_attn_W_Q)
    WK = stage_wqkv(L0_attn_W_K)
    WV = stage_wqkv(L0_attn_W_V)
    nonce = int(np.random.randint(1, 2**31 - 1))
    nc = build_fused()
    ins = [fused_core_inputs(c, X, wmap, WQ, WK, WV, nonce) for c in range(8)]
    r = run_bass_kernel_spmd(nc, ins, core_ids=list(range(8)))
    out = np.empty((B, S, D), dtype=np.float32)
    for c in range(8):
        b, role, half = c // 4, (c % 4) // 2, c % 2
        own0 = (RA if role else 0) + half * TOK
        out[b, own0:own0 + TOK, :] = r.results[c]["xout"].T
    return out
